# version 09
# baseline (speedup 1.0000x reference)
import math
import numpy as np
import ml_dtypes
import concourse.bass as bass
import concourse.mybir as mybir
from concourse.bass_core_rust import IndirectOffsetOnAxis

F32 = mybir.dt.float32
BF16 = mybir.dt.bfloat16
I32 = mybir.dt.int32
AF = mybir.ActivationFunctionType
ALU = mybir.AluOpType
AXX = mybir.AxisListType.X

C0G = 0.7978845608028654
C1G = C0G * 0.044715
TWO_PI = 2.0 * math.pi


class Cfg:
    def __init__(self, S=4096):
        self.S = S
        self.SQ = S // 2
        self.D = 1024
        self.H = 16
        self.DH = 64
        self.NP = 8
        self.C = 512
        self.F = 4096
        self.KSEL = S // 8
        self.KC = self.KSEL - 1
        self.SELP = max(128, ((self.KC + 127) // 128) * 128)
        self.NSC = self.SELP // 128
        self.NKC = S // 128
        self.NQC = self.SQ // 128
        self.QB = min(512, self.SQ)
        self.NQB = self.SQ // self.QB
        self.DC = self.D // 128
        self.CC = self.C // 128
        self.FC = self.F // 128
        self.WIN = min(3 * 128, self.SELP) if self.SELP >= 512 else self.SELP
        self.NWC = self.WIN // 128


def host_constants(cfg, perm):
    S, SQ, SELP, NKC = cfg.S, cfg.SQ, cfg.SELP, cfg.NKC
    t = np.arange(S, dtype=np.float64)[perm]
    inv = 10000.0 ** (-(np.arange(32) % 16) / 16.0)
    th = t[:, None] * inv[None, :]
    cosA = np.cos(th).astype(np.float32)
    sinA = np.sin(th).astype(np.float32)
    rows_j = (np.arange(128) % 64) // 2
    sgn = np.where(np.arange(128) % 2 == 0, -1.0, 1.0).astype(np.float32)
    C_tab = np.ascontiguousarray(cosA[:, rows_j].T)
    S_tab = np.ascontiguousarray(sinA[:, rows_j].T * sgn[:, None])
    okp = np.ones((128, 1), np.float32)
    okp[(cfg.KC % 128) if cfg.SELP > cfg.KC else 128:, 0] = 0.0
    pf = np.zeros((1, SELP), np.float32)
    pf[0, cfg.KC:] = float(2 * S)
    swap = np.arange(128)
    swap = np.where(swap % 64 < 32, swap + 32, swap - 32)
    P = np.zeros((128, 128), np.float32)
    P[swap, np.arange(128)] = 1.0
    return {
        "c_ctabk": C_tab.astype(ml_dtypes.bfloat16),
        "c_stabk": S_tab.astype(ml_dtypes.bfloat16),
        "c_ctabq": np.ascontiguousarray((C_tab[:, :SQ] * 0.125).astype(ml_dtypes.bfloat16)),
        "c_stabq": np.ascontiguousarray((S_tab[:, :SQ] * 0.125).astype(ml_dtypes.bfloat16)),
        "c_idf32": np.eye(128, dtype=np.float32),
        "c_idbf": np.eye(128, dtype=np.float32).astype(ml_dtypes.bfloat16),
        "c_ones_row": np.ones((1, 512), np.float32),
        "c_upper01": (np.arange(128)[:, None] <= np.arange(128)[None, :]).astype(ml_dtypes.bfloat16),
        "c_ones_col_bf": np.ones((128, 1), np.float32).astype(ml_dtypes.bfloat16),
        "c_ones_kpad_bf": okp.astype(ml_dtypes.bfloat16),
        "c_iota_rank": np.broadcast_to(np.arange(1, SELP + 1, dtype=np.float32)[None, :], (128, SELP)).copy(),
        "c_c128": np.broadcast_to((128.0 * np.arange(NKC, dtype=np.float32))[None, :], (128, NKC)).copy().astype(ml_dtypes.bfloat16),
        "c_pcol": np.arange(128, dtype=np.float32).reshape(128, 1),
        "c_pcol_bf": np.arange(128, dtype=np.float32).astype(ml_dtypes.bfloat16).reshape(128, 1),
        "c_padfix": pf,
        "c_pswapb": P.astype(ml_dtypes.bfloat16),
        "c_invb": (10000.0 ** (-2 * (np.arange(128) % 32) / 64.0)).astype(np.float32).reshape(128, 1),
        "c_eps": np.broadcast_to(np.array([1e-5, -0.5], np.float32)[None, :], (128, 2)).copy(),
        "c_sgnb": np.where(np.arange(128) % 64 < 32, -1.0, 1.0).astype(np.float32).reshape(128, 1),
    }


CONST_SPECS = lambda cfg: [
    ("c_ctabk", [128, cfg.S], BF16), ("c_stabk", [128, cfg.S], BF16),
    ("c_ctabq", [128, cfg.SQ], BF16), ("c_stabq", [128, cfg.SQ], BF16),
    ("c_idf32", [128, 128], F32), ("c_idbf", [128, 128], BF16),
    ("c_ones_row", [1, 512], F32), ("c_upper01", [128, 128], BF16),
    ("c_ones_col_bf", [128, 1], BF16), ("c_ones_kpad_bf", [128, 1], BF16),
    ("c_iota_rank", [128, cfg.SELP], F32), ("c_c128", [128, cfg.NKC], BF16),
    ("c_pcol", [128, 1], F32), ("c_pcol_bf", [128, 1], BF16),
    ("c_padfix", [1, cfg.SELP], F32), ("c_pswapb", [128, 128], BF16),
    ("c_invb", [128, 1], F32), ("c_sgnb", [128, 1], F32),
    ("c_eps", [128, 2], F32),
]


class Script:
    ENGINES = ("sync", "vector", "scalar", "tensor", "gpsimd")

    def __init__(self, nc):
        self.nc = nc
        self.q = {e: [] for e in self.ENGINES}
        self.sems = {}

    def sem(self, name):
        if name not in self.sems:
            self.sems[name] = [self.nc.alloc_semaphore(name), 0]
        return self.sems[name]

    def inc(self, name, k=1):
        s = self.sem(name)
        s[1] += k
        return s[0], s[1]

    def val(self, name):
        return self.sem(name)[1]

    def do(self, eng, fn):
        self.q[eng].append(fn)

    def barrier(self):
        snap = [(s[0], s[1]) for s in self.sems.values() if s[1] > 0]
        for e in self.ENGINES:
            def f(eng, snap=snap):
                for h, v in snap:
                    eng.wait_ge(h, v)
            self.do(e, f)

    def build(self):
        with self.nc.Block() as blk:
            blk.sync(lambda e: [f(e) for f in self.q["sync"]])
            blk.vector(lambda e: [f(e) for f in self.q["vector"]])
            blk.scalar(lambda e: [f(e) for f in self.q["scalar"]])
            blk.tensor(lambda e: [f(e) for f in self.q["tensor"]])
            blk.gpsimd(lambda e: [f(e) for f in self.q["gpsimd"]])


def build_program(cfg, dbg=False):
    nc = bass.Bass("TRN2", target_bir_lowering=False)
    S, SQ, D, C, F = cfg.S, cfg.SQ, cfg.D, cfg.C, cfg.F
    NP, KC, SELP, NSC, NKC, NQC, QB, NQB = (cfg.NP, cfg.KC, cfg.SELP, cfg.NSC,
                                            cfg.NKC, cfg.NQC, cfg.QB, cfg.NQB)
    DC, CC, FC, WIN, NWC = cfg.DC, cfg.CC, cfg.FC, cfg.WIN, cfg.NWC
    KSEL = cfg.KSEL

    def ein(name, shape, dt=F32):
        return nc.dram_tensor(name, shape, dt, kind="ExternalInput").ap()

    x_in = ein("x", [S, D])
    m_in = ein("m", [S, S])
    pos_in = ein("pos", [S], I32)
    wdq_t = ein("wdq_t", [D, C]); wdkv_t = ein("wdkv_t", [D, C])
    wuq_t = ein("wuq_t", [C, D]); wuk_t = ein("wuk_t", [C, D]); wuv_t = ein("wuv_t", [C, D])
    wo_t = ein("wo_t", [D, D])
    g2_in = ein("g2", [D]); b2_in = ein("b2", [D])
    rw_in = ein("rw", [D]); rb_in = ein("rb", [1])
    fwq = ein("fwq", [D, D]); fwk = ein("fwk", [D, D]); fwv = ein("fwv", [D, D]); fwo = ein("fwo", [D, D])
    fw1 = ein("fw1", [D, F]); fw2 = ein("fw2", [F, D])
    ln1_in = ein("ln1", [D]); ln2_in = ein("ln2", [D])
    cst = {nm: ein(nm, sh, dt) for nm, sh, dt in CONST_SPECS(cfg)}

    y_out = nc.dram_tensor("y", [SQ, D], F32, kind="ExternalOutput").ap()

    xn_scr = nc.dram_tensor("xn_scr", [S, D], F32, kind="ExternalOutput").ap()
    hp_scr = nc.dram_tensor("hp_scr", [S, D], BF16, kind="ExternalOutput").ap()
    mt_scr = nc.dram_tensor("mt_scr", [S, SELP], F32, kind="ExternalOutput").ap()
    s_scr = nc.dram_tensor("s_scr", [SQ, D], F32, kind="ExternalOutput").ap()
    w_scr = nc.dram_tensor("w_scr", [S], F32, kind="ExternalOutput").ap()
    sel_scr = nc.dram_tensor("sel_scr", [SELP], I32, kind="ExternalOutput").ap()
    q_scr = nc.dram_tensor("q_scr", [NP, 128, SQ], BF16, kind="ExternalOutput").ap()
    o_scr = nc.dram_tensor("o_scr", [NP, 128, SQ], BF16, kind="ExternalOutput").ap()

    def sb(name, shape, dt):
        return nc.alloc_sbuf_tensor(name, shape, dt).ap()

    idf32 = sb("idf32", [128, 128], F32)
    idbf = sb("idbf", [128, 128], BF16)
    ones_row = sb("ones_row", [1, 512], F32)
    onesbf_row = sb("onesbf_row", [1, 512], BF16)
    upper01 = sb("upper01", [128, 128], BF16)
    ones_col_bf = sb("ones_col_bf", [128, 1], BF16)
    ones_kpad_bf = sb("ones_kpad_bf", [128, 1], BF16)
    iota_rank = sb("iota_rank", [128, SELP], F32)
    c128c = sb("c128c", [128, NKC], BF16)
    pcol = sb("pcol", [128, 1], F32)
    pcol_bf = sb("pcol_bf", [128, 1], BF16)
    padfix = sb("padfix", [1, SELP], F32)
    pswapb = sb("pswapb", [128, 128], BF16)
    invb = sb("invb", [128, 1], F32)
    sgnb = sb("sgnb", [128, 1], F32)
    epsc = sb("epsc", [128, 2], F32)
    g2c = sb("g2c", [128, DC], F32)
    b2cc = sb("b2cc", [128, DC], F32)
    b2bf = sb("b2bf", [128, DC], BF16)
    vrowbf = sb("vrowbf", [1, 2 * C], BF16)
    rowst = sb("rowst", [1, D], F32)
    bcast = sb("bcast", [128, D], F32)
    stat = sb("stat", [128, 24], F32)
    w_pm = sb("w_pm", [128, NKC], F32)
    bis = sb("bis", [128, 16], F32)
    sc1 = sb("sc1", [2, 16], F32)
    rankc = sb("rankc", [128, NKC], F32)
    maskc = sb("maskc", [128, NKC], BF16)
    maskf = sb("maskf", [128, NKC], F32)
    oh = sb("oh", [128, SELP], BF16)
    selrow = sb("selrow", [1, SELP], F32)
    selc = sb("selc", [128, NSC], I32)
    posc = sb("posc", [128, NSC], I32)
    wselc = sb("wselc", [128, NSC], F32)
    rrow = sb("rrow", [1, 1024], F32)
    vrow = rrow[0:1, 0:2 * C]
    selrow_i = rrow[0:1, 512:512 + SELP].bitcast(I32)
    posrow = rrow[0:1, 0:SELP]
    poscf = sb("poscf", [128, NSC], F32)

    SCRA = sb("SCRA", [128, 16384], BF16)
    PH5_A1 = 2 * NSC * D + 2 * S + NSC * D + 2 * NSC * SELP
    VST = 8 * 65
    A1 = sb("A1", [128, max(4 * S + NKC * VST, PH5_A1, FC * WIN)], BF16)
    A2 = sb("A2", [128, max(CC * S, 2 * DC * D)], BF16)
    A3 = sb("A3", [128, max(CC * SQ, 8192, 2 * NP * SELP)], BF16)
    A4 = sb("A4", [128, max(2 * S, NP * SELP + DC * WIN, 2 * SQ)], BF16)
    WA = sb("WA", [128, max(2 * DC * C, 2 * CC * D, DC * D)], BF16)

    def bfv(arena, c0, n, p0=0, p=128):
        return arena[p0:p0 + p, c0:c0 + n]

    def f32v(arena, c0, n, p0=0, p=128):
        return arena[p0:p0 + p, 2 * c0:2 * (c0 + n)].bitcast(F32)

    def i32v(arena, c0, n, p0=0, p=128):
        return arena[p0:p0 + p, 2 * c0:2 * (c0 + n)].bitcast(I32)

    PS_A = nc.alloc_psum_tensor("PS_A", [128, 2048], F32).ap()
    PS_B = nc.alloc_psum_tensor("PS_B", [128, 2048], F32).ap()
    ps_S = PS_A[:, 0:1024]
    ps_G = PS_A[:, 1024:1536]
    ps_d3 = PS_A[:, 1536:2048]
    ps_O = PS_B[:, 0:512]
    ps_d0 = PS_B[:, 512:1024]
    ps_d1 = PS_B[:, 1024:1536]
    ps_d2 = PS_B[:, 1536:2048]

    Sp = Script(nc)

    def DMA(queue, ch, out, in_, **kw):
        h, v = Sp.inc(ch, 16)
        Sp.do(queue, lambda e, out=out, in_=in_, kw=kw, h=h: e.dma_start(out=out, in_=in_, **kw).then_inc(h, 16))
        return v

    def IDMA(ch, out, out_off, in_, in_off, **kw):
        h, v = Sp.inc(ch, 16)
        Sp.do("gpsimd", lambda e, out=out, out_off=out_off, in_=in_, in_off=in_off, kw=kw, h=h:
              e.indirect_dma_start(out, out_off, in_, in_off, **kw).then_inc(h, 16))
        return v

    def W(eng, sem, v=None):
        s = Sp.sem(sem)
        if v is None:
            v = s[1]
        if v <= 0:
            return
        Sp.do(eng, lambda e, h=s[0], v=v: e.wait_ge(h, v))

    SEM_OF = {"tensor": "cP", "vector": "cV", "scalar": "cA", "gpsimd": "cG"}

    def OP(eng, fn, selfwait=True):
        s = Sp.sem(SEM_OF[eng])
        nprev = s[1]
        h, v = Sp.inc(SEM_OF[eng], 1)
        if selfwait and nprev > 0 and eng != "tensor":
            Sp.do(eng, lambda e, fn=fn, h=h, nprev=nprev: (
                e.wait_ge(h, nprev), fn(e).then_inc(h, 1)))
        else:
            Sp.do(eng, lambda e, fn=fn, h=h: fn(e).then_inc(h, 1))
        return v

    def OPn(eng, fn):
        Sp.do(eng, fn)

    def GPS(eng="tensor"):
        W(eng, "cV", Sp.val("cV"))

    def MSET(eng, ap, val):
        return OP(eng, lambda e, ap=ap, val=val: e.memset(ap, val))

    MSET("vector", ps_d3[:, :], 0.0)
    MSET("vector", bis[:, :], 0.0)
    MSET("vector", stat[:, :], 0.0)
    v_init0 = MSET("vector", sc1[:, :], 0.0)
    W("scalar", "cV", v_init0)
    for dst, key in [(idf32, "c_idf32"), (idbf, "c_idbf"), (ones_row, "c_ones_row"),
                     (upper01, "c_upper01"), (ones_col_bf, "c_ones_col_bf"),
                     (ones_kpad_bf, "c_ones_kpad_bf"), (iota_rank, "c_iota_rank"),
                     (c128c, "c_c128"), (pcol, "c_pcol"), (pcol_bf, "c_pcol_bf"),
                     (padfix, "c_padfix"), (pswapb, "c_pswapb"), (invb, "c_invb"),
                     (sgnb, "c_sgnb"), (epsc, "c_eps")]:
        DMA("sync", "dC", dst, cst[key])
    DMA("gpsimd", "dC2", onesbf_row, cst["c_ones_row"])
    T_CQ, T_SQ = 0, SQ
    T_CK, T_SK = 0, S
    DMA("sync", "dC", bfv(A4, T_CQ, SQ), cst["c_ctabq"])
    DMA("sync", "dC", bfv(A4, T_SQ, SQ), cst["c_stabq"])
    DMA("sync", "dC", g2c, g2_in.rearrange("(c p) -> p c", p=128), allow_slow_non_contiguous=True)
    DMA("sync", "dC", b2cc, b2_in.rearrange("(c p) -> p c", p=128), allow_slow_non_contiguous=True)
    W_A, W_B = 0, DC * C
    DMA("gpsimd", "dW", bfv(WA, W_A, DC * C).rearrange("p (dc c) -> p dc c", dc=DC), wdq_t.rearrange("(dc p) c -> p dc c", p=128))
    vwAB = DMA("gpsimd", "dW", bfv(WA, W_B, DC * C).rearrange("p (dc c) -> p dc c", dc=DC), wdkv_t.rearrange("(dc p) c -> p dc c", p=128))

    vrw = DMA("sync", "dC", rowst, rw_in.rearrange("(a d) -> a d", a=1))

    def bc_row(src_row, dst_full, width, wait_val=None):
        for i in range((width + 511) // 512):
            w0, w1 = i * 512, min(width, (i + 1) * 512)
            W("tensor", "cV", Sp.val("cV"))
            vm = OP("tensor", lambda e, w0=w0, w1=w1: e.matmul(
                ps_G[:, 0:w1 - w0], ones_row[0:1, 0:128], src_row[0:1, w0:w1], start=True, stop=True))
            W("vector", "cP", vm)
            OP("vector", lambda e, w0=w0, w1=w1: e.tensor_copy(dst_full[:, w0:w1], ps_G[:, 0:w1 - w0]))

    W("tensor", "dC")
    bc_row(rowst, bcast, D)
    v_rwbc = Sp.val("cV")

    W("vector", "dW")
    W("vector", "dC")
    for dc in range(DC):
        OP("vector", lambda e, dc=dc: e.tensor_scalar(
            bfv(WA, W_A + dc * C, C), bfv(WA, W_A + dc * C, C), g2c[:, dc:dc + 1], None, ALU.mult))
        OP("vector", lambda e, dc=dc: e.tensor_scalar(
            bfv(WA, W_B + dc * C, C), bfv(WA, W_B + dc * C, C), g2c[:, dc:dc + 1], None, ALU.mult))
    OP("vector", lambda e: e.reciprocal(stat[:, 16:16 + DC], g2c))
    OP("vector", lambda e: e.tensor_tensor(stat[:, 16:16 + DC], b2cc, stat[:, 16:16 + DC], ALU.mult))
    vbg = OP("vector", lambda e: e.tensor_copy(b2bf, stat[:, 16:16 + DC]))
    W("tensor", "cV", vbg)
    for row, woff in ((0, W_A), (1, W_B)):
        vm = None
        W("tensor", "cV", Sp.val("cV"))
        for dc in range(DC):
            vm = OP("tensor", lambda e, row=row, woff=woff, dc=dc: e.matmul(
                ps_G[0:1, 0:C], b2bf[:, dc:dc + 1], bfv(WA, woff + dc * C, C),
                start=(dc == 0), stop=(dc == DC - 1)))
        W("vector", "cP", vm)
        OP("vector", lambda e, row=row: e.tensor_copy(vrow[0:1, row * C:(row + 1) * C], ps_G[0:1, 0:C]))
    v_vrow = OP("vector", lambda e: e.tensor_copy(vrowbf, vrow))

    hist = {}

    def ph1_a(t):
        b = t % 2
        xc = f32v(SCRA, b * D, D)
        if t >= 2:
            W("sync", "cV", hist[t - 2]["vxn"])
            W("sync", "cA", hist[t - 2]["vcp"])
        DMA("sync", f"dX{b}", xc, x_in[t * 128:(t + 1) * 128, :])
        dumpA = f32v(SCRA, 4096, D)
        W("scalar", f"dX{b}")
        if t >= 2:
            W("scalar", "cV", hist[t - 2]["v2"])
        vsq = OP("scalar", lambda e, xc=xc, dumpA=dumpA, b=b: e.activation(
            dumpA, xc, AF.Square, accum_out=stat[:, 12 * b + 1:12 * b + 2]))
        vcp = OP("scalar", lambda e, xc=xc, dumpA=dumpA, b=b: e.activation(
            dumpA, xc, AF.Copy, accum_out=stat[:, 12 * b:12 * b + 1]))
        hist[t] = {"vsq": vsq, "vcp": vcp}

    def ph1_b(t):
        b = t % 2
        sc = 12 * b
        st = lambda k: stat[:, sc + k:sc + k + 1]
        xc = f32v(SCRA, b * D, D)
        xnc = f32v(SCRA, 2048 + b * D, D)
        dumpB = f32v(SCRA, 5120, D)
        hpc = bfv(SCRA, 12288 + b * D, D)
        W("vector", "cA", hist[t]["vcp"])
        if t == 0:
            W("vector", "cV", v_rwbc)
        OP("vector", lambda e: e.tensor_scalar(st(2), st(0), 1.0 / D, None, ALU.mult))
        OP("vector", lambda e: e.tensor_tensor(st(3), st(2), st(2), ALU.mult))
        v2 = OP("vector", lambda e: e.scalar_tensor_tensor(
            st(4), st(1), 1.0 / D, st(3), ALU.mult, ALU.subtract))
        hist[t]["v2"] = v2
        W("scalar", "cV", v2)
        OP("scalar", lambda e: e.activation(st(5), st(4), AF.Ln, bias=epsc[:, 0:1]))
        v3 = OP("scalar", lambda e: e.activation(st(6), st(5), AF.Exp, scale=epsc[:, 1:2]))
        OP("vector", lambda e: e.tensor_scalar(st(7), st(4), 1e-5, None, ALU.add))
        OP("vector", lambda e: e.reciprocal(st(8), st(7)))
        v5 = OP("vector", lambda e: e.tensor_tensor(st(9), st(4), st(8), ALU.mult))
        W("scalar", "cV", v5)
        OP("scalar", lambda e: e.activation(st(10), st(9), AF.Ln, bias=epsc[:, 0:1]))
        v6 = OP("scalar", lambda e: e.activation(st(11), st(10), AF.Exp, scale=epsc[:, 1:2]))
        W("vector", "cA", v3)
        if t >= 2:
            W("vector", f"dXN{b}")
        vxn = OP("vector", lambda e, xc=xc, xnc=xnc: e.tensor_scalar(
            xnc, xc, st(2), st(6), ALU.subtract, ALU.mult))
        vwc = OP("vector", lambda e, xnc=xnc, dumpB=dumpB, t=t: e.scalar_tensor_tensor(
            dumpB, xnc, 1.0, bcast, ALU.mult, ALU.mult, accum_out=w_pm[:, t:t + 1]))
        W("vector", "cA", v6)
        if t >= 2:
            W("vector", f"dHP{b}")
        vhp = OP("vector", lambda e, xnc=xnc, hpc=hpc: e.tensor_scalar(
            hpc, xnc, st(11), None, ALU.mult))
        W("sync", "cV", vxn)
        DMA("sync", f"dXN{b}", xn_scr[t * 128:(t + 1) * 128, :], xnc)
        W("scalar", "cV", vhp)
        DMA("scalar", f"dHP{b}", hp_scr[t * 128:(t + 1) * 128, :], hpc)
        W("sync", "cV", vwc)
        DMA("sync", "dWS", w_scr[t * 128:(t + 1) * 128].rearrange("(p a) -> p a", a=1), w_pm[:, t:t + 1])
        hist[t]["vxn"] = vxn

    for t in range(NKC):
        ph1_a(t)
        if t >= 1:
            ph1_b(t - 1)
    ph1_b(NKC - 1)
    Sp.barrier()

    for dc in range(DC):
        DMA("scalar", "dT", bfv(A1, dc * S, S), hp_scr[:, dc * 128:(dc + 1) * 128], transpose=True)
    v_hpT = Sp.val("dT")

    WBC, CMPC = 0, S
    CW = min(2048, S)
    stg = f32v(SCRA, CMPC, CW)[0:1, 0:S if S <= 2048 else 2048]

    nhalf = S // 2048 if S >= 2048 else 1
    HWW = S // max(1, nhalf)
    for hwi in range(max(1, S // HWW)):
        W("sync", "dWS")
        vld = DMA("sync", "dSTG", f32v(SCRA, CMPC, HWW)[0:1, :], w_scr[hwi * HWW:(hwi + 1) * HWW].rearrange("(a s) -> a s", a=1))
        W("tensor", "dSTG")
        vm = None
        for i in range(HWW // 512):
            GPS()
            vm = OP("tensor", lambda e, i=i, hwi=hwi: e.matmul(
                ps_G[:, 0:512], ones_row[0:1, 0:128],
                f32v(SCRA, CMPC, HWW)[0:1, i * 512:(i + 1) * 512], start=True, stop=True))
            W("vector", "cP", vm)
            vv = OP("vector", lambda e, i=i, hwi=hwi: e.tensor_copy(
                f32v(SCRA, WBC + hwi * HWW + i * 512, 512), ps_G[:, 0:512]))
        W("sync", "cV", Sp.val("cV"))
    wb = lambda c0, n: f32v(SCRA, WBC + c0, n)
    cmp_t = lambda n: f32v(SCRA, CMPC, n)
    NCMP = (S + CW - 1) // CW

    OP("vector", lambda e: e.memset(bis[:, 0:1], -64.0))
    OP("vector", lambda e: e.memset(bis[:, 1:2], 128.0 / 129.0))
    OP("vector", lambda e: e.memset(bis[:, 8:9], 64.0))

    def count_gt(thr_col, out_col):
        for ci in range(NCMP):
            OP("vector", lambda e, ci=ci: e.tensor_scalar(
                cmp_t(CW), wb(ci * CW, CW), bis[:, thr_col:thr_col + 1], 0.0, ALU.is_gt,
                ALU.add, accum_out=stat[:, 20 + ci:21 + ci]))
        if NCMP == 1:
            return OP("vector", lambda e: e.tensor_copy(bis[:, out_col:out_col + 1], stat[:, 20:21]))
        v = OP("vector", lambda e: e.tensor_tensor(bis[:, out_col:out_col + 1], stat[:, 20:21], stat[:, 21:22], ALU.add))
        return v

    def cross_sum_bcast(src_col, dst_col):
        vcb = OP("vector", lambda e: e.tensor_copy(maskc[:, 0:1], bis[:, src_col:src_col + 1]))
        W("tensor", "cV", vcb)
        vm = OP("tensor", lambda e: e.matmul(ps_G[0:1, 0:1], maskc[:, 0:1], ones_col_bf, start=True, stop=True))
        W("vector", "cP", vm)
        vv = OP("vector", lambda e: e.tensor_copy(sc1[0:1, 15:16], ps_G[0:1, 0:1]))
        W("tensor", "cV", vv)
        vb = OP("tensor", lambda e: e.matmul(ps_G[:, 1:2], ones_row[0:1, 0:128], sc1[0:1, 15:16], start=True, stop=True))
        W("vector", "cP", vb)
        return OP("vector", lambda e: e.tensor_copy(bis[:, dst_col:dst_col + 1], ps_G[:, 1:2]))

    for rnd in range(5):
        OP("vector", lambda e: e.tensor_scalar(bis[:, 2:3], pcol, 1.0, bis[:, 1:2], ALU.add, ALU.mult))
        OP("vector", lambda e: e.tensor_tensor(bis[:, 2:3], bis[:, 2:3], bis[:, 0:1], ALU.add))
        count_gt(2, 5)
        OP("vector", lambda e: e.tensor_scalar(bis[:, 6:7], bis[:, 5:6], float(KSEL), None, ALU.is_ge))
        cross_sum_bcast(6, 7)
        OP("vector", lambda e: e.tensor_scalar(stat[:, 17:18], bis[:, 7:8], 1.0, bis[:, 1:2], ALU.add, ALU.mult))
        OP("vector", lambda e: e.tensor_tensor(stat[:, 17:18], stat[:, 17:18], bis[:, 0:1], ALU.add))
        OP("vector", lambda e: e.tensor_tensor(stat[:, 18:19], stat[:, 17:18], bis[:, 8:9], ALU.min))
        OP("vector", lambda e: e.tensor_scalar(stat[:, 19:20], bis[:, 7:8], 127.5, None, ALU.is_le))
        OP("vector", lambda e: e.tensor_tensor(stat[:, 18:19], stat[:, 18:19], stat[:, 19:20], ALU.mult))
        OP("vector", lambda e: e.tensor_scalar(stat[:, 19:20], stat[:, 19:20], -1.0, 1.0, ALU.mult, ALU.add))
        OP("vector", lambda e: e.tensor_tensor(stat[:, 19:20], stat[:, 19:20], bis[:, 8:9], ALU.mult))
        OP("vector", lambda e: e.tensor_tensor(bis[:, 8:9], stat[:, 18:19], stat[:, 19:20], ALU.add))
        OP("vector", lambda e: e.scalar_tensor_tensor(stat[:, 12:13], bis[:, 7:8], 1.0, bis[:, 1:2], ALU.mult, ALU.mult))
        OP("vector", lambda e: e.tensor_tensor(bis[:, 0:1], bis[:, 0:1], stat[:, 12:13], ALU.add))
        OP("vector", lambda e: e.tensor_scalar(bis[:, 1:2], bis[:, 1:2], 1.0 / 129.0, None, ALU.mult))

    def cross_max(col, out_idx):
        GPS()
        vt = OP("tensor", lambda e, col=col: e.transpose(ps_G[0:1, 0:128], stat[:, col:col + 1], idf32))
        W("vector", "cP", vt)
        return OP("vector", lambda e, out_idx=out_idx: e.tensor_reduce(
            sc1[0:1, out_idx:out_idx + 1], ps_G[0:1, 0:128], AXX, ALU.max))

    def masked_max(cmp_op, thr_col, out_idx):
        BIG = 1.0e30
        for ci in range(NCMP):
            flg = cmp_t(CW)
            wf = f32v(SCRA, CMPC + CW, CW)
            OP("vector", lambda e, ci=ci, flg=flg: e.tensor_scalar(
                flg, wb(ci * CW, CW), bis[:, thr_col:thr_col + 1], None, cmp_op))
            OP("vector", lambda e, ci=ci, flg=flg, wf=wf: e.tensor_tensor(
                wf, wb(ci * CW, CW), flg, ALU.mult))
            OP("vector", lambda e, flg=flg: e.tensor_scalar(
                flg, flg, BIG, -BIG, ALU.mult, ALU.add))
            OP("vector", lambda e, flg=flg, wf=wf: e.tensor_tensor(
                wf, wf, flg, ALU.add))
            OP("vector", lambda e, ci=ci, wf=wf: e.tensor_reduce(
                stat[:, 13 + ci:14 + ci], wf, AXX, ALU.max))
        if NCMP == 1:
            vv = OP("vector", lambda e: e.tensor_copy(stat[:, 15:16], stat[:, 13:14]))
        else:
            vv = OP("vector", lambda e: e.tensor_tensor(stat[:, 15:16], stat[:, 13:14], stat[:, 14:15], ALU.max))
        W("tensor", "cV", vv)
        return cross_max(15, out_idx)

    v1 = masked_max(ALU.is_le, 8, 0)
    W("tensor", "cV", v1)
    GPS()
    vb = OP("tensor", lambda e: e.matmul(ps_G[:, 2:3], ones_row[0:1, 0:128], sc1[0:1, 0:1], start=True, stop=True))
    W("vector", "cP", vb)
    OP("vector", lambda e: e.tensor_copy(bis[:, 9:10], ps_G[:, 2:3]))
    count_gt(9, 10)
    OP("vector", lambda e: e.tensor_copy(sc1[0:1, 1:2], bis[0:1, 10:11]))
    v2 = masked_max(ALU.is_lt, 9, 2)
    OP("vector", lambda e: e.tensor_scalar(sc1[0:1, 3:4], sc1[0:1, 1:2], float(KC), None, ALU.is_equal))
    OP("vector", lambda e: e.tensor_tensor(sc1[0:1, 4:5], sc1[0:1, 3:4], sc1[0:1, 0:1], ALU.mult))
    OP("vector", lambda e: e.tensor_scalar(sc1[0:1, 5:6], sc1[0:1, 3:4], -1.0, 1.0, ALU.mult, ALU.add))
    OP("vector", lambda e: e.tensor_tensor(sc1[0:1, 5:6], sc1[0:1, 5:6], sc1[0:1, 2:3], ALU.mult))
    vthr = OP("vector", lambda e: e.tensor_tensor(sc1[0:1, 6:7], sc1[0:1, 4:5], sc1[0:1, 5:6], ALU.add))
    W("tensor", "cV", vthr)
    GPS()
    vbt = OP("tensor", lambda e: e.matmul(ps_G[:, 4:5], ones_row[0:1, 0:128], sc1[0:1, 6:7], start=True, stop=True))
    W("vector", "cP", vbt)
    OP("vector", lambda e: e.tensor_copy(bis[:, 12:13], ps_G[:, 4:5]))
    OP("vector", lambda e: e.tensor_scalar(maskf[:, 0:NKC], w_pm, bis[:, 12:13], None, ALU.is_gt))
    vmk = OP("vector", lambda e: e.tensor_copy(maskc[:, 0:NKC], maskf[:, 0:NKC]))
    W("tensor", "cV", vmk)
    vtr = OP("tensor", lambda e: e.matmul(ps_d2[0:1, 16:16 + NKC], ones_col_bf, maskc, start=True, stop=True))
    GPS()
    vri = OP("tensor", lambda e: e.matmul(ps_G[:, 0:NKC], upper01, maskc, start=True, stop=True))
    W("vector", "cP", vri)
    OP("vector", lambda e: e.tensor_copy(rrow[0:1, 0:NKC], ps_d2[0:1, 16:16 + NKC]))
    OP("vector", lambda e: e.memset(rrow[0:1, 128:129], 0.0))
    OP("vector", lambda e: e.memset(rrow[0:1, 256:256 + NKC], 0.0))
    vsc = None
    if NKC > 1:
        vsc = OP("vector", lambda e: e.tensor_tensor_scan(
            rrow[0:1, 129:128 + NKC], rrow[0:1, 0:NKC - 1], rrow[0:1, 256:256 + NKC - 1], 0.0, ALU.add, ALU.add))
    W("tensor", "cV", Sp.val("cV"))
    vbx = OP("tensor", lambda e: e.matmul(ps_d3[:, 32:32 + NKC], ones_row[0:1, 0:128], rrow[0:1, 128:128 + NKC], start=True, stop=True))
    W("vector", "cP", vbx)
    OP("vector", lambda e: e.tensor_copy(rankc, ps_G[:, 0:NKC]))
    vrk = OP("vector", lambda e: e.tensor_tensor(rankc, rankc, ps_d3[:, 32:32 + NKC], ALU.add))

    vm1 = vm2 = None
    for t in range(NKC):
        voh = OP("vector", lambda e, t=t: e.tensor_scalar(
            oh, iota_rank, rankc[:, t:t + 1], maskf[:, t:t + 1], ALU.is_equal, ALU.mult))
        W("tensor", "cV", voh)
        vm1 = OP("tensor", lambda e, t=t: e.matmul(
            ps_d0[0:1, 0:SELP], c128c[:, t:t + 1], oh, start=(t == 0), stop=(t == NKC - 1)))
        vm2 = OP("tensor", lambda e, t=t: e.matmul(
            ps_d1[0:1, 0:SELP], pcol_bf, oh, start=(t == 0), stop=(t == NKC - 1)))
        if t < NKC - 1:
            W("vector", "cP", vm2)
    W("vector", "cP", vm2)
    OP("vector", lambda e: e.tensor_copy(selrow, ps_d0[0:1, 0:SELP]))
    OP("vector", lambda e: e.tensor_tensor(selrow, selrow, ps_d1[0:1, 0:SELP], ALU.add))
    OP("vector", lambda e: e.tensor_tensor(selrow, selrow, padfix, ALU.add))
    vsi = OP("vector", lambda e: e.tensor_copy(selrow_i, selrow))
    W("sync", "cV", vsi)
    DMA("sync", "dSELW", sel_scr.rearrange("(a s) -> a s", a=1), selrow_i)
    W("sync", "dSELW")
    for sc_ in range(NSC):
        DMA("sync", "dSEL", selc[:, sc_:sc_ + 1], sel_scr[sc_ * 128:(sc_ + 1) * 128].rearrange("(p a) -> p a", a=1))
    Sp.barrier()

    CKV0 = 0
    CQ0 = 0

    def c_proj(arena, dst_base, stride, woff, vrow_idx, ntok):
        for cs in range(CC):
            for tb in range(ntok // QB):
                t0 = tb * QB
                for dc in range(DC):
                    OP("tensor", lambda e, cs=cs, dc=dc, t0=t0, woff=woff: e.matmul(
                        ps_G[:, 0:QB], bfv(WA, woff + dc * C + cs * 128, 128),
                        bfv(A1, dc * S + t0, QB), start=(dc == 0), stop=False))
                vm = OP("tensor", lambda e, cs=cs, t0=t0, vrow_idx=vrow_idx: e.matmul(
                    ps_G[:, 0:QB], vrowbf[0:1, vrow_idx * C + cs * 128:vrow_idx * C + (cs + 1) * 128],
                    onesbf_row[0:1, 0:QB], start=False, stop=True))
                W("vector", "cP", vm)
                vv = OP("vector", lambda e, cs=cs, t0=t0, arena=arena, dst_base=dst_base, stride=stride: e.tensor_copy(
                    bfv(arena, dst_base + cs * stride + t0, QB), ps_G[:, 0:QB]))
                W("tensor", "cV", vv)
        return

    W("tensor", "cV", v_vrow)
    W("tensor", "dT")
    c_proj(A3, CQ0, SQ, W_A, 0, SQ)
    c_proj(A2, CKV0, S, W_B, 1, S)
    v_c = Sp.val("cV")

    U_Q = 0
    U_K, U_V = 0, CC * D
    W("gpsimd", "cP", Sp.val("cP"))
    vwq = DMA("gpsimd", "dW", bfv(WA, U_Q, CC * D).rearrange("p (cc o) -> p cc o", cc=CC), wuq_t.rearrange("(cc p) o -> p cc o", p=128))

    RQ, RS, RT = 0, 512, 1024
    SHUF = [i ^ 1 for i in range(32)]

    def rope_tile(ct_off, st_off, t0, dst_ap, vmm, war_dma=None):
        W("vector", "cP", vmm)
        W("vector", "dQW")
        v1 = OP("vector", lambda e: e.tensor_copy(bfv(SCRA, RQ, QB), ps_G[:, 0:QB]))
        W("tensor", "cV", v1)
        v2 = OP("vector", lambda e: e.stream_shuffle(bfv(SCRA, RS, QB), bfv(SCRA, RQ, QB), SHUF))
        OP("vector", lambda e, t0=t0, ct_off=ct_off: e.tensor_tensor(
            bfv(SCRA, RT, QB), bfv(SCRA, RQ, QB), bfv(A4, ct_off + t0, QB), ALU.mult))
        OP("vector", lambda e, t0=t0, st_off=st_off: e.tensor_tensor(
            bfv(SCRA, RS, QB), bfv(SCRA, RS, QB), bfv(A4, st_off + t0, QB), ALU.mult))
        return OP("vector", lambda e, dst_ap=dst_ap: e.tensor_tensor(
            dst_ap, bfv(SCRA, RT, QB), bfv(SCRA, RS, QB), ALU.add))

    W("tensor", "dW")
    W("tensor", "cV", v_c)
    vqd = None
    vr = None
    for pr in range(NP):
        for tb in range(NQB):
            t0 = tb * QB
            vm = None
            for cc_ in range(CC):
                vm = OP("tensor", lambda e, pr=pr, cc_=cc_, t0=t0: e.matmul(
                    ps_G[:, 0:QB], bfv(WA, U_Q + cc_ * D + pr * 128, 128),
                    bfv(A3, CQ0 + cc_ * SQ + t0, QB), start=(cc_ == 0), stop=(cc_ == CC - 1)))
            vr = rope_tile(T_CQ, T_SQ, t0, bfv(SCRA, RQ, QB), vm, war_dma=vqd)
            W("sync", "cV", vr)
            vqd = DMA("sync", "dQW", q_scr[pr, :, t0:t0 + QB], bfv(SCRA, RQ, QB))
    v_qmm = Sp.val("cP")

    W("sync", "cV", vr)
    DMA("sync", "dTK", bfv(A4, T_CK, S), cst["c_ctabk"])
    DMA("sync", "dTK", bfv(A4, T_SK, S), cst["c_stabk"])
    W("gpsimd", "cP", v_qmm)
    DMA("gpsimd", "dW", bfv(WA, U_K, CC * D).rearrange("p (cc o) -> p cc o", cc=CC), wuk_t.rearrange("(cc p) o -> p cc o", p=128))
    vwu = DMA("gpsimd", "dW", bfv(WA, U_V, CC * D).rearrange("p (cc o) -> p cc o", cc=CC), wuv_t.rearrange("(cc p) o -> p cc o", p=128))
    W("tensor", "dW")
    W("tensor", "dTK")
    W("vector", "dTK")

    K0 = 0
    V0 = 4 * S
    GRP = 2
    PPG = NP // GRP

    QT, PT = 0, 1024
    EVA, EVB = 1536, 2048
    OT = 5120
    DRW = 3584
    OAb = [PS_B[:, 0:512], PS_B[:, 1024:1536]]
    OBb = [PS_B[:, 512:1024], PS_B[:, 1536:2048]]
    vex = {}
    vavden = {}
    vsc_last = {}
    v_oT = {}
    pend = []
    vwo = None
    for g in range(GRP):
        MSET("vector", bfv(A1, V0, NKC * VST).rearrange(
            "p (n d) -> p n d", d=65)[:, :, 64:65], 1.0)
        for pl in range(PPG):
            pr = g * PPG + pl
            for tb in range(S // QB):
                t0 = tb * QB
                vm = None
                for cc_ in range(CC):
                    vm = OP("tensor", lambda e, pr=pr, cc_=cc_, t0=t0: e.matmul(
                        ps_G[:, 0:QB], bfv(WA, U_K + cc_ * D + pr * 128, 128),
                        bfv(A2, CKV0 + cc_ * S + t0, QB), start=(cc_ == 0), stop=(cc_ == CC - 1)))
                rope_tile(T_CK, T_SK, t0, bfv(A1, K0 + pl * S + t0, QB), vm)
        for tc in range(NKC):
            vm = None
            for cc_ in range(CC):
                vm = OP("tensor", lambda e, tc=tc, cc_=cc_, g=g: e.matmul(
                    ps_G[:, 0:512], bfv(A2, CKV0 + cc_ * S + tc * 128, 128),
                    bfv(WA, U_V + cc_ * D + g * 512, 512), start=(cc_ == 0), stop=(cc_ == CC - 1)))
            W("vector", "cP", vm)
            vv = OP("vector", lambda e, tc=tc: e.tensor_copy(
                bfv(A1, V0 + tc * VST, VST).rearrange("p (h d) -> p h d", h=8)[:, :, 0:64],
                ps_G[:, 0:512].rearrange("p (h d) -> p h d", h=8)))
            W("tensor", "cV", vv)
        if g == GRP - 1:
            W("gpsimd", "cP", Sp.val("cP"))
            vwo = DMA("gpsimd", "dW", bfv(WA, 0, DC * D).rearrange("p (hc o) -> p hc o", hc=DC), wo_t.rearrange("(hc p) o -> p hc o", p=128))
            DMA("gpsimd", "dW", bfv(A2, 0, DC * D).rearrange("p (dc o) -> p dc o", dc=DC), fwq.rearrange("(dc p) o -> p dc o", p=128))
            DMA("gpsimd", "dW", bfv(A2, DC * D, DC * D).rearrange("p (dc o) -> p dc o", dc=DC), fwk.rearrange("(dc p) o -> p dc o", p=128))

        W("sync", "dQW")
        W("sync", "cV", Sp.val("cV"))

        def flush_ep(item):
            (pu, pslot, ppr, pt0, dra_p, drb_p, OA_p, OB_p, vrc_p) = item
            W("tensor", "cV", vrc_p)
            OP("tensor", lambda e, OA_p=OA_p, dra_p=dra_p: e.matmul(
                OA_p[0:64, 0:QB], ones_row[0:1, 0:64], dra_p, start=True, stop=True))
            vb2 = OP("tensor", lambda e, OB_p=OB_p, drb_p=drb_p: e.matmul(
                OB_p[0:64, 0:QB], ones_row[0:1, 0:64], drb_p, start=True, stop=True))
            W("vector", "cP", vb2)
            W("vector", f"dOW{pslot}")
            OP("vector", lambda e, OA_p=OA_p, pslot=pslot: e.tensor_tensor(
                bfv(SCRA, OT + pslot * 1024, QB, p0=0, p=64),
                f32v(SCRA, EVA, QB, p0=0, p=64), OA_p[0:64, 0:QB], ALU.mult))
            voTB = OP("vector", lambda e, OB_p=OB_p, pslot=pslot: e.tensor_tensor(
                bfv(SCRA, OT + pslot * 1024 + 512, QB, p0=0, p=64),
                f32v(SCRA, EVB, QB, p0=0, p=64), OB_p[0:64, 0:QB], ALU.mult))
            v_oT[pu] = voTB
            W("sync", "cV", voTB)
            DMA("sync", f"dOW{pslot}", o_scr[ppr, 0:64, pt0:pt0 + QB],
                bfv(SCRA, OT + pslot * 1024, QB, p0=0, p=64))
            DMA("sync", f"dOW{pslot}", o_scr[ppr, 64:128, pt0:pt0 + QB],
                bfv(SCRA, OT + pslot * 1024 + 512, QB, p0=0, p=64))

        for pl in range(PPG):
            pr = g * PPG + pl
            for qb in range(NQB):
                t0 = qb * QB
                u = (g * PPG + pl) * NQB + qb
                uslot = u % 2
                qtile = bfv(SCRA, QT + uslot * 512, QB)
                if (u - 2) in vsc_last:
                    W("sync", "cP", vsc_last[u - 2])
                DMA("sync", f"dQL{uslot}", qtile, q_scr[pr, :, t0:t0 + QB])
                W("tensor", f"dQL{uslot}")
                OA, OB = OAb[uslot], OBb[uslot]

                def avden(kc, u=u, pl=pl, OA=OA, OB=OB):
                    ptile = bfv(SCRA, PT + (kc % 2) * 1024, 1024)
                    W("tensor", "cA", vex[(u, kc)])
                    if kc == 0 and (u - 2) in v_oT:
                        W("tensor", "cV", v_oT[u - 2])
                    OP("tensor", lambda e, kc=kc, ptile=ptile, OA=OA, pl=pl: e.matmul(
                        OA[0:65, 0:QB], bfv(A1, V0 + kc * VST + (2 * pl) * 65, 65),
                        ptile[:, 0:QB], start=(kc == 0), stop=(kc == NKC - 1)))
                    vavden[(u, kc)] = OP("tensor", lambda e, kc=kc, ptile=ptile, OB=OB, pl=pl: e.matmul(
                        OB[0:65, 0:QB], bfv(A1, V0 + kc * VST + (2 * pl + 1) * 65, 65),
                        ptile[:, 512:512 + QB], start=(kc == 0), stop=(kc == NKC - 1)))

                for kc in range(NKC):
                    sb_ = PS_A[:, (kc % 2) * 1024:(kc % 2) * 1024 + 1024]
                    bk = (u, kc - 2) if kc >= 2 else (u - 1, NKC - 2 + kc)
                    if bk in vex:
                        W("tensor", "cA", vex[bk])
                    OP("tensor", lambda e, pl=pl, kc=kc, qtile=qtile, sb_=sb_: e.matmul(
                        sb_[:, 0:QB], bfv(A1, K0 + pl * S + kc * 128, 128, p0=0, p=64),
                        qtile[0:64, :], start=True, stop=True))
                    vs2 = OP("tensor", lambda e, pl=pl, kc=kc, qtile=qtile, sb_=sb_: e.matmul(
                        sb_[:, 512:512 + QB], bfv(A1, K0 + pl * S + kc * 128, 128, p0=64, p=64),
                        qtile[64:128, :], start=True, stop=True))
                    vsc_last[u] = vs2
                    if kc >= 1:
                        avden(kc - 1)
                    if kc == 2 and pend:
                        flush_ep(pend.pop())
                    ptile = bfv(SCRA, PT + (kc % 2) * 1024, 1024)
                    W("scalar", "cP", vs2)
                    pk = (u, kc - 2) if kc >= 2 else (u - 1, NKC - 2 + kc)
                    if pk in vavden:
                        W("scalar", "cP", vavden[pk])
                    if QB == 512:
                        vex[(u, kc)] = OP("scalar", lambda e, ptile=ptile, sb_=sb_: e.activation(
                            ptile, sb_, AF.Exp))
                    else:
                        OP("scalar", lambda e, ptile=ptile, sb_=sb_: e.activation(
                            ptile[:, 0:QB], sb_[:, 0:QB], AF.Exp))
                        vex[(u, kc)] = OP("scalar", lambda e, ptile=ptile, sb_=sb_: e.activation(
                            ptile[:, 512:512 + QB], sb_[:, 512:512 + QB], AF.Exp))
                avden(NKC - 1)
                dra = f32v(SCRA, DRW + uslot * 1024, QB, p0=0, p=1)
                drb = f32v(SCRA, DRW + uslot * 1024 + 512, QB, p0=0, p=1)
                W("vector", "cP", vavden[(u, NKC - 1)])
                OP("vector", lambda e, OA=OA: e.tensor_copy(
                    f32v(SCRA, EVA, QB, p0=0, p=64), OA[0:64, 0:QB]))
                OP("vector", lambda e, OB=OB: e.tensor_copy(
                    f32v(SCRA, EVB, QB, p0=0, p=64), OB[0:64, 0:QB]))
                OP("vector", lambda e, OA=OA, dra=dra: e.tensor_copy(dra, OA[64:65, 0:QB]))
                OP("vector", lambda e, OB=OB, drb=drb: e.tensor_copy(drb, OB[64:65, 0:QB]))
                OP("vector", lambda e, dra=dra: e.reciprocal(dra, dra))
                vrc = OP("vector", lambda e, drb=drb: e.reciprocal(drb, drb))
                pend.append((u, uslot, pr, t0, dra, drb, OA, OB, vrc))
        while pend:
            flush_ep(pend.pop())
        if g < GRP - 1:
            Sp.barrier()
    Sp.barrier()

    OS = 0
    ohist = {}
    for tc in range(NQC):
        slot = tc % 2
        if tc >= 2:
            W("sync", "cP", ohist[tc - 2]["vm"])
        vld = None
        for pr in range(NP):
            vld = DMA("sync", f"dOS{slot}", bfv(A4, OS + slot * NP * 128 + pr * 128, 128),
                      o_scr[pr, :, tc * 128:(tc + 1) * 128])
        W("tensor", f"dOS{slot}")
        if tc == 0:
            W("tensor", "dW")
        for hb in range(2):
            vm = None
            for pr in range(NP):
                vm = OP("tensor", lambda e, pr=pr, hb=hb, slot=slot: e.matmul(
                    ps_G[:, 0:512], bfv(A4, OS + slot * NP * 128 + pr * 128, 128),
                    bfv(WA, pr * D + hb * 512, 512), start=(pr == 0), stop=(pr == NP - 1)))
            if tc >= 1:
                W("scalar", "cV", ohist[tc - 1]["vs"][hb])
            vxn = DMA("scalar", f"dXR{hb}", f32v(SCRA, 0 + hb * 512, 512), xn_scr[tc * 128:(tc + 1) * 128, hb * 512:(hb + 1) * 512])
            W("vector", "cP", vm)
            W("vector", f"dXR{hb}")
            if tc >= 1:
                W("vector", f"dSW{hb}")
            vs = OP("vector", lambda e, hb=hb: e.tensor_tensor(
                f32v(SCRA, 1024 + hb * 512, 512), ps_G[:, 0:512], f32v(SCRA, 0 + hb * 512, 512), ALU.add))
            W("tensor", "cV", vs)
            W("sync", "cV", vs)
            vsw = DMA("sync", f"dSW{hb}", s_scr[tc * 128:(tc + 1) * 128, hb * 512:(hb + 1) * 512], f32v(SCRA, 1024 + hb * 512, 512))
            ohist.setdefault(tc, {"vm": None, "vs": {}, "vsw": {}})
            ohist[tc]["vm"] = vm
            ohist[tc]["vs"][hb] = vs
            ohist[tc]["vsw"][hb] = vsw
    Sp.barrier()

    FQW, FKW = 0, DC * D
    FVW, FOW = 0, DC * D
    X5 = 0
    MR = 2 * NSC * D
    FV5 = MR + 2 * S
    MSL = FV5 + NSC * D
    W("gpsimd", "dSEL")
    for sc_ in range(NSC):
        vz = MSET("gpsimd", f32v(A1, X5 // 2 + sc_ * D, D), 0.0)
        W("gpsimd", "cG", Sp.val("cG"))
        IDMA("dGX", f32v(A1, X5 // 2 + sc_ * D, D), None, xn_scr, IndirectOffsetOnAxis(selc[:, sc_:sc_ + 1], 0),
             bounds_check=S - 1, oob_is_err=False)
        MSET("gpsimd", posc[:, sc_:sc_ + 1], 0)
        W("gpsimd", "cG", Sp.val("cG"))
        IDMA("dGX", posc[:, sc_:sc_ + 1], None, pos_in.rearrange('(s a) -> s a', a=1), IndirectOffsetOnAxis(selc[:, sc_:sc_ + 1], 0),
             bounds_check=S - 1, oob_is_err=False)
    v_gcmp = Sp.val("cG")

    W("sync", "dC")
    W("sync", "cP", Sp.val("cP"))
    vrw2 = DMA("sync", "dRW", rowst, rw_in.rearrange("(a d) -> a d", a=1))
    W("tensor", "dRW")
    bc_row(rowst, bcast, D)
    W("vector", "dGX")
    W("vector", "cG", v_gcmp)
    for sc_ in range(NSC):
        OP("vector", lambda e, sc_=sc_: e.scalar_tensor_tensor(
            f32v(SCRA, 0, D), f32v(A1, X5 // 2 + sc_ * D, D), 1.0, bcast, ALU.mult, ALU.mult,
            accum_out=wselc[:, sc_:sc_ + 1]))
    W("sync", "dRW")
    W("sync", "cP", Sp.val("cP"))
    vrb = DMA("sync", "dRB", rowst[0:1, 0:1], rb_in.rearrange("(b a) -> b a", b=1))
    W("tensor", "dRB")
    GPS()
    vbb = OP("tensor", lambda e: e.matmul(ps_G[:, 5:6], ones_row[0:1, 0:128], rowst[0:1, 0:1], start=True, stop=True))
    W("vector", "cP", vbb)
    OP("vector", lambda e: e.tensor_copy(stat[:, 0:1], ps_G[:, 5:6]))
    for sc_ in range(NSC):
        OP("vector", lambda e, sc_=sc_: e.tensor_tensor(
            wselc[:, sc_:sc_ + 1], wselc[:, sc_:sc_ + 1], stat[:, 0:1], ALU.add))

    W("sync", "dRB")
    W("sync", "cP", Sp.val("cP"))
    vl1 = DMA("sync", "dL1", rowst, ln1_in.rearrange("(a d) -> a d", a=1))
    W("tensor", "dL1")
    bc_row(rowst, bcast, D)
    HN = 0
    vhn_prev = Sp.val("cV")
    for sc_ in range(NSC):
        W("scalar", "cV", vhn_prev)
        vsq = OP("scalar", lambda e, sc_=sc_: e.activation(
            f32v(SCRA, 0, D), f32v(A1, X5 // 2 + sc_ * D, D), AF.Square, accum_out=stat[:, 1:2]))
        W("vector", "cA", vsq)
        vv = OP("vector", lambda e: e.tensor_scalar(stat[:, 2:3], stat[:, 1:2], 1.0 / D, 1e-6, ALU.mult, ALU.add))
        W("scalar", "cV", vv)
        OP("scalar", lambda e: e.activation(stat[:, 3:4], stat[:, 2:3], AF.Ln))
        vr2 = OP("scalar", lambda e: e.activation(stat[:, 4:5], stat[:, 3:4], AF.Exp, scale=epsc[:, 1:2]))
        W("vector", "cA", vr2)
        OP("vector", lambda e, sc_=sc_: e.tensor_scalar(
            f32v(SCRA, 0, D), f32v(A1, X5 // 2 + sc_ * D, D), stat[:, 4:5], None, ALU.mult))
        vhn = OP("vector", lambda e: e.tensor_tensor(
            bfv(SCRA, 2048, D), f32v(SCRA, 0, D), bcast, ALU.mult))
        vhn_prev = vhn
        W("tensor", "cV", vhn)
        for dc in range(DC):
            GPS()
            vt = OP("tensor", lambda e, dc=dc: e.transpose(
                ps_G.bitcast(BF16)[:, 0:128], bfv(SCRA, 2048 + dc * 128, 128), idbf))
            W("vector", "cP", vt)
            vv = OP("vector", lambda e, dc=dc, sc_=sc_: e.tensor_copy(
                bfv(A4, HN + dc * SELP + sc_ * 128, 128), ps_G.bitcast(BF16)[:, 0:128]))
            W("tensor", "cV", vv)

    TH, TW, TCK, TSK, TCQ, TSQ2 = 4096, 4608, 5120, 5632, 6144, 6656
    for sc_ in range(NSC):
        vv = OP("vector", lambda e, sc_=sc_: e.tensor_copy(poscf[:, sc_:sc_ + 1], posc[:, sc_:sc_ + 1]))
        W("sync", "cV", vv)
        DMA("sync", "dPW", w_scr[sc_ * 128:(sc_ + 1) * 128].rearrange("(p a) -> p a", a=1), poscf[:, sc_:sc_ + 1])
    W("sync", "dPW")
    vpr = DMA("sync", "dPR", posrow, w_scr[0:SELP].rearrange("(a s) -> a s", a=1))
    W("tensor", "dPR")
    GPS()
    vm = OP("tensor", lambda e: e.matmul(ps_G[:, 0:SELP], ones_row[0:1, 0:128], posrow, start=True, stop=True))
    W("vector", "cP", vm)
    vth = OP("vector", lambda e: e.tensor_scalar(f32v(SCRA, TH, SELP), ps_G[:, 0:SELP], invb, None, ALU.mult))

    def build_sincos(shift, dst_off):
        W("vector", "cA", Sp.val("cA"))
        OP("vector", lambda e: e.tensor_scalar(
            f32v(SCRA, TW, SELP), f32v(SCRA, TH, SELP), 1.0 / TWO_PI, 0.5 + shift / TWO_PI, ALU.mult, ALU.add))
        vv = OP("vector", lambda e: e.tensor_copy(i32v(SCRA, TW + 512, SELP), f32v(SCRA, TW, SELP)))
        OP("vector", lambda e: e.tensor_copy(f32v(SCRA, TW, SELP), i32v(SCRA, TW + 512, SELP)))
        OP("vector", lambda e: e.tensor_scalar(
            f32v(SCRA, TW, SELP), f32v(SCRA, TW, SELP), -TWO_PI, None, ALU.mult))
        OP("vector", lambda e: e.tensor_tensor(
            f32v(SCRA, TW, SELP), f32v(SCRA, TW, SELP), f32v(SCRA, TH, SELP), ALU.add))
        OP("vector", lambda e: e.tensor_scalar(
            f32v(SCRA, TW, SELP), f32v(SCRA, TW, SELP), shift, None, ALU.add))
        FB = f32v(SCRA, TW + 512, SELP)
        OP("vector", lambda e: e.tensor_scalar(FB, f32v(SCRA, TW, SELP), math.pi, None, ALU.is_gt))
        OP("vector", lambda e: e.tensor_scalar(FB, FB, TWO_PI, None, ALU.mult))
        OP("vector", lambda e: e.tensor_tensor(f32v(SCRA, TW, SELP), f32v(SCRA, TW, SELP), FB, ALU.subtract))
        OP("vector", lambda e: e.tensor_scalar(FB, f32v(SCRA, TW, SELP), -math.pi, None, ALU.is_lt))
        OP("vector", lambda e: e.tensor_scalar(FB, FB, TWO_PI, None, ALU.mult))
        vv = OP("vector", lambda e: e.tensor_tensor(f32v(SCRA, TW, SELP), f32v(SCRA, TW, SELP), FB, ALU.add))
        W("scalar", "cV", vv)
        return OP("scalar", lambda e, dst_off=dst_off: e.activation(
            f32v(SCRA, dst_off, SELP), f32v(SCRA, TW, SELP), AF.Sin))

    vs1 = build_sincos(0.0, TSK)
    vs2 = build_sincos(math.pi / 2.0, TCK)
    W("vector", "cA", vs2)
    OP("vector", lambda e: e.tensor_scalar(f32v(SCRA, TSK, SELP), f32v(SCRA, TSK, SELP), sgnb, None, ALU.mult))
    OP("vector", lambda e: e.tensor_scalar(f32v(SCRA, TCQ, SELP), f32v(SCRA, TCK, SELP), 0.125, None, ALU.mult))
    vtb = OP("vector", lambda e: e.tensor_scalar(f32v(SCRA, TSQ2, SELP), f32v(SCRA, TSK, SELP), 0.125, None, ALU.mult))

    FQ0, FK0 = 0, NP * SELP
    W("tensor", "cV", vtb)
    W("tensor", "dW")
    for (wq_off, dst0, tc_off, ts_off) in ((FQW, FQ0, TCQ, TSQ2), (FKW, FK0, TCK, TSK)):
        for pr in range(NP):
            vm = None
            for dc in range(DC):
                vm = OP("tensor", lambda e, pr=pr, dc=dc, wq_off=wq_off: e.matmul(
                    ps_G[:, 0:SELP], bfv(A2, wq_off + dc * D + pr * 128, 128),
                    bfv(A4, HN + dc * SELP, SELP), start=(dc == 0), stop=(dc == DC - 1)))
            W("vector", "cP", vm)
            v1 = OP("vector", lambda e: e.tensor_copy(bfv(SCRA, 2048, SELP), ps_G.bitcast(BF16)[:, 0:SELP] if False else ps_G[:, 0:SELP]))
            W("tensor", "cV", v1)
            GPS()
            vsw = OP("tensor", lambda e: e.matmul(ps_G[:, 0:SELP], pswapb, bfv(SCRA, 2048, SELP), start=True, stop=True))
            OP("vector", lambda e, tc_off=tc_off: e.tensor_tensor(
                f32v(SCRA, 0, SELP), bfv(SCRA, 2048, SELP), f32v(SCRA, tc_off, SELP), ALU.mult))
            W("vector", "cP", vsw)
            OP("vector", lambda e, ts_off=ts_off: e.tensor_tensor(
                f32v(SCRA, 512, SELP), ps_G[:, 0:SELP], f32v(SCRA, ts_off, SELP), ALU.mult))
            vv = OP("vector", lambda e, dst0=dst0, pr=pr: e.tensor_tensor(
                bfv(A3, dst0 + pr * SELP, SELP), f32v(SCRA, 0, SELP), f32v(SCRA, 512, SELP), ALU.add))
            W("tensor", "cV", vv)
    W("gpsimd", "cP", Sp.val("cP"))
    vfv = DMA("gpsimd", "dW", bfv(A2, FVW, DC * D).rearrange("p (dc o) -> p dc o", dc=DC), fwv.rearrange("(dc p) o -> p dc o", p=128))
    vfo = DMA("gpsimd", "dW", bfv(A2, FOW, DC * D).rearrange("p (dc o) -> p dc o", dc=DC), fwo.rearrange("(dc p) o -> p dc o", p=128))
    W("tensor", "dW")
    for sc_ in range(NSC):
        for hb in range(2):
            vm = None
            for dc in range(DC):
                vm = OP("tensor", lambda e, sc_=sc_, dc=dc, hb=hb: e.matmul(
                    ps_G[:, 0:512], bfv(A4, HN + dc * SELP + sc_ * 128, 128),
                    bfv(A2, FVW + dc * D + hb * 512, 512), start=(dc == 0), stop=(dc == DC - 1)))
            W("vector", "cP", vm)
            vv = OP("vector", lambda e, sc_=sc_, hb=hb: e.tensor_copy(
                bfv(A1, FV5 + sc_ * D + hb * 512, 512), ps_G[:, 0:512]))
            W("tensor", "cV", vv)

    vstg_dma = None
    for sc_ in range(NSC):
        if sc_ == NSC - 1:
            MSET("gpsimd", f32v(A1, MR // 2, S), 0.0)
        W("gpsimd", "cG", Sp.val("cG"))
        vg = IDMA("dGM", f32v(A1, MR // 2, S), None, m_in, IndirectOffsetOnAxis(selc[:, sc_:sc_ + 1], 0),
                  bounds_check=S - 1, oob_is_err=False)
        W("tensor", "dGM")
        W("tensor", "cG", Sp.val("cG"))
        for t in range(NKC):
            GPS()
            vt = OP("tensor", lambda e, t=t: e.transpose(
                ps_G[:, 0:128], f32v(A1, MR // 2 + t * 128, 128), idf32))
            W("vector", "cP", vt)
            if vstg_dma is not None:
                W("vector", "dMT")
            vv = OP("vector", lambda e: e.tensor_copy(f32v(SCRA, 6144, 128), ps_G[:, 0:128]))
            W("tensor", "cV", vv)
            W("sync", "cV", vv)
            vstg_dma = DMA("sync", "dMT", mt_scr[t * 128:(t + 1) * 128, sc_ * 128:(sc_ + 1) * 128],
                           f32v(SCRA, 6144, 128))
        W("gpsimd", "cP", Sp.val("cP"))
    W("gpsimd", "dMT")
    for sc_ in range(NSC):
        MSET("gpsimd", f32v(A1, MSL // 2 + sc_ * SELP, SELP), 0.0)
        W("gpsimd", "cG", Sp.val("cG"))
        IDMA("dGS", f32v(A1, MSL // 2 + sc_ * SELP, SELP), None, mt_scr, IndirectOffsetOnAxis(selc[:, sc_:sc_ + 1], 0),
             bounds_check=S - 1, oob_is_err=False)
    v_mselg = Sp.val("cG")

    FOT = 0
    W("vector", "dGS")
    W("vector", "cG", v_mselg)
    vex = None
    for pr in range(NP):
        vav = None
        for kc in range(NSC):
            if vex is not None:
                W("vector", "cA", vex)
            vs1 = OP("tensor", lambda e, pr=pr, kc=kc: e.matmul(
                ps_S[:, 0:SELP], bfv(A3, FK0 + pr * SELP + kc * 128, 128, p0=0, p=64),
                bfv(A3, FQ0 + pr * SELP, SELP, p0=0, p=64), start=True, stop=True))
            vs2 = OP("tensor", lambda e, pr=pr, kc=kc: e.matmul(
                ps_S[:, 512:512 + SELP], bfv(A3, FK0 + pr * SELP + kc * 128, 128, p0=64, p=64),
                bfv(A3, FQ0 + pr * SELP, SELP, p0=64, p=64), start=True, stop=True))
            W("vector", "cP", vs2)
            OP("vector", lambda e, kc=kc: e.tensor_tensor(
                f32v(SCRA, 0, SELP), ps_S[:, 0:SELP], f32v(A1, MSL // 2 + kc * SELP, SELP), ALU.add))
            vma = OP("vector", lambda e, kc=kc: e.tensor_tensor(
                f32v(SCRA, 512, SELP), ps_S[:, 512:512 + SELP], f32v(A1, MSL // 2 + kc * SELP, SELP), ALU.add))
            W("scalar", "cV", vma)
            if kc >= 2:
                W("scalar", "cP", vav)
            ptile = bfv(SCRA, 2048 + (kc % 2) * 1024, 1024)
            if SELP == 512:
                vex = OP("scalar", lambda e, ptile=ptile: e.activation(
                    ptile, f32v(SCRA, 0, 1024), AF.Exp))
            else:
                OP("scalar", lambda e, ptile=ptile: e.activation(
                    ptile[:, 0:SELP], f32v(SCRA, 0, SELP), AF.Exp))
                vex = OP("scalar", lambda e, ptile=ptile: e.activation(
                    ptile[:, 512:512 + SELP], f32v(SCRA, 512, SELP), AF.Exp))
            W("tensor", "cA", vex)
            ocol = ones_kpad_bf if kc == NSC - 1 else ones_col_bf
            OP("tensor", lambda e, pr=pr, kc=kc, ptile=ptile: e.matmul(
                ps_O[0:64, 0:SELP], bfv(A1, FV5 + kc * D + pr * 128, 64), ptile[:, 0:SELP],
                start=(kc == 0), stop=(kc == NSC - 1), tile_position=(0, 0)))
            vav = OP("tensor", lambda e, pr=pr, kc=kc, ptile=ptile: e.matmul(
                ps_O[64:128, 0:SELP], bfv(A1, FV5 + kc * D + pr * 128 + 64, 64), ptile[:, 512:512 + SELP],
                start=(kc == 0), stop=(kc == NSC - 1), tile_position=(0, 64),
                skip_group_check=True))
            dbank_a = ps_d0 if kc % 2 == 0 else ps_d1
            dbank_b = ps_d2 if kc % 2 == 0 else ps_d3
            rowa = 0 if kc % 2 == 0 else 32
            rowb = 64 if kc % 2 == 0 else 32
            st = kc < 2
            sp = kc >= NSC - 2
            OP("tensor", lambda e, dbank_a=dbank_a, rowa=rowa, st=st, sp=sp, ptile=ptile, ocol=ocol: e.matmul(
                dbank_a[rowa:rowa + 1, 0:SELP], ocol, ptile[:, 0:SELP],
                start=st, stop=sp, tile_position=(0, rowa)))
            vav = OP("tensor", lambda e, dbank_b=dbank_b, rowb=rowb, st=st, sp=sp, ptile=ptile, ocol=ocol: e.matmul(
                dbank_b[rowb:rowb + 1, 0:SELP], ocol, ptile[:, 512:512 + SELP],
                start=st, stop=sp, tile_position=(0, rowb)))
        W("vector", "cP", vav)
        hasodd = NSC > 1
        OP("vector", lambda e: e.tensor_copy(rrow[0:1, 0:SELP], ps_d0[0:1, 0:SELP]))
        if hasodd:
            OP("vector", lambda e: e.tensor_tensor(rrow[0:1, 0:SELP], rrow[0:1, 0:SELP], ps_d1[32:33, 0:SELP], ALU.add))
        OP("vector", lambda e: e.tensor_copy(rrow[0:1, 512:512 + SELP], ps_d2[64:65, 0:SELP]))
        if hasodd:
            OP("vector", lambda e: e.tensor_tensor(rrow[0:1, 512:512 + SELP], rrow[0:1, 512:512 + SELP], ps_d3[32:33, 0:SELP], ALU.add))
        if SELP == 512:
            vrc = OP("vector", lambda e: e.reciprocal(rrow, rrow))
        else:
            OP("vector", lambda e: e.reciprocal(rrow[0:1, 0:SELP], rrow[0:1, 0:SELP]))
            vrc = OP("vector", lambda e: e.reciprocal(rrow[0:1, 512:512 + SELP], rrow[0:1, 512:512 + SELP]))
        W("tensor", "cV", vrc)
        vb1 = OP("tensor", lambda e: e.matmul(
            ps_d0[0:64, 0:SELP], ones_row[0:1, 0:64], rrow[0:1, 0:SELP], start=True, stop=True))
        vb2 = OP("tensor", lambda e: e.matmul(
            ps_d0[64:128, 0:SELP], ones_row[0:1, 0:64], rrow[0:1, 512:512 + SELP],
            start=True, stop=True, tile_position=(0, 64)))
        W("vector", "cP", vb2)
        OP("vector", lambda e: e.tensor_copy(f32v(SCRA, 0, SELP), ps_d0[:, 0:SELP]))
        voT = OP("vector", lambda e, pr=pr: e.tensor_tensor(
            bfv(A4, FOT + pr * SELP, SELP), ps_O[:, 0:SELP], f32v(SCRA, 0, SELP), ALU.mult))
        W("tensor", "cV", voT)
    v_fattn = Sp.val("cV")

    W("tensor", "dW")
    for wc in range(NWC):
        for hb in range(2):
            vm = None
            for pr in range(NP):
                vm = OP("tensor", lambda e, wc=wc, pr=pr, hb=hb: e.matmul(
                    ps_G[:, 0:512], bfv(A4, FOT + pr * SELP + wc * 128, 128),
                    bfv(A2, FOW + pr * D + hb * 512, 512), start=(pr == 0), stop=(pr == NP - 1)))
            W("vector", "cP", vm)
            vv = OP("vector", lambda e, wc=wc, hb=hb: e.tensor_tensor(
                f32v(SCRA, wc * D + hb * 512, 512), ps_G[:, 0:512],
                f32v(A1, X5 // 2 + wc * D + hb * 512, 512), ALU.add))
            W("tensor", "cV", vv)
    v_x1 = Sp.val("cV")

    X1T = NP * SELP
    W("sync", "dL1")
    W("sync", "cP", Sp.val("cP"))
    vl2 = DMA("sync", "dL2", rowst, ln2_in.rearrange("(a d) -> a d", a=1))
    W("tensor", "dL2")
    bc_row(rowst, bcast, D)
    vhn_prev = Sp.val("cV")
    for wc in range(NWC):
        W("scalar", "cV", vhn_prev)
        vsq = OP("scalar", lambda e, wc=wc: e.activation(
            f32v(SCRA, NWC * D, D), f32v(SCRA, wc * D, D), AF.Square, accum_out=stat[:, 1:2]))
        W("vector", "cA", vsq)
        vv = OP("vector", lambda e: e.tensor_scalar(stat[:, 2:3], stat[:, 1:2], 1.0 / D, 1e-6, ALU.mult, ALU.add))
        W("scalar", "cV", vv)
        OP("scalar", lambda e: e.activation(stat[:, 3:4], stat[:, 2:3], AF.Ln))
        vr2 = OP("scalar", lambda e: e.activation(stat[:, 4:5], stat[:, 3:4], AF.Exp, scale=epsc[:, 1:2]))
        W("vector", "cA", vr2)
        OP("vector", lambda e, wc=wc: e.tensor_scalar(
            f32v(SCRA, NWC * D, D), f32v(SCRA, wc * D, D), stat[:, 4:5], None, ALU.mult))
        vhn = OP("vector", lambda e: e.tensor_tensor(
            bfv(SCRA, 2 * (NWC + 1) * D, D), f32v(SCRA, NWC * D, D), bcast, ALU.mult))
        vhn_prev = vhn
        W("tensor", "cV", vhn)
        for dc in range(DC):
            GPS()
            vt = OP("tensor", lambda e, dc=dc: e.transpose(
                ps_G.bitcast(BF16)[:, 0:128], bfv(SCRA, 2 * (NWC + 1) * D + dc * 128, 128), idbf))
            W("vector", "cP", vt)
            vv = OP("vector", lambda e, dc=dc, wc=wc: e.tensor_copy(
                bfv(A4, X1T + dc * WIN + wc * 128, 128), ps_G.bitcast(BF16)[:, 0:128]))
            W("tensor", "cV", vv)
    v_x1T = Sp.val("cV")

    H1G = 0
    _v0 = Sp.val("cP")
    w1_reads = {-2: _v0, -1: _v0}
    for fg in range(FC // 4):
        W("gpsimd", "cP", w1_reads[fg - 2])
        vw1 = DMA("gpsimd", f"dW1{fg % 2}", bfv(A3, 0 if fg % 2 == 0 else 4096, DC * 512).rearrange("p (dc f) -> p dc f", dc=DC),
                  fw1[:, fg * 512:(fg + 1) * 512].rearrange("(dc p) f -> p dc f", p=128))
        W("tensor", f"dW1{fg % 2}")
        for fs in range(4):
            vm = None
            for dc in range(DC):
                vm = OP("tensor", lambda e, fg=fg, fs=fs, dc=dc: e.matmul(
                    ps_G[:, 0:WIN], bfv(A3, (0 if fg % 2 == 0 else 4096) + dc * 512 + fs * 128, 128),
                    bfv(A4, X1T + dc * WIN, WIN), start=(dc == 0), stop=(dc == DC - 1)))
            w1_reads[fg] = vm
            W("vector", "cP", vm)
            OP("vector", lambda e: e.tensor_copy(f32v(SCRA, 2 * (NWC + 1) * D // 2, WIN), ps_G[:, 0:WIN]))
            h1 = f32v(SCRA, 2 * (NWC + 1) * D // 2, WIN)
            vv = OP("vector", lambda e, h1=h1: e.tensor_tensor(f32v(SCRA, 7000, WIN), h1, h1, ALU.mult))
            W("tensor", "cV", vv)
            OP("vector", lambda e: e.tensor_scalar(f32v(SCRA, 7000, WIN), f32v(SCRA, 7000, WIN), C1G, C0G, ALU.mult, ALU.add))
            vu = OP("vector", lambda e, h1=h1: e.tensor_tensor(f32v(SCRA, 7424, WIN), f32v(SCRA, 7000, WIN), h1, ALU.mult))
            W("scalar", "cV", vu)
            vt = OP("scalar", lambda e: e.activation(f32v(SCRA, 7000, WIN), f32v(SCRA, 7424, WIN), AF.Tanh))
            W("vector", "cA", vt)
            OP("vector", lambda e: e.tensor_scalar(f32v(SCRA, 7000, WIN), f32v(SCRA, 7000, WIN), 0.5, 0.5, ALU.mult, ALU.add))
            vg = OP("vector", lambda e, fg=fg, fs=fs, h1=h1: e.tensor_tensor(
                bfv(A1, H1G + (fg * 4 + fs) * WIN, WIN), f32v(SCRA, 7000, WIN), h1, ALU.mult))
    v_h1g = Sp.val("cV")

    _v0 = Sp.val("cP")
    w2_reads = {(0,): _v0, (1,): _v0}
    W("tensor", "cV", v_h1g)
    for wc in range(NWC):
        for hb in range(2):
            vm = None
            for fg in range(FC // 4):
                key = (fg % 2,)
                W("gpsimd", "cP", w2_reads[key])
                vw2 = DMA("gpsimd", f"dW2{fg % 2}", bfv(A2, FVW + 2048 * (fg % 2), 4 * 512).rearrange("p (g o) -> p g o", g=4),
                          fw2[fg * 512:(fg + 1) * 512, hb * 512:(hb + 1) * 512].rearrange("(g p) o -> p g o", p=128))
                W("tensor", f"dW2{fg % 2}")
                for g in range(4):
                    vm = OP("tensor", lambda e, wc=wc, hb=hb, fg=fg, g=g: e.matmul(
                        ps_G[:, 0:512], bfv(A1, H1G + (fg * 4 + g) * WIN + wc * 128, 128),
                        bfv(A2, FVW + 2048 * (fg % 2) + g * 512, 512),
                        start=(fg == 0 and g == 0), stop=(fg == FC // 4 - 1 and g == 3)))
                w2_reads[key] = vm
            W("vector", "cP", vm)
            OP("vector", lambda e, wc=wc, hb=hb: e.tensor_tensor(
                f32v(SCRA, wc * D + hb * 512, 512), f32v(SCRA, wc * D + hb * 512, 512), ps_G[:, 0:512], ALU.add))
            vpr2 = OP("vector", lambda e, wc=wc, hb=hb: e.tensor_scalar(
                f32v(SCRA, wc * D + hb * 512, 512), f32v(SCRA, wc * D + hb * 512, 512),
                wselc[:, wc:wc + 1], None, ALU.mult))
            W("tensor", "cV", vpr2)
        W("gpsimd", "cV", Sp.val("cV"))
        IDMA("dSC", xn_scr, IndirectOffsetOnAxis(selc[:, wc:wc + 1], 0), f32v(SCRA, wc * D, D), None,
             bounds_check=S - 1, oob_is_err=False)
    Sp.barrier()

    vo_hist = {}

    def ph6_load(tc):
        b = tc % 2
        g0 = f32v(SCRA, b * D, D)
        s0 = f32v(SCRA, 2048 + b * D, D)
        x0 = f32v(SCRA, 4096 + b * D, D)
        if tc >= 2:
            W("sync", "cV", vo_hist[tc - 2])
            W("scalar", "cV", vo_hist[tc - 2])
            W("gpsimd", "cV", vo_hist[tc - 2])
            W("sync", f"dY{b}")
        DMA("sync", f"dE{b}", g0, xn_scr[tc * 128:(tc + 1) * 128, :])
        DMA("scalar", f"dES{b}", s0, s_scr[tc * 128:(tc + 1) * 128, :])
        DMA("gpsimd", f"dX6{b}", x0, x_in[tc * 128:(tc + 1) * 128, :])

    def ph6_compute(tc):
        b = tc % 2
        g0 = f32v(SCRA, b * D, D)
        s0 = f32v(SCRA, 2048 + b * D, D)
        x0 = f32v(SCRA, 4096 + b * D, D)
        W("scalar", f"dE{b}")
        vsl = OP("scalar", lambda e, g0=g0: e.activation(g0, g0, AF.Silu))
        W("vector", "cA", vsl)
        W("vector", f"dES{b}")
        W("vector", f"dX6{b}")
        OP("vector", lambda e, g0=g0, s0=s0: e.tensor_tensor(g0, g0, s0, ALU.mult))
        vo = OP("vector", lambda e, g0=g0, x0=x0: e.tensor_tensor(g0, g0, x0, ALU.add))
        vo_hist[tc] = vo
        W("sync", "cV", vo)
        DMA("sync", f"dY{b}", y_out[tc * 128:(tc + 1) * 128, :], g0)

    ph6_load(0)
    for tc in range(NQC):
        if tc + 1 < NQC:
            ph6_load(tc + 1)
        ph6_compute(tc)
    Sp.barrier()
    if dbg:
        dbg_a4 = nc.dram_tensor("dbg_a4", [128, A4.shape[-1]], BF16, kind="ExternalOutput").ap()
        dbg_a1 = nc.dram_tensor("dbg_a1", [128, NSC * SELP], F32, kind="ExternalOutput").ap()
        dbg_ws = nc.dram_tensor("dbg_ws", [128, NSC], F32, kind="ExternalOutput").ap()
        dbg_sel = nc.dram_tensor("dbg_sel", [128, NSC], I32, kind="ExternalOutput").ap()
        dbg_wpm = nc.dram_tensor("dbg_wpm", [128, NKC], F32, kind="ExternalOutput").ap()
        dbg_bis = nc.dram_tensor("dbg_bis", [128, 16], F32, kind="ExternalOutput").ap()
        dbg_sc1 = nc.dram_tensor("dbg_sc1", [2, 16], F32, kind="ExternalOutput").ap()
        dbg_rank = nc.dram_tensor("dbg_rank", [128, NKC], F32, kind="ExternalOutput").ap()
        dbg_mask = nc.dram_tensor("dbg_mask", [128, NKC], F32, kind="ExternalOutput").ap()
        dbg_stat = nc.dram_tensor("dbg_stat", [128, 24], F32, kind="ExternalOutput").ap()
        DMA("sync", "dDBG", dbg_a4, A4[:, :])
        DMA("sync", "dDBG", dbg_a1, f32v(A1, MSL // 2, NSC * SELP))
        DMA("sync", "dDBG", dbg_ws, wselc)
        DMA("sync", "dDBG", dbg_sel, selc)
        DMA("sync", "dDBG", dbg_wpm, w_pm)
        DMA("sync", "dDBG", dbg_bis, bis)
        DMA("sync", "dDBG", dbg_sc1, sc1)
        DMA("sync", "dDBG", dbg_rank, rankc)
        vmf = OP("vector", lambda e: e.tensor_copy(stat[:, 0:NKC] if NKC <= 24 else stat[:, 0:24], maskf[:, 0:min(NKC, 24)]))
        W("sync", "cV", vmf)
        DMA("sync", "dDBG", dbg_mask, maskf if False else maskf)
        DMA("sync", "dDBG", dbg_stat, stat)
        Sp.barrier()
    Sp.build()
    return nc


def make_in_maps(cfg, inputs):
    S, SQ = cfg.S, cfg.SQ
    x = np.ascontiguousarray(inputs["L0_attn_x"])
    m = inputs["L0_ffn_attention_mask"]
    pos = inputs["L0_ffn_position_ids"]
    B = x.shape[0]
    in_maps = []
    for core in range(2 * B):
        b, half = core // 2, core % 2
        perm = np.concatenate([np.arange(half * SQ, (half + 1) * SQ),
                               np.arange((1 - half) * SQ, (2 - half) * SQ) % S])
        cst = host_constants(cfg, perm)
        xb = np.ascontiguousarray(x[b][perm])
        mb = np.ascontiguousarray(m[b, 0][perm][:, perm])
        pb = np.ascontiguousarray(pos[b][perm]).astype(np.int32)
        im = {
            "x": xb, "m": mb, "pos": pb,
            "wdq_t": np.ascontiguousarray(inputs["L0_attn_w_down_q"].T),
            "wdkv_t": np.ascontiguousarray(inputs["L0_attn_w_down_kv"].T),
            "wuq_t": np.ascontiguousarray(inputs["L0_attn_w_up_q"].T),
            "wuk_t": np.ascontiguousarray(inputs["L0_attn_w_up_k"].T),
            "wuv_t": np.ascontiguousarray(inputs["L0_attn_w_up_v"].T),
            "wo_t": np.ascontiguousarray(inputs["L0_attn_w_out"].T),
            "g2": inputs["L0_attn_ln_scale"], "b2": inputs["L0_attn_ln_bias"],
            "rw": np.ascontiguousarray(inputs["L0_ffn_router_w"][:, 0]),
            "rb": inputs["L0_ffn_router_b"],
            "fwq": inputs["L0_ffn_wq"], "fwk": inputs["L0_ffn_wk"],
            "fwv": inputs["L0_ffn_wv"], "fwo": inputs["L0_ffn_wo"],
            "fw1": inputs["L0_ffn_w1"], "fw2": inputs["L0_ffn_w2"],
            "ln1": inputs["L0_ffn_ln1"], "ln2": inputs["L0_ffn_ln2"],
        }
        im.update(cst)
        in_maps.append(im)
    return in_maps


def assemble(cfg, results, B):
    S, SQ, D = cfg.S, cfg.SQ, cfg.D
    out = np.zeros((B, S, D), np.float32)
    for core, res in enumerate(results):
        b, half = core // 2, core % 2
        out[b, half * SQ:(half + 1) * SQ] = res["y"]
    return out


def kernel(**inputs):
    from concourse.bass_utils import run_bass_kernel_spmd
    cfg = Cfg(4096)
    nc = build_program(cfg)
    in_maps = make_in_maps(cfg, inputs)
    res = run_bass_kernel_spmd(nc, in_maps, core_ids=list(range(len(in_maps))))
    return assemble(cfg, res.results, inputs["L0_attn_x"].shape[0])


# version 11
# speedup vs baseline: 1.1292x; 1.1292x over previous
import math
import numpy as np
import ml_dtypes
import concourse.bass as bass
import concourse.mybir as mybir
from concourse.bass_core_rust import IndirectOffsetOnAxis

F32 = mybir.dt.float32
BF16 = mybir.dt.bfloat16
I32 = mybir.dt.int32
U16 = mybir.dt.uint16
AF = mybir.ActivationFunctionType
ALU = mybir.AluOpType
AXX = mybir.AxisListType.X

C0G = 0.7978845608028654
C1G = C0G * 0.044715
TWO_PI = 2.0 * math.pi


class Cfg:
    def __init__(self, S=4096):
        self.S = S
        self.SQ = S // 2
        self.D = 1024
        self.H = 16
        self.DH = 64
        self.NP = 8
        self.C = 512
        self.F = 4096
        self.KSEL = S // 8
        self.KC = self.KSEL - 1
        self.SELP = max(128, ((self.KC + 127) // 128) * 128)
        self.NSC = self.SELP // 128
        self.NKC = S // 128
        self.NQC = self.SQ // 128
        self.QB = min(512, self.SQ)
        self.NQB = self.SQ // self.QB
        self.DC = self.D // 128
        self.CC = self.C // 128
        self.FC = self.F // 128
        self.WIN = min(3 * 128, self.SELP) if self.SELP >= 512 else self.SELP
        self.NWC = self.WIN // 128


def host_constants(cfg, perm):
    S, SQ, SELP, NKC = cfg.S, cfg.SQ, cfg.SELP, cfg.NKC
    t = np.arange(S, dtype=np.float64)[perm]
    inv = 10000.0 ** (-(np.arange(32) % 16) / 16.0)
    th = t[:, None] * inv[None, :]
    cosA = np.cos(th).astype(np.float32)
    sinA = np.sin(th).astype(np.float32)
    rows_j = (np.arange(128) % 64) // 2
    sgn = np.where(np.arange(128) % 2 == 0, -1.0, 1.0).astype(np.float32)
    C_tab = np.ascontiguousarray(cosA[:, rows_j].T)
    S_tab = np.ascontiguousarray(sinA[:, rows_j].T * sgn[:, None])
    okp = np.ones((128, 1), np.float32)
    okp[(cfg.KC % 128) if cfg.SELP > cfg.KC else 128:, 0] = 0.0
    pf = np.zeros((1, SELP), np.float32)
    pf[0, cfg.KC:] = float(2 * S)
    swap = np.arange(128)
    swap = np.where(swap % 64 < 32, swap + 32, swap - 32)
    P = np.zeros((128, 128), np.float32)
    P[swap, np.arange(128)] = 1.0
    return {
        "c_ctabk": C_tab.astype(ml_dtypes.bfloat16),
        "c_stabk": S_tab.astype(ml_dtypes.bfloat16),
        "c_ctabq": np.ascontiguousarray((C_tab[:, :SQ] * 0.125).astype(ml_dtypes.bfloat16)),
        "c_stabq": np.ascontiguousarray((S_tab[:, :SQ] * 0.125).astype(ml_dtypes.bfloat16)),
        "c_idf32": np.eye(128, dtype=np.float32),
        "c_idbf": np.eye(128, dtype=np.float32).astype(ml_dtypes.bfloat16),
        "c_ones_row": np.ones((1, 512), np.float32),
        "c_upper01": (np.arange(128)[:, None] <= np.arange(128)[None, :]).astype(ml_dtypes.bfloat16),
        "c_ones_col_bf": np.ones((128, 1), np.float32).astype(ml_dtypes.bfloat16),
        "c_ones_kpad_bf": okp.astype(ml_dtypes.bfloat16),
        "c_iota_rank": np.broadcast_to(np.arange(1, SELP + 1, dtype=np.float32)[None, :], (128, SELP)).copy(),
        "c_c128": np.broadcast_to((128.0 * np.arange(NKC, dtype=np.float32))[None, :], (128, NKC)).copy().astype(ml_dtypes.bfloat16),
        "c_pcol": np.arange(128, dtype=np.float32).reshape(128, 1),
        "c_pcol_bf": np.arange(128, dtype=np.float32).astype(ml_dtypes.bfloat16).reshape(128, 1),
        "c_padfix": pf,
        "c_pswapb": P.astype(ml_dtypes.bfloat16),
        "c_invb": (10000.0 ** (-2 * (np.arange(128) % 32) / 64.0)).astype(np.float32).reshape(128, 1),
        "c_eps": np.broadcast_to(np.array([1e-5, -0.5], np.float32)[None, :], (128, 2)).copy(),
        "c_sgnb": np.where(np.arange(128) % 64 < 32, -1.0, 1.0).astype(np.float32).reshape(128, 1),
    }


CONST_SPECS = lambda cfg: [
    ("c_ctabk", [128, cfg.S], BF16), ("c_stabk", [128, cfg.S], BF16),
    ("c_ctabq", [128, cfg.SQ], BF16), ("c_stabq", [128, cfg.SQ], BF16),
    ("c_idf32", [128, 128], F32), ("c_idbf", [128, 128], BF16),
    ("c_ones_row", [1, 512], F32), ("c_upper01", [128, 128], BF16),
    ("c_ones_col_bf", [128, 1], BF16), ("c_ones_kpad_bf", [128, 1], BF16),
    ("c_iota_rank", [128, cfg.SELP], F32), ("c_c128", [128, cfg.NKC], BF16),
    ("c_pcol", [128, 1], F32), ("c_pcol_bf", [128, 1], BF16),
    ("c_padfix", [1, cfg.SELP], F32), ("c_pswapb", [128, 128], BF16),
    ("c_invb", [128, 1], F32), ("c_sgnb", [128, 1], F32),
    ("c_eps", [128, 2], F32),
]


class Script:
    ENGINES = ("sync", "vector", "scalar", "tensor", "gpsimd")

    def __init__(self, nc):
        self.nc = nc
        self.q = {e: [] for e in self.ENGINES}
        self.sems = {}

    def sem(self, name):
        if name not in self.sems:
            self.sems[name] = [self.nc.alloc_semaphore(name), 0]
        return self.sems[name]

    def inc(self, name, k=1):
        s = self.sem(name)
        s[1] += k
        return s[0], s[1]

    def val(self, name):
        return self.sem(name)[1]

    def do(self, eng, fn):
        self.q[eng].append(fn)

    def barrier(self):
        snap = [(s[0], s[1]) for s in self.sems.values() if s[1] > 0]
        for e in self.ENGINES:
            def f(eng, snap=snap):
                for h, v in snap:
                    eng.wait_ge(h, v)
            self.do(e, f)

    def build(self):
        with self.nc.Block() as blk:
            blk.sync(lambda e: [f(e) for f in self.q["sync"]])
            blk.vector(lambda e: [f(e) for f in self.q["vector"]])
            blk.scalar(lambda e: [f(e) for f in self.q["scalar"]])
            blk.tensor(lambda e: [f(e) for f in self.q["tensor"]])
            blk.gpsimd(lambda e: [f(e) for f in self.q["gpsimd"]])


def build_program(cfg, dbg=False):
    nc = bass.Bass("TRN2", target_bir_lowering=False)
    S, SQ, D, C, F = cfg.S, cfg.SQ, cfg.D, cfg.C, cfg.F
    NP, KC, SELP, NSC, NKC, NQC, QB, NQB = (cfg.NP, cfg.KC, cfg.SELP, cfg.NSC,
                                            cfg.NKC, cfg.NQC, cfg.QB, cfg.NQB)
    DC, CC, FC, WIN, NWC = cfg.DC, cfg.CC, cfg.FC, cfg.WIN, cfg.NWC
    KSEL = cfg.KSEL

    def ein(name, shape, dt=F32):
        return nc.dram_tensor(name, shape, dt, kind="ExternalInput").ap()

    x_in = ein("x", [S, D])
    m_in = ein("m", [S, S])
    pos_in = ein("pos", [S], I32)
    wdq_t = ein("wdq_t", [D, C]); wdkv_t = ein("wdkv_t", [D, C])
    wuq_t = ein("wuq_t", [C, D]); wuk_t = ein("wuk_t", [C, D]); wuv_t = ein("wuv_t", [C, D])
    wo_t = ein("wo_t", [D, D])
    g2_in = ein("g2", [D]); b2_in = ein("b2", [D])
    rw_in = ein("rw", [D]); rb_in = ein("rb", [1])
    fwq = ein("fwq", [D, D]); fwk = ein("fwk", [D, D]); fwv = ein("fwv", [D, D]); fwo = ein("fwo", [D, D])
    fw1 = ein("fw1", [D, F]); fw2 = ein("fw2", [F, D])
    ln1_in = ein("ln1", [D]); ln2_in = ein("ln2", [D])
    cst = {nm: ein(nm, sh, dt) for nm, sh, dt in CONST_SPECS(cfg)}

    y_out = nc.dram_tensor("y", [SQ, D], F32, kind="ExternalOutput").ap()

    xn_scr = nc.dram_tensor("xn_scr", [S, D], F32, kind="ExternalOutput").ap()
    hp_scr = nc.dram_tensor("hp_scr", [S, D], BF16, kind="ExternalOutput").ap()
    mt_scr = nc.dram_tensor("mt_scr", [S, SELP], F32, kind="ExternalOutput").ap()
    sel_cl = nc.dram_tensor("sel_cl", [2 * SELP], U16, kind="ExternalOutput").ap()
    s_scr = nc.dram_tensor("s_scr", [SQ, D], F32, kind="ExternalOutput").ap()
    w_scr = nc.dram_tensor("w_scr", [S], F32, kind="ExternalOutput").ap()
    sel_scr = nc.dram_tensor("sel_scr", [SELP], I32, kind="ExternalOutput").ap()
    q_scr = nc.dram_tensor("q_scr", [NP, 128, SQ], BF16, kind="ExternalOutput").ap()
    o_scr = nc.dram_tensor("o_scr", [NP, 128, SQ], BF16, kind="ExternalOutput").ap()

    def sb(name, shape, dt):
        return nc.alloc_sbuf_tensor(name, shape, dt).ap()

    idf32 = sb("idf32", [128, 128], F32)
    idbf = sb("idbf", [128, 128], BF16)
    ones_row = sb("ones_row", [1, 512], F32)
    onesbf_row = sb("onesbf_row", [1, 512], BF16)
    upper01 = sb("upper01", [128, 128], BF16)
    ones_col_bf = sb("ones_col_bf", [128, 1], BF16)
    ones_kpad_bf = sb("ones_kpad_bf", [128, 1], BF16)
    iota_rank = sb("iota_rank", [128, SELP], F32)
    c128c = sb("c128c", [128, NKC], BF16)
    pcol = sb("pcol", [128, 1], F32)
    pcol_bf = sb("pcol_bf", [128, 1], BF16)
    padfix = sb("padfix", [1, SELP], F32)
    pswapb = sb("pswapb", [128, 128], BF16)
    invb = sb("invb", [128, 1], F32)
    sgnb = sb("sgnb", [128, 1], F32)
    epsc = sb("epsc", [128, 2], F32)
    g2c = sb("g2c", [128, DC], F32)
    b2cc = sb("b2cc", [128, DC], F32)
    b2bf = sb("b2bf", [128, DC], BF16)
    vrowbf = sb("vrowbf", [1, 2 * C], BF16)
    rowst = sb("rowst", [1, D], F32)
    bcast = sb("bcast", [128, D], F32)
    stat = sb("stat", [128, 24], F32)
    w_pm = sb("w_pm", [128, NKC], F32)
    bis = sb("bis", [128, 16], F32)
    sc1 = sb("sc1", [2, 16], F32)
    rankc = sb("rankc", [128, NKC], F32)
    maskc = sb("maskc", [128, NKC], BF16)
    iw16 = sb("iw16", [128, SELP // 16], U16)
    maskf = sb("maskf", [128, NKC], F32)
    oh = sb("oh", [128, SELP], BF16)
    selrow = sb("selrow", [1, SELP], F32)
    selc = sb("selc", [128, NSC], I32)
    posc = sb("posc", [128, NSC], I32)
    wselc = sb("wselc", [128, NSC], F32)
    rrow = sb("rrow", [1, 1024], F32)
    vrow = rrow[0:1, 0:2 * C]
    selrow_i = rrow[0:1, 512:512 + SELP].bitcast(I32)
    posrow = rrow[0:1, 0:SELP]
    poscf = sb("poscf", [128, NSC], F32)

    SCRA = sb("SCRA", [128, 16384], BF16)
    PH5_A1 = 2 * NSC * D + 2 * S + NSC * D + 2 * NSC * SELP
    VST = 8 * 65
    A1 = sb("A1", [128, max(4 * S + NKC * VST, PH5_A1, FC * WIN)], BF16)
    A2 = sb("A2", [128, max(CC * S, 2 * DC * D)], BF16)
    A3 = sb("A3", [128, max(CC * SQ, 8192, 2 * NP * SELP)], BF16)
    A4 = sb("A4", [128, max(2 * S, NP * SELP + DC * WIN, 2 * SQ)], BF16)
    WA = sb("WA", [128, max(2 * DC * C, 2 * CC * D, DC * D)], BF16)

    def bfv(arena, c0, n, p0=0, p=128):
        return arena[p0:p0 + p, c0:c0 + n]

    def f32v(arena, c0, n, p0=0, p=128):
        return arena[p0:p0 + p, 2 * c0:2 * (c0 + n)].bitcast(F32)

    def i32v(arena, c0, n, p0=0, p=128):
        return arena[p0:p0 + p, 2 * c0:2 * (c0 + n)].bitcast(I32)

    PS_A = nc.alloc_psum_tensor("PS_A", [128, 2048], F32).ap()
    PS_B = nc.alloc_psum_tensor("PS_B", [128, 2048], F32).ap()
    ps_S = PS_A[:, 0:1024]
    ps_G = PS_A[:, 1024:1536]
    ps_d3 = PS_A[:, 1536:2048]
    ps_O = PS_B[:, 0:512]
    ps_d0 = PS_B[:, 512:1024]
    ps_d1 = PS_B[:, 1024:1536]
    ps_d2 = PS_B[:, 1536:2048]

    Sp = Script(nc)

    def DMA(queue, ch, out, in_, **kw):
        h, v = Sp.inc(ch, 16)
        Sp.do(queue, lambda e, out=out, in_=in_, kw=kw, h=h: e.dma_start(out=out, in_=in_, **kw).then_inc(h, 16))
        return v

    def IDMA(ch, out, out_off, in_, in_off, **kw):
        h, v = Sp.inc(ch, 16)
        Sp.do("gpsimd", lambda e, out=out, out_off=out_off, in_=in_, in_off=in_off, kw=kw, h=h:
              e.indirect_dma_start(out, out_off, in_, in_off, **kw).then_inc(h, 16))
        return v

    def W(eng, sem, v=None):
        s = Sp.sem(sem)
        if v is None:
            v = s[1]
        if v <= 0:
            return
        Sp.do(eng, lambda e, h=s[0], v=v: e.wait_ge(h, v))

    SEM_OF = {"tensor": "cP", "vector": "cV", "scalar": "cA", "gpsimd": "cG"}

    def OP(eng, fn, selfwait=True):
        s = Sp.sem(SEM_OF[eng])
        nprev = s[1]
        h, v = Sp.inc(SEM_OF[eng], 1)
        if selfwait and nprev > 0 and eng != "tensor":
            Sp.do(eng, lambda e, fn=fn, h=h, nprev=nprev: (
                e.wait_ge(h, nprev), fn(e).then_inc(h, 1)))
        else:
            Sp.do(eng, lambda e, fn=fn, h=h: fn(e).then_inc(h, 1))
        return v

    def OPn(eng, fn):
        Sp.do(eng, fn)

    def GPS(eng="tensor"):
        W(eng, "cV", Sp.val("cV"))

    def MSET(eng, ap, val):
        return OP(eng, lambda e, ap=ap, val=val: e.memset(ap, val))

    MSET("vector", ps_d3[:, :], 0.0)
    MSET("vector", bis[:, :], 0.0)
    MSET("vector", stat[:, :], 0.0)
    v_init0 = MSET("vector", sc1[:, :], 0.0)
    W("scalar", "cV", v_init0)
    for dst, key in [(idf32, "c_idf32"), (idbf, "c_idbf"), (ones_row, "c_ones_row"),
                     (upper01, "c_upper01"), (ones_col_bf, "c_ones_col_bf"),
                     (ones_kpad_bf, "c_ones_kpad_bf"), (iota_rank, "c_iota_rank"),
                     (c128c, "c_c128"), (pcol, "c_pcol"), (pcol_bf, "c_pcol_bf"),
                     (padfix, "c_padfix"), (pswapb, "c_pswapb"), (invb, "c_invb"),
                     (sgnb, "c_sgnb"), (epsc, "c_eps")]:
        DMA("sync", "dC", dst, cst[key])
    DMA("gpsimd", "dC2", onesbf_row, cst["c_ones_row"])
    T_CQ, T_SQ = 0, SQ
    T_CK, T_SK = 0, S
    DMA("sync", "dC", bfv(A4, T_CQ, SQ), cst["c_ctabq"])
    DMA("sync", "dC", bfv(A4, T_SQ, SQ), cst["c_stabq"])
    DMA("sync", "dC", g2c, g2_in.rearrange("(c p) -> p c", p=128), allow_slow_non_contiguous=True)
    DMA("sync", "dC", b2cc, b2_in.rearrange("(c p) -> p c", p=128), allow_slow_non_contiguous=True)
    W_A, W_B = 0, DC * C
    DMA("gpsimd", "dW", bfv(WA, W_A, DC * C).rearrange("p (dc c) -> p dc c", dc=DC), wdq_t.rearrange("(dc p) c -> p dc c", p=128))
    vwAB = DMA("gpsimd", "dW", bfv(WA, W_B, DC * C).rearrange("p (dc c) -> p dc c", dc=DC), wdkv_t.rearrange("(dc p) c -> p dc c", p=128))

    vrw = DMA("sync", "dC", rowst, rw_in.rearrange("(a d) -> a d", a=1))

    def bc_row(src_row, dst_full, width, wait_val=None):
        for i in range((width + 511) // 512):
            w0, w1 = i * 512, min(width, (i + 1) * 512)
            W("tensor", "cV", Sp.val("cV"))
            vm = OP("tensor", lambda e, w0=w0, w1=w1: e.matmul(
                ps_G[:, 0:w1 - w0], ones_row[0:1, 0:128], src_row[0:1, w0:w1], start=True, stop=True))
            W("vector", "cP", vm)
            OP("vector", lambda e, w0=w0, w1=w1: e.tensor_copy(dst_full[:, w0:w1], ps_G[:, 0:w1 - w0]))

    W("tensor", "dC")
    bc_row(rowst, bcast, D)
    v_rwbc = Sp.val("cV")

    W("vector", "dW")
    W("vector", "dC")
    for dc in range(DC):
        OP("vector", lambda e, dc=dc: e.tensor_scalar(
            bfv(WA, W_A + dc * C, C), bfv(WA, W_A + dc * C, C), g2c[:, dc:dc + 1], None, ALU.mult))
        OP("vector", lambda e, dc=dc: e.tensor_scalar(
            bfv(WA, W_B + dc * C, C), bfv(WA, W_B + dc * C, C), g2c[:, dc:dc + 1], None, ALU.mult))
    OP("vector", lambda e: e.reciprocal(stat[:, 16:16 + DC], g2c))
    OP("vector", lambda e: e.tensor_tensor(stat[:, 16:16 + DC], b2cc, stat[:, 16:16 + DC], ALU.mult))
    vbg = OP("vector", lambda e: e.tensor_copy(b2bf, stat[:, 16:16 + DC]))
    W("tensor", "cV", vbg)
    for row, woff in ((0, W_A), (1, W_B)):
        vm = None
        W("tensor", "cV", Sp.val("cV"))
        for dc in range(DC):
            vm = OP("tensor", lambda e, row=row, woff=woff, dc=dc: e.matmul(
                ps_G[0:1, 0:C], b2bf[:, dc:dc + 1], bfv(WA, woff + dc * C, C),
                start=(dc == 0), stop=(dc == DC - 1)))
        W("vector", "cP", vm)
        OP("vector", lambda e, row=row: e.tensor_copy(vrow[0:1, row * C:(row + 1) * C], ps_G[0:1, 0:C]))
    v_vrow = OP("vector", lambda e: e.tensor_copy(vrowbf, vrow))

    hist = {}

    def ph1_a(t):
        b = t % 2
        xc = f32v(SCRA, b * D, D)
        if t >= 2:
            W("sync", "cV", hist[t - 2]["vxn"])
            W("sync", "cA", hist[t - 2]["vcp"])
        DMA("sync", f"dX{b}", xc, x_in[t * 128:(t + 1) * 128, :])
        dumpA = f32v(SCRA, 4096, D)
        W("scalar", f"dX{b}")
        if t >= 2:
            W("scalar", "cV", hist[t - 2]["v2"])
        vsq = OP("scalar", lambda e, xc=xc, dumpA=dumpA, b=b: e.activation(
            dumpA, xc, AF.Square, accum_out=stat[:, 12 * b + 1:12 * b + 2]))
        vcp = OP("scalar", lambda e, xc=xc, dumpA=dumpA, b=b: e.activation(
            dumpA, xc, AF.Copy, accum_out=stat[:, 12 * b:12 * b + 1]))
        hist[t] = {"vsq": vsq, "vcp": vcp}

    def ph1_b(t):
        b = t % 2
        sc = 12 * b
        st = lambda k: stat[:, sc + k:sc + k + 1]
        xc = f32v(SCRA, b * D, D)
        xnc = f32v(SCRA, 2048 + b * D, D)
        dumpB = f32v(SCRA, 5120, D)
        hpc = bfv(SCRA, 12288 + b * D, D)
        W("vector", "cA", hist[t]["vcp"])
        if t == 0:
            W("vector", "cV", v_rwbc)
        OP("vector", lambda e: e.tensor_scalar(st(2), st(0), 1.0 / D, None, ALU.mult))
        OP("vector", lambda e: e.tensor_tensor(st(3), st(2), st(2), ALU.mult))
        v2 = OP("vector", lambda e: e.scalar_tensor_tensor(
            st(4), st(1), 1.0 / D, st(3), ALU.mult, ALU.subtract))
        hist[t]["v2"] = v2
        W("scalar", "cV", v2)
        OP("scalar", lambda e: e.activation(st(5), st(4), AF.Ln, bias=epsc[:, 0:1]))
        v3 = OP("scalar", lambda e: e.activation(st(6), st(5), AF.Exp, scale=epsc[:, 1:2]))
        OP("vector", lambda e: e.tensor_scalar(st(7), st(4), 1e-5, None, ALU.add))
        OP("vector", lambda e: e.reciprocal(st(8), st(7)))
        v5 = OP("vector", lambda e: e.tensor_tensor(st(9), st(4), st(8), ALU.mult))
        W("scalar", "cV", v5)
        OP("scalar", lambda e: e.activation(st(10), st(9), AF.Ln, bias=epsc[:, 0:1]))
        v6 = OP("scalar", lambda e: e.activation(st(11), st(10), AF.Exp, scale=epsc[:, 1:2]))
        W("vector", "cA", v3)
        if t >= 2:
            W("vector", f"dXN{b}")
        vxn = OP("vector", lambda e, xc=xc, xnc=xnc: e.tensor_scalar(
            xnc, xc, st(2), st(6), ALU.subtract, ALU.mult))
        vwc = OP("vector", lambda e, xnc=xnc, dumpB=dumpB, t=t: e.scalar_tensor_tensor(
            dumpB, xnc, 1.0, bcast, ALU.mult, ALU.mult, accum_out=w_pm[:, t:t + 1]))
        W("vector", "cA", v6)
        if t >= 2:
            W("vector", f"dHP{b}")
        vhp = OP("vector", lambda e, xnc=xnc, hpc=hpc: e.tensor_scalar(
            hpc, xnc, st(11), None, ALU.mult))
        W("sync", "cV", vxn)
        DMA("sync", f"dXN{b}", xn_scr[t * 128:(t + 1) * 128, :], xnc)
        W("scalar", "cV", vhp)
        DMA("scalar", f"dHP{b}", hp_scr[t * 128:(t + 1) * 128, :], hpc)
        W("sync", "cV", vwc)
        DMA("sync", "dWS", w_scr[t * 128:(t + 1) * 128].rearrange("(p a) -> p a", a=1), w_pm[:, t:t + 1])
        hist[t]["vxn"] = vxn

    for t in range(NKC):
        ph1_a(t)
        if t >= 1:
            ph1_b(t - 1)
    ph1_b(NKC - 1)
    Sp.barrier()

    for dc in range(DC):
        DMA("scalar", "dT", bfv(A1, dc * S, S), hp_scr[:, dc * 128:(dc + 1) * 128], transpose=True)
    v_hpT = Sp.val("dT")

    WBC, CMPC = 0, S
    CW = min(2048, S)
    stg = f32v(SCRA, CMPC, CW)[0:1, 0:S if S <= 2048 else 2048]

    nhalf = S // 2048 if S >= 2048 else 1
    HWW = S // max(1, nhalf)
    for hwi in range(max(1, S // HWW)):
        W("sync", "dWS")
        vld = DMA("sync", "dSTG", f32v(SCRA, CMPC, HWW)[0:1, :], w_scr[hwi * HWW:(hwi + 1) * HWW].rearrange("(a s) -> a s", a=1))
        W("tensor", "dSTG")
        vm = None
        for i in range(HWW // 512):
            GPS()
            vm = OP("tensor", lambda e, i=i, hwi=hwi: e.matmul(
                ps_G[:, 0:512], ones_row[0:1, 0:128],
                f32v(SCRA, CMPC, HWW)[0:1, i * 512:(i + 1) * 512], start=True, stop=True))
            W("vector", "cP", vm)
            vv = OP("vector", lambda e, i=i, hwi=hwi: e.tensor_copy(
                f32v(SCRA, WBC + hwi * HWW + i * 512, 512), ps_G[:, 0:512]))
        W("sync", "cV", Sp.val("cV"))
    wb = lambda c0, n: f32v(SCRA, WBC + c0, n)
    cmp_t = lambda n: f32v(SCRA, CMPC, n)
    NCMP = (S + CW - 1) // CW

    OP("vector", lambda e: e.memset(bis[:, 0:1], -64.0))
    OP("vector", lambda e: e.memset(bis[:, 1:2], 128.0 / 129.0))
    OP("vector", lambda e: e.memset(bis[:, 8:9], 64.0))

    def count_gt(thr_col, out_col):
        for ci in range(NCMP):
            OP("vector", lambda e, ci=ci: e.tensor_scalar(
                cmp_t(CW), wb(ci * CW, CW), bis[:, thr_col:thr_col + 1], 0.0, ALU.is_gt,
                ALU.add, accum_out=stat[:, 20 + ci:21 + ci]))
        if NCMP == 1:
            return OP("vector", lambda e: e.tensor_copy(bis[:, out_col:out_col + 1], stat[:, 20:21]))
        v = OP("vector", lambda e: e.tensor_tensor(bis[:, out_col:out_col + 1], stat[:, 20:21], stat[:, 21:22], ALU.add))
        return v

    def cross_sum_bcast(src_col, dst_col):
        vcb = OP("vector", lambda e: e.tensor_copy(maskc[:, 0:1], bis[:, src_col:src_col + 1]))
        W("tensor", "cV", vcb)
        vm = OP("tensor", lambda e: e.matmul(ps_G[0:1, 0:1], maskc[:, 0:1], ones_col_bf, start=True, stop=True))
        W("vector", "cP", vm)
        vv = OP("vector", lambda e: e.tensor_copy(sc1[0:1, 15:16], ps_G[0:1, 0:1]))
        W("tensor", "cV", vv)
        vb = OP("tensor", lambda e: e.matmul(ps_G[:, 1:2], ones_row[0:1, 0:128], sc1[0:1, 15:16], start=True, stop=True))
        W("vector", "cP", vb)
        return OP("vector", lambda e: e.tensor_copy(bis[:, dst_col:dst_col + 1], ps_G[:, 1:2]))

    for rnd in range(5):
        OP("vector", lambda e: e.tensor_scalar(bis[:, 2:3], pcol, 1.0, bis[:, 1:2], ALU.add, ALU.mult))
        OP("vector", lambda e: e.tensor_tensor(bis[:, 2:3], bis[:, 2:3], bis[:, 0:1], ALU.add))
        count_gt(2, 5)
        OP("vector", lambda e: e.tensor_scalar(bis[:, 6:7], bis[:, 5:6], float(KSEL), None, ALU.is_ge))
        cross_sum_bcast(6, 7)
        OP("vector", lambda e: e.tensor_scalar(stat[:, 17:18], bis[:, 7:8], 1.0, bis[:, 1:2], ALU.add, ALU.mult))
        OP("vector", lambda e: e.tensor_tensor(stat[:, 17:18], stat[:, 17:18], bis[:, 0:1], ALU.add))
        OP("vector", lambda e: e.tensor_tensor(stat[:, 18:19], stat[:, 17:18], bis[:, 8:9], ALU.min))
        OP("vector", lambda e: e.tensor_scalar(stat[:, 19:20], bis[:, 7:8], 127.5, None, ALU.is_le))
        OP("vector", lambda e: e.tensor_tensor(stat[:, 18:19], stat[:, 18:19], stat[:, 19:20], ALU.mult))
        OP("vector", lambda e: e.tensor_scalar(stat[:, 19:20], stat[:, 19:20], -1.0, 1.0, ALU.mult, ALU.add))
        OP("vector", lambda e: e.tensor_tensor(stat[:, 19:20], stat[:, 19:20], bis[:, 8:9], ALU.mult))
        OP("vector", lambda e: e.tensor_tensor(bis[:, 8:9], stat[:, 18:19], stat[:, 19:20], ALU.add))
        OP("vector", lambda e: e.scalar_tensor_tensor(stat[:, 12:13], bis[:, 7:8], 1.0, bis[:, 1:2], ALU.mult, ALU.mult))
        OP("vector", lambda e: e.tensor_tensor(bis[:, 0:1], bis[:, 0:1], stat[:, 12:13], ALU.add))
        OP("vector", lambda e: e.tensor_scalar(bis[:, 1:2], bis[:, 1:2], 1.0 / 129.0, None, ALU.mult))

    def cross_max(col, out_idx):
        GPS()
        vt = OP("tensor", lambda e, col=col: e.transpose(ps_G[0:1, 0:128], stat[:, col:col + 1], idf32))
        W("vector", "cP", vt)
        return OP("vector", lambda e, out_idx=out_idx: e.tensor_reduce(
            sc1[0:1, out_idx:out_idx + 1], ps_G[0:1, 0:128], AXX, ALU.max))

    def masked_max(cmp_op, thr_col, out_idx):
        BIG = 1.0e30
        for ci in range(NCMP):
            flg = cmp_t(CW)
            wf = f32v(SCRA, CMPC + CW, CW)
            OP("vector", lambda e, ci=ci, flg=flg: e.tensor_scalar(
                flg, wb(ci * CW, CW), bis[:, thr_col:thr_col + 1], None, cmp_op))
            OP("vector", lambda e, ci=ci, flg=flg, wf=wf: e.tensor_tensor(
                wf, wb(ci * CW, CW), flg, ALU.mult))
            OP("vector", lambda e, flg=flg: e.tensor_scalar(
                flg, flg, BIG, -BIG, ALU.mult, ALU.add))
            OP("vector", lambda e, flg=flg, wf=wf: e.tensor_tensor(
                wf, wf, flg, ALU.add))
            OP("vector", lambda e, ci=ci, wf=wf: e.tensor_reduce(
                stat[:, 13 + ci:14 + ci], wf, AXX, ALU.max))
        if NCMP == 1:
            vv = OP("vector", lambda e: e.tensor_copy(stat[:, 15:16], stat[:, 13:14]))
        else:
            vv = OP("vector", lambda e: e.tensor_tensor(stat[:, 15:16], stat[:, 13:14], stat[:, 14:15], ALU.max))
        W("tensor", "cV", vv)
        return cross_max(15, out_idx)

    v1 = masked_max(ALU.is_le, 8, 0)
    W("tensor", "cV", v1)
    GPS()
    vb = OP("tensor", lambda e: e.matmul(ps_G[:, 2:3], ones_row[0:1, 0:128], sc1[0:1, 0:1], start=True, stop=True))
    W("vector", "cP", vb)
    OP("vector", lambda e: e.tensor_copy(bis[:, 9:10], ps_G[:, 2:3]))
    count_gt(9, 10)
    OP("vector", lambda e: e.tensor_copy(sc1[0:1, 1:2], bis[0:1, 10:11]))
    v2 = masked_max(ALU.is_lt, 9, 2)
    OP("vector", lambda e: e.tensor_scalar(sc1[0:1, 3:4], sc1[0:1, 1:2], float(KC), None, ALU.is_equal))
    OP("vector", lambda e: e.tensor_tensor(sc1[0:1, 4:5], sc1[0:1, 3:4], sc1[0:1, 0:1], ALU.mult))
    OP("vector", lambda e: e.tensor_scalar(sc1[0:1, 5:6], sc1[0:1, 3:4], -1.0, 1.0, ALU.mult, ALU.add))
    OP("vector", lambda e: e.tensor_tensor(sc1[0:1, 5:6], sc1[0:1, 5:6], sc1[0:1, 2:3], ALU.mult))
    vthr = OP("vector", lambda e: e.tensor_tensor(sc1[0:1, 6:7], sc1[0:1, 4:5], sc1[0:1, 5:6], ALU.add))
    W("tensor", "cV", vthr)
    GPS()
    vbt = OP("tensor", lambda e: e.matmul(ps_G[:, 4:5], ones_row[0:1, 0:128], sc1[0:1, 6:7], start=True, stop=True))
    W("vector", "cP", vbt)
    OP("vector", lambda e: e.tensor_copy(bis[:, 12:13], ps_G[:, 4:5]))
    OP("vector", lambda e: e.tensor_scalar(maskf[:, 0:NKC], w_pm, bis[:, 12:13], None, ALU.is_gt))
    vmk = OP("vector", lambda e: e.tensor_copy(maskc[:, 0:NKC], maskf[:, 0:NKC]))
    W("tensor", "cV", vmk)
    vtr = OP("tensor", lambda e: e.matmul(ps_d2[0:1, 16:16 + NKC], ones_col_bf, maskc, start=True, stop=True))
    GPS()
    vri = OP("tensor", lambda e: e.matmul(ps_G[:, 0:NKC], upper01, maskc, start=True, stop=True))
    W("vector", "cP", vri)
    OP("vector", lambda e: e.tensor_copy(rrow[0:1, 0:NKC], ps_d2[0:1, 16:16 + NKC]))
    OP("vector", lambda e: e.memset(rrow[0:1, 128:129], 0.0))
    OP("vector", lambda e: e.memset(rrow[0:1, 256:256 + NKC], 0.0))
    vsc = None
    if NKC > 1:
        vsc = OP("vector", lambda e: e.tensor_tensor_scan(
            rrow[0:1, 129:128 + NKC], rrow[0:1, 0:NKC - 1], rrow[0:1, 256:256 + NKC - 1], 0.0, ALU.add, ALU.add))
    W("tensor", "cV", Sp.val("cV"))
    vbx = OP("tensor", lambda e: e.matmul(ps_d3[:, 32:32 + NKC], ones_row[0:1, 0:128], rrow[0:1, 128:128 + NKC], start=True, stop=True))
    W("vector", "cP", vbx)
    OP("vector", lambda e: e.tensor_copy(rankc, ps_G[:, 0:NKC]))
    vrk = OP("vector", lambda e: e.tensor_tensor(rankc, rankc, ps_d3[:, 32:32 + NKC], ALU.add))

    vm1 = vm2 = None
    for t in range(NKC):
        voh = OP("vector", lambda e, t=t: e.tensor_scalar(
            oh, iota_rank, rankc[:, t:t + 1], maskf[:, t:t + 1], ALU.is_equal, ALU.mult))
        W("tensor", "cV", voh)
        vm1 = OP("tensor", lambda e, t=t: e.matmul(
            ps_d0[0:1, 0:SELP], c128c[:, t:t + 1], oh, start=(t == 0), stop=(t == NKC - 1)))
        vm2 = OP("tensor", lambda e, t=t: e.matmul(
            ps_d1[0:1, 0:SELP], pcol_bf, oh, start=(t == 0), stop=(t == NKC - 1)))
        if t < NKC - 1:
            W("vector", "cP", vm2)
    W("vector", "cP", vm2)
    OP("vector", lambda e: e.tensor_copy(selrow, ps_d0[0:1, 0:SELP]))
    OP("vector", lambda e: e.tensor_tensor(selrow, selrow, ps_d1[0:1, 0:SELP], ALU.add))
    OP("vector", lambda e: e.tensor_tensor(selrow, selrow, padfix, ALU.add))
    vsi = OP("vector", lambda e: e.tensor_copy(selrow_i, selrow))
    W("sync", "cV", vsi)
    DMA("sync", "dSELW", sel_scr.rearrange("(a s) -> a s", a=1), selrow_i)
    W("sync", "dSELW")
    OP("vector", lambda e: e.tensor_scalar(selrow, selrow, float(S - 1), None, ALU.min))
    W("vector", "dSELW")
    vsi2 = OP("vector", lambda e: e.tensor_copy(selrow_i, selrow))
    W("sync", "cV", vsi2)
    DMA("sync", "dSCL", sel_cl.rearrange("(a s) -> a s", a=1), selrow_i.bitcast(U16))
    W("sync", "dSCL")
    for sc_ in range(NSC):
        DMA("sync", "dSEL", selc[:, sc_:sc_ + 1], sel_scr[sc_ * 128:(sc_ + 1) * 128].rearrange("(p a) -> p a", a=1))
    Sp.barrier()

    CKV0 = 0
    CQ0 = 0

    def c_proj(arena, dst_base, stride, woff, vrow_idx, ntok):
        for cs in range(CC):
            for tb in range(ntok // QB):
                t0 = tb * QB
                for dc in range(DC):
                    OP("tensor", lambda e, cs=cs, dc=dc, t0=t0, woff=woff: e.matmul(
                        ps_G[:, 0:QB], bfv(WA, woff + dc * C + cs * 128, 128),
                        bfv(A1, dc * S + t0, QB), start=(dc == 0), stop=False))
                vm = OP("tensor", lambda e, cs=cs, t0=t0, vrow_idx=vrow_idx: e.matmul(
                    ps_G[:, 0:QB], vrowbf[0:1, vrow_idx * C + cs * 128:vrow_idx * C + (cs + 1) * 128],
                    onesbf_row[0:1, 0:QB], start=False, stop=True))
                W("vector", "cP", vm)
                vv = OP("vector", lambda e, cs=cs, t0=t0, arena=arena, dst_base=dst_base, stride=stride: e.tensor_copy(
                    bfv(arena, dst_base + cs * stride + t0, QB), ps_G[:, 0:QB]))
                W("tensor", "cV", vv)
        return

    W("tensor", "cV", v_vrow)
    W("tensor", "dT")
    c_proj(A3, CQ0, SQ, W_A, 0, SQ)
    c_proj(A2, CKV0, S, W_B, 1, S)
    v_c = Sp.val("cV")

    U_Q = 0
    U_K, U_V = 0, CC * D
    W("gpsimd", "cP", Sp.val("cP"))
    vwq = DMA("gpsimd", "dW", bfv(WA, U_Q, CC * D).rearrange("p (cc o) -> p cc o", cc=CC), wuq_t.rearrange("(cc p) o -> p cc o", p=128))

    RQ, RS, RT = 0, 512, 1024
    SHUF = [i ^ 1 for i in range(32)]

    def rope_tile(ct_off, st_off, t0, dst_ap, vmm, war_dma=None):
        W("vector", "cP", vmm)
        W("vector", "dQW")
        v1 = OP("vector", lambda e: e.tensor_copy(bfv(SCRA, RQ, QB), ps_G[:, 0:QB]))
        W("tensor", "cV", v1)
        v2 = OP("vector", lambda e: e.stream_shuffle(bfv(SCRA, RS, QB), bfv(SCRA, RQ, QB), SHUF))
        OP("vector", lambda e, t0=t0, ct_off=ct_off: e.tensor_tensor(
            bfv(SCRA, RT, QB), bfv(SCRA, RQ, QB), bfv(A4, ct_off + t0, QB), ALU.mult))
        OP("vector", lambda e, t0=t0, st_off=st_off: e.tensor_tensor(
            bfv(SCRA, RS, QB), bfv(SCRA, RS, QB), bfv(A4, st_off + t0, QB), ALU.mult))
        return OP("vector", lambda e, dst_ap=dst_ap: e.tensor_tensor(
            dst_ap, bfv(SCRA, RT, QB), bfv(SCRA, RS, QB), ALU.add))

    W("tensor", "dW")
    W("tensor", "cV", v_c)
    vqd = None
    vr = None
    for pr in range(NP):
        for tb in range(NQB):
            t0 = tb * QB
            vm = None
            for cc_ in range(CC):
                vm = OP("tensor", lambda e, pr=pr, cc_=cc_, t0=t0: e.matmul(
                    ps_G[:, 0:QB], bfv(WA, U_Q + cc_ * D + pr * 128, 128),
                    bfv(A3, CQ0 + cc_ * SQ + t0, QB), start=(cc_ == 0), stop=(cc_ == CC - 1)))
            vr = rope_tile(T_CQ, T_SQ, t0, bfv(SCRA, RQ, QB), vm, war_dma=vqd)
            W("sync", "cV", vr)
            vqd = DMA("sync", "dQW", q_scr[pr, :, t0:t0 + QB], bfv(SCRA, RQ, QB))
    v_qmm = Sp.val("cP")

    W("sync", "cV", vr)
    DMA("sync", "dTK", bfv(A4, T_CK, S), cst["c_ctabk"])
    DMA("sync", "dTK", bfv(A4, T_SK, S), cst["c_stabk"])
    W("gpsimd", "cP", v_qmm)
    DMA("gpsimd", "dW", bfv(WA, U_K, CC * D).rearrange("p (cc o) -> p cc o", cc=CC), wuk_t.rearrange("(cc p) o -> p cc o", p=128))
    vwu = DMA("gpsimd", "dW", bfv(WA, U_V, CC * D).rearrange("p (cc o) -> p cc o", cc=CC), wuv_t.rearrange("(cc p) o -> p cc o", p=128))
    W("tensor", "dW")
    W("tensor", "dTK")
    W("vector", "dTK")

    K0 = 0
    V0 = 4 * S
    GRP = 2
    PPG = NP // GRP

    QT, PT = 0, 1024
    EVA, EVB = 2560, 3072
    OT = 8192
    DRW = 5120
    FLUSH_KC = 8 if NKC > 8 else 2
    NPT = 4 if NKC >= 4 else 2
    OAb = [PS_B[:, 0:512], PS_B[:, 1024:1536]]
    OBb = [PS_B[:, 512:1024], PS_B[:, 1536:2048]]
    vex = {}
    vavden = {}
    vsc_last = {}
    v_oT = {}
    pend = []
    vwo = None
    for g in range(GRP):
        MSET("vector", bfv(A1, V0, NKC * VST).rearrange(
            "p (n d) -> p n d", d=65)[:, :, 64:65], 1.0)
        for pl in range(PPG):
            pr = g * PPG + pl
            for tb in range(S // QB):
                t0 = tb * QB
                vm = None
                for cc_ in range(CC):
                    vm = OP("tensor", lambda e, pr=pr, cc_=cc_, t0=t0: e.matmul(
                        ps_G[:, 0:QB], bfv(WA, U_K + cc_ * D + pr * 128, 128),
                        bfv(A2, CKV0 + cc_ * S + t0, QB), start=(cc_ == 0), stop=(cc_ == CC - 1)))
                rope_tile(T_CK, T_SK, t0, bfv(A1, K0 + pl * S + t0, QB), vm)
        for tc in range(NKC):
            vm = None
            for cc_ in range(CC):
                vm = OP("tensor", lambda e, tc=tc, cc_=cc_, g=g: e.matmul(
                    ps_G[:, 0:512], bfv(A2, CKV0 + cc_ * S + tc * 128, 128),
                    bfv(WA, U_V + cc_ * D + g * 512, 512), start=(cc_ == 0), stop=(cc_ == CC - 1)))
            W("vector", "cP", vm)
            vv = OP("vector", lambda e, tc=tc: e.tensor_copy(
                bfv(A1, V0 + tc * VST, VST).rearrange("p (h d) -> p h d", h=8)[:, :, 0:64],
                ps_G[:, 0:512].rearrange("p (h d) -> p h d", h=8)))
            W("tensor", "cV", vv)
        if g == GRP - 1:
            W("gpsimd", "cP", Sp.val("cP"))
            vwo = DMA("gpsimd", "dW", bfv(WA, 0, DC * D).rearrange("p (hc o) -> p hc o", hc=DC), wo_t.rearrange("(hc p) o -> p hc o", p=128))
            DMA("gpsimd", "dW", bfv(A2, 0, DC * D).rearrange("p (dc o) -> p dc o", dc=DC), fwq.rearrange("(dc p) o -> p dc o", p=128))
            DMA("gpsimd", "dW", bfv(A2, DC * D, DC * D).rearrange("p (dc o) -> p dc o", dc=DC), fwk.rearrange("(dc p) o -> p dc o", p=128))

        W("sync", "dQW")
        W("sync", "cV", Sp.val("cV"))

        def flush_ep(item):
            (pu, pslot, ppr, pt0, dra_p, drb_p, OA_p, OB_p, vrc_p) = item
            W("tensor", "cV", vrc_p)
            OP("tensor", lambda e, OA_p=OA_p, dra_p=dra_p: e.matmul(
                OA_p[0:64, 0:QB], ones_row[0:1, 0:64], dra_p, start=True, stop=True))
            vb2 = OP("tensor", lambda e, OB_p=OB_p, drb_p=drb_p: e.matmul(
                OB_p[0:64, 0:QB], ones_row[0:1, 0:64], drb_p, start=True, stop=True))
            W("vector", "cP", vb2)
            W("vector", f"dOW{pslot}")
            OP("vector", lambda e, OA_p=OA_p, pslot=pslot: e.tensor_tensor(
                bfv(SCRA, OT + pslot * 1024, QB, p0=0, p=64),
                f32v(SCRA, EVA, QB, p0=0, p=64), OA_p[0:64, 0:QB], ALU.mult))
            voTB = OP("vector", lambda e, OB_p=OB_p, pslot=pslot: e.tensor_tensor(
                bfv(SCRA, OT + pslot * 1024 + 512, QB, p0=0, p=64),
                f32v(SCRA, EVB, QB, p0=0, p=64), OB_p[0:64, 0:QB], ALU.mult))
            v_oT[pu] = voTB
            W("sync", "cV", voTB)
            DMA("sync", f"dOW{pslot}", o_scr[ppr, 0:64, pt0:pt0 + QB],
                bfv(SCRA, OT + pslot * 1024, QB, p0=0, p=64))
            DMA("sync", f"dOW{pslot}", o_scr[ppr, 64:128, pt0:pt0 + QB],
                bfv(SCRA, OT + pslot * 1024 + 512, QB, p0=0, p=64))

        for pl in range(PPG):
            pr = g * PPG + pl
            for qb in range(NQB):
                t0 = qb * QB
                u = (g * PPG + pl) * NQB + qb
                uslot = u % 2
                qtile = bfv(SCRA, QT + uslot * 512, QB)
                if (u - 2) in vsc_last:
                    W("sync", "cP", vsc_last[u - 2])
                DMA("sync", f"dQL{uslot}", qtile, q_scr[pr, :, t0:t0 + QB])
                W("tensor", f"dQL{uslot}")
                OA, OB = OAb[uslot], OBb[uslot]

                def avden(kc, u=u, pl=pl, OA=OA, OB=OB):
                    ptile = bfv(SCRA, PT + (kc % NPT) * 1024, 1024)
                    W("tensor", "cA", vex[(u, kc)])
                    if kc == 0 and (u - 2) in v_oT:
                        W("tensor", "cV", v_oT[u - 2])
                    OP("tensor", lambda e, kc=kc, ptile=ptile, OA=OA, pl=pl: e.matmul(
                        OA[0:65, 0:QB], bfv(A1, V0 + kc * VST + (2 * pl) * 65, 65),
                        ptile[:, 0:QB], start=(kc == 0), stop=(kc == NKC - 1)))
                    vavden[(u, kc)] = OP("tensor", lambda e, kc=kc, ptile=ptile, OB=OB, pl=pl: e.matmul(
                        OB[0:65, 0:QB], bfv(A1, V0 + kc * VST + (2 * pl + 1) * 65, 65),
                        ptile[:, 512:512 + QB], start=(kc == 0), stop=(kc == NKC - 1)))

                for kc in range(NKC):
                    sb_ = PS_A[:, (kc % 2) * 1024:(kc % 2) * 1024 + 1024]
                    bk = (u, kc - 2) if kc >= 2 else (u - 1, NKC - 2 + kc)
                    if bk in vex:
                        W("tensor", "cA", vex[bk])
                    OP("tensor", lambda e, pl=pl, kc=kc, qtile=qtile, sb_=sb_: e.matmul(
                        sb_[:, 0:QB], bfv(A1, K0 + pl * S + kc * 128, 128, p0=0, p=64),
                        qtile[0:64, :], start=True, stop=True))
                    vs2 = OP("tensor", lambda e, pl=pl, kc=kc, qtile=qtile, sb_=sb_: e.matmul(
                        sb_[:, 512:512 + QB], bfv(A1, K0 + pl * S + kc * 128, 128, p0=64, p=64),
                        qtile[64:128, :], start=True, stop=True))
                    vsc_last[u] = vs2
                    if kc >= 1:
                        avden(kc - 1)
                    if kc == FLUSH_KC and pend:
                        flush_ep(pend.pop())
                    ptile = bfv(SCRA, PT + (kc % NPT) * 1024, 1024)
                    W("scalar", "cP", vs2)
                    pk = (u, kc - NPT) if kc >= NPT else (u - 1, NKC - NPT + kc)
                    if pk in vavden:
                        W("scalar", "cP", vavden[pk])
                    if QB == 512:
                        vex[(u, kc)] = OP("scalar", lambda e, ptile=ptile, sb_=sb_: e.activation(
                            ptile, sb_, AF.Exp))
                    else:
                        OP("scalar", lambda e, ptile=ptile, sb_=sb_: e.activation(
                            ptile[:, 0:QB], sb_[:, 0:QB], AF.Exp))
                        vex[(u, kc)] = OP("scalar", lambda e, ptile=ptile, sb_=sb_: e.activation(
                            ptile[:, 512:512 + QB], sb_[:, 512:512 + QB], AF.Exp))
                avden(NKC - 1)
                dra = f32v(SCRA, DRW + uslot * 1024, QB, p0=0, p=1)
                drb = f32v(SCRA, DRW + uslot * 1024 + 512, QB, p0=0, p=1)
                W("vector", "cP", vavden[(u, NKC - 1)])
                OP("vector", lambda e, OA=OA: e.tensor_copy(
                    f32v(SCRA, EVA, QB, p0=0, p=64), OA[0:64, 0:QB]))
                OP("vector", lambda e, OB=OB: e.tensor_copy(
                    f32v(SCRA, EVB, QB, p0=0, p=64), OB[0:64, 0:QB]))
                OP("vector", lambda e, OA=OA, dra=dra: e.tensor_copy(dra, OA[64:65, 0:QB]))
                OP("vector", lambda e, OB=OB, drb=drb: e.tensor_copy(drb, OB[64:65, 0:QB]))
                OP("vector", lambda e, dra=dra: e.reciprocal(dra, dra))
                vrc = OP("vector", lambda e, drb=drb: e.reciprocal(drb, drb))
                pend.append((u, uslot, pr, t0, dra, drb, OA, OB, vrc))
        while pend:
            flush_ep(pend.pop())
        if g < GRP - 1:
            Sp.barrier()
    Sp.barrier()

    OS = 0
    ohist = {}
    for tc in range(NQC):
        slot = tc % 2
        if tc >= 2:
            W("sync", "cP", ohist[tc - 2]["vm"])
        vld = None
        for pr in range(NP):
            vld = DMA("sync", f"dOS{slot}", bfv(A4, OS + slot * NP * 128 + pr * 128, 128),
                      o_scr[pr, :, tc * 128:(tc + 1) * 128])
        W("tensor", f"dOS{slot}")
        if tc == 0:
            W("tensor", "dW")
        for hb in range(2):
            vm = None
            for pr in range(NP):
                vm = OP("tensor", lambda e, pr=pr, hb=hb, slot=slot: e.matmul(
                    ps_G[:, 0:512], bfv(A4, OS + slot * NP * 128 + pr * 128, 128),
                    bfv(WA, pr * D + hb * 512, 512), start=(pr == 0), stop=(pr == NP - 1)))
            if tc >= 1:
                W("scalar", "cV", ohist[tc - 1]["vs"][hb])
            vxn = DMA("scalar", f"dXR{hb}", f32v(SCRA, 0 + hb * 512, 512), xn_scr[tc * 128:(tc + 1) * 128, hb * 512:(hb + 1) * 512])
            W("vector", "cP", vm)
            W("vector", f"dXR{hb}")
            if tc >= 1:
                W("vector", f"dSW{hb}")
            vs = OP("vector", lambda e, hb=hb: e.tensor_tensor(
                f32v(SCRA, 1024 + hb * 512, 512), ps_G[:, 0:512], f32v(SCRA, 0 + hb * 512, 512), ALU.add))
            W("tensor", "cV", vs)
            W("sync", "cV", vs)
            vsw = DMA("sync", f"dSW{hb}", s_scr[tc * 128:(tc + 1) * 128, hb * 512:(hb + 1) * 512], f32v(SCRA, 1024 + hb * 512, 512))
            ohist.setdefault(tc, {"vm": None, "vs": {}, "vsw": {}})
            ohist[tc]["vm"] = vm
            ohist[tc]["vs"][hb] = vs
            ohist[tc]["vsw"][hb] = vsw
    Sp.barrier()

    FQW, FKW = 0, DC * D
    FVW, FOW = 0, DC * D
    X5 = 0
    MR = 2 * NSC * D
    FV5 = MR + 2 * S
    MSL = FV5 + NSC * D
    W("gpsimd", "dSEL")
    for sc_ in range(NSC):
        vz = MSET("gpsimd", f32v(A1, X5 // 2 + sc_ * D, D), 0.0)
        W("gpsimd", "cG", Sp.val("cG"))
        IDMA("dGX", f32v(A1, X5 // 2 + sc_ * D, D), None, xn_scr, IndirectOffsetOnAxis(selc[:, sc_:sc_ + 1], 0),
             bounds_check=S - 1, oob_is_err=False)
        MSET("gpsimd", posc[:, sc_:sc_ + 1], 0)
        W("gpsimd", "cG", Sp.val("cG"))
        IDMA("dGX", posc[:, sc_:sc_ + 1], None, pos_in.rearrange('(s a) -> s a', a=1), IndirectOffsetOnAxis(selc[:, sc_:sc_ + 1], 0),
             bounds_check=S - 1, oob_is_err=False)
    v_gcmp = Sp.val("cG")

    W("sync", "dC")
    W("sync", "cP", Sp.val("cP"))
    vrw2 = DMA("sync", "dRW", rowst, rw_in.rearrange("(a d) -> a d", a=1))
    W("tensor", "dRW")
    bc_row(rowst, bcast, D)
    W("vector", "dGX")
    W("vector", "cG", v_gcmp)
    for sc_ in range(NSC):
        OP("vector", lambda e, sc_=sc_: e.scalar_tensor_tensor(
            f32v(SCRA, 0, D), f32v(A1, X5 // 2 + sc_ * D, D), 1.0, bcast, ALU.mult, ALU.mult,
            accum_out=wselc[:, sc_:sc_ + 1]))
    W("sync", "dRW")
    W("sync", "cP", Sp.val("cP"))
    vrb = DMA("sync", "dRB", rowst[0:1, 0:1], rb_in.rearrange("(b a) -> b a", b=1))
    W("tensor", "dRB")
    GPS()
    vbb = OP("tensor", lambda e: e.matmul(ps_G[:, 5:6], ones_row[0:1, 0:128], rowst[0:1, 0:1], start=True, stop=True))
    W("vector", "cP", vbb)
    OP("vector", lambda e: e.tensor_copy(stat[:, 0:1], ps_G[:, 5:6]))
    for sc_ in range(NSC):
        OP("vector", lambda e, sc_=sc_: e.tensor_tensor(
            wselc[:, sc_:sc_ + 1], wselc[:, sc_:sc_ + 1], stat[:, 0:1], ALU.add))

    W("sync", "dRB")
    W("sync", "cP", Sp.val("cP"))
    vl1 = DMA("sync", "dL1", rowst, ln1_in.rearrange("(a d) -> a d", a=1))
    W("tensor", "dL1")
    bc_row(rowst, bcast, D)
    HN = 0
    vhn_prev = Sp.val("cV")
    for sc_ in range(NSC):
        W("scalar", "cV", vhn_prev)
        vsq = OP("scalar", lambda e, sc_=sc_: e.activation(
            f32v(SCRA, 0, D), f32v(A1, X5 // 2 + sc_ * D, D), AF.Square, accum_out=stat[:, 1:2]))
        W("vector", "cA", vsq)
        vv = OP("vector", lambda e: e.tensor_scalar(stat[:, 2:3], stat[:, 1:2], 1.0 / D, 1e-6, ALU.mult, ALU.add))
        W("scalar", "cV", vv)
        OP("scalar", lambda e: e.activation(stat[:, 3:4], stat[:, 2:3], AF.Ln))
        vr2 = OP("scalar", lambda e: e.activation(stat[:, 4:5], stat[:, 3:4], AF.Exp, scale=epsc[:, 1:2]))
        W("vector", "cA", vr2)
        OP("vector", lambda e, sc_=sc_: e.tensor_scalar(
            f32v(SCRA, 0, D), f32v(A1, X5 // 2 + sc_ * D, D), stat[:, 4:5], None, ALU.mult))
        vhn = OP("vector", lambda e: e.tensor_tensor(
            bfv(SCRA, 2048, D), f32v(SCRA, 0, D), bcast, ALU.mult))
        vhn_prev = vhn
        W("tensor", "cV", vhn)
        for dc in range(DC):
            GPS()
            vt = OP("tensor", lambda e, dc=dc: e.transpose(
                ps_G.bitcast(BF16)[:, 0:128], bfv(SCRA, 2048 + dc * 128, 128), idbf))
            W("vector", "cP", vt)
            vv = OP("vector", lambda e, dc=dc, sc_=sc_: e.tensor_copy(
                bfv(A4, HN + dc * SELP + sc_ * 128, 128), ps_G.bitcast(BF16)[:, 0:128]))
            W("tensor", "cV", vv)

    TH, TW, TCK, TSK, TCQ, TSQ2 = 4096, 4608, 5120, 5632, 6144, 6656
    for sc_ in range(NSC):
        vv = OP("vector", lambda e, sc_=sc_: e.tensor_copy(poscf[:, sc_:sc_ + 1], posc[:, sc_:sc_ + 1]))
        W("sync", "cV", vv)
        DMA("sync", "dPW", w_scr[sc_ * 128:(sc_ + 1) * 128].rearrange("(p a) -> p a", a=1), poscf[:, sc_:sc_ + 1])
    W("sync", "dPW")
    vpr = DMA("sync", "dPR", posrow, w_scr[0:SELP].rearrange("(a s) -> a s", a=1))
    W("tensor", "dPR")
    GPS()
    vm = OP("tensor", lambda e: e.matmul(ps_G[:, 0:SELP], ones_row[0:1, 0:128], posrow, start=True, stop=True))
    W("vector", "cP", vm)
    vth = OP("vector", lambda e: e.tensor_scalar(f32v(SCRA, TH, SELP), ps_G[:, 0:SELP], invb, None, ALU.mult))

    def build_sincos(shift, dst_off):
        W("vector", "cA", Sp.val("cA"))
        OP("vector", lambda e: e.tensor_scalar(
            f32v(SCRA, TW, SELP), f32v(SCRA, TH, SELP), 1.0 / TWO_PI, 0.5 + shift / TWO_PI, ALU.mult, ALU.add))
        vv = OP("vector", lambda e: e.tensor_copy(i32v(SCRA, TW + 512, SELP), f32v(SCRA, TW, SELP)))
        OP("vector", lambda e: e.tensor_copy(f32v(SCRA, TW, SELP), i32v(SCRA, TW + 512, SELP)))
        OP("vector", lambda e: e.tensor_scalar(
            f32v(SCRA, TW, SELP), f32v(SCRA, TW, SELP), -TWO_PI, None, ALU.mult))
        OP("vector", lambda e: e.tensor_tensor(
            f32v(SCRA, TW, SELP), f32v(SCRA, TW, SELP), f32v(SCRA, TH, SELP), ALU.add))
        OP("vector", lambda e: e.tensor_scalar(
            f32v(SCRA, TW, SELP), f32v(SCRA, TW, SELP), shift, None, ALU.add))
        FB = f32v(SCRA, TW + 512, SELP)
        OP("vector", lambda e: e.tensor_scalar(FB, f32v(SCRA, TW, SELP), math.pi, None, ALU.is_gt))
        OP("vector", lambda e: e.tensor_scalar(FB, FB, TWO_PI, None, ALU.mult))
        OP("vector", lambda e: e.tensor_tensor(f32v(SCRA, TW, SELP), f32v(SCRA, TW, SELP), FB, ALU.subtract))
        OP("vector", lambda e: e.tensor_scalar(FB, f32v(SCRA, TW, SELP), -math.pi, None, ALU.is_lt))
        OP("vector", lambda e: e.tensor_scalar(FB, FB, TWO_PI, None, ALU.mult))
        vv = OP("vector", lambda e: e.tensor_tensor(f32v(SCRA, TW, SELP), f32v(SCRA, TW, SELP), FB, ALU.add))
        W("scalar", "cV", vv)
        return OP("scalar", lambda e, dst_off=dst_off: e.activation(
            f32v(SCRA, dst_off, SELP), f32v(SCRA, TW, SELP), AF.Sin))

    vs1 = build_sincos(0.0, TSK)
    vs2 = build_sincos(math.pi / 2.0, TCK)
    W("vector", "cA", vs2)
    OP("vector", lambda e: e.tensor_scalar(f32v(SCRA, TSK, SELP), f32v(SCRA, TSK, SELP), sgnb, None, ALU.mult))
    OP("vector", lambda e: e.tensor_scalar(f32v(SCRA, TCQ, SELP), f32v(SCRA, TCK, SELP), 0.125, None, ALU.mult))
    vtb = OP("vector", lambda e: e.tensor_scalar(f32v(SCRA, TSQ2, SELP), f32v(SCRA, TSK, SELP), 0.125, None, ALU.mult))

    FQ0, FK0 = 0, NP * SELP
    W("tensor", "cV", vtb)
    W("tensor", "dW")
    for (wq_off, dst0, tc_off, ts_off) in ((FQW, FQ0, TCQ, TSQ2), (FKW, FK0, TCK, TSK)):
        for pr in range(NP):
            vm = None
            for dc in range(DC):
                vm = OP("tensor", lambda e, pr=pr, dc=dc, wq_off=wq_off: e.matmul(
                    ps_G[:, 0:SELP], bfv(A2, wq_off + dc * D + pr * 128, 128),
                    bfv(A4, HN + dc * SELP, SELP), start=(dc == 0), stop=(dc == DC - 1)))
            W("vector", "cP", vm)
            v1 = OP("vector", lambda e: e.tensor_copy(bfv(SCRA, 2048, SELP), ps_G.bitcast(BF16)[:, 0:SELP] if False else ps_G[:, 0:SELP]))
            W("tensor", "cV", v1)
            GPS()
            vsw = OP("tensor", lambda e: e.matmul(ps_G[:, 0:SELP], pswapb, bfv(SCRA, 2048, SELP), start=True, stop=True))
            OP("vector", lambda e, tc_off=tc_off: e.tensor_tensor(
                f32v(SCRA, 0, SELP), bfv(SCRA, 2048, SELP), f32v(SCRA, tc_off, SELP), ALU.mult))
            W("vector", "cP", vsw)
            OP("vector", lambda e, ts_off=ts_off: e.tensor_tensor(
                f32v(SCRA, 512, SELP), ps_G[:, 0:SELP], f32v(SCRA, ts_off, SELP), ALU.mult))
            vv = OP("vector", lambda e, dst0=dst0, pr=pr: e.tensor_tensor(
                bfv(A3, dst0 + pr * SELP, SELP), f32v(SCRA, 0, SELP), f32v(SCRA, 512, SELP), ALU.add))
            W("tensor", "cV", vv)
    W("gpsimd", "cP", Sp.val("cP"))
    vfv = DMA("gpsimd", "dW", bfv(A2, FVW, DC * D).rearrange("p (dc o) -> p dc o", dc=DC), fwv.rearrange("(dc p) o -> p dc o", p=128))
    vfo = DMA("gpsimd", "dW", bfv(A2, FOW, DC * D).rearrange("p (dc o) -> p dc o", dc=DC), fwo.rearrange("(dc p) o -> p dc o", p=128))
    W("tensor", "dW")
    for sc_ in range(NSC):
        for hb in range(2):
            vm = None
            for dc in range(DC):
                vm = OP("tensor", lambda e, sc_=sc_, dc=dc, hb=hb: e.matmul(
                    ps_G[:, 0:512], bfv(A4, HN + dc * SELP + sc_ * 128, 128),
                    bfv(A2, FVW + dc * D + hb * 512, 512), start=(dc == 0), stop=(dc == DC - 1)))
            W("vector", "cP", vm)
            vv = OP("vector", lambda e, sc_=sc_, hb=hb: e.tensor_copy(
                bfv(A1, FV5 + sc_ * D + hb * 512, 512), ps_G[:, 0:512]))
            W("tensor", "cV", vv)

    iw_src = sel_cl.rearrange("(col x two) -> x col two", x=16, two=2)[:, :, 0:1]
    for gg in range(8):
        DMA("sync", "dIW", iw16[16 * gg:16 * gg + 16, :].rearrange("x (c a) -> x c a", a=1), iw_src,
            allow_slow_non_contiguous=True)
    T1 = 6144
    W("gpsimd", "dIW")
    for sc_ in range(NSC):
        if sc_ == NSC - 1:
            MSET("gpsimd", f32v(A1, MR // 2, S), 0.0)
        W("gpsimd", "cG", Sp.val("cG"))
        IDMA("dGM", f32v(A1, MR // 2, S), None, m_in, IndirectOffsetOnAxis(selc[:, sc_:sc_ + 1], 0),
             bounds_check=S - 1, oob_is_err=False)
        W("gpsimd", "dGM")
        W("gpsimd", "cV", Sp.val("cV"))
        vic = OP("gpsimd", lambda e: e.indirect_copy(
            f32v(SCRA, T1, SELP), f32v(A1, MR // 2, S), iw16,
            i_know_ap_gather_is_preferred=True))
        W("tensor", "cG", vic)
        for b in range(NSC):
            GPS()
            vt = OP("tensor", lambda e, b=b: e.transpose(
                ps_G[:, 0:128], f32v(SCRA, T1 + b * 128, 128), idf32))
            W("vector", "cP", vt)
            vv = OP("vector", lambda e, b=b, sc_=sc_: e.tensor_copy(
                f32v(A1, MSL // 2 + b * SELP + sc_ * 128, 128), ps_G[:, 0:128]))
            W("tensor", "cV", vv)
    v_mselg = Sp.val("cV")

    FOT = 0
    W("vector", "cV", v_mselg)
    vex = None
    for pr in range(NP):
        vav = None
        for kc in range(NSC):
            if vex is not None:
                W("vector", "cA", vex)
            vs1 = OP("tensor", lambda e, pr=pr, kc=kc: e.matmul(
                ps_S[:, 0:SELP], bfv(A3, FK0 + pr * SELP + kc * 128, 128, p0=0, p=64),
                bfv(A3, FQ0 + pr * SELP, SELP, p0=0, p=64), start=True, stop=True))
            vs2 = OP("tensor", lambda e, pr=pr, kc=kc: e.matmul(
                ps_S[:, 512:512 + SELP], bfv(A3, FK0 + pr * SELP + kc * 128, 128, p0=64, p=64),
                bfv(A3, FQ0 + pr * SELP, SELP, p0=64, p=64), start=True, stop=True))
            W("vector", "cP", vs2)
            OP("vector", lambda e, kc=kc: e.tensor_tensor(
                f32v(SCRA, 0, SELP), ps_S[:, 0:SELP], f32v(A1, MSL // 2 + kc * SELP, SELP), ALU.add))
            vma = OP("vector", lambda e, kc=kc: e.tensor_tensor(
                f32v(SCRA, 512, SELP), ps_S[:, 512:512 + SELP], f32v(A1, MSL // 2 + kc * SELP, SELP), ALU.add))
            W("scalar", "cV", vma)
            if kc >= 2:
                W("scalar", "cP", vav)
            ptile = bfv(SCRA, 2048 + (kc % 2) * 1024, 1024)
            if SELP == 512:
                vex = OP("scalar", lambda e, ptile=ptile: e.activation(
                    ptile, f32v(SCRA, 0, 1024), AF.Exp))
            else:
                OP("scalar", lambda e, ptile=ptile: e.activation(
                    ptile[:, 0:SELP], f32v(SCRA, 0, SELP), AF.Exp))
                vex = OP("scalar", lambda e, ptile=ptile: e.activation(
                    ptile[:, 512:512 + SELP], f32v(SCRA, 512, SELP), AF.Exp))
            W("tensor", "cA", vex)
            ocol = ones_kpad_bf if kc == NSC - 1 else ones_col_bf
            OP("tensor", lambda e, pr=pr, kc=kc, ptile=ptile: e.matmul(
                ps_O[0:64, 0:SELP], bfv(A1, FV5 + kc * D + pr * 128, 64), ptile[:, 0:SELP],
                start=(kc == 0), stop=(kc == NSC - 1), tile_position=(0, 0)))
            vav = OP("tensor", lambda e, pr=pr, kc=kc, ptile=ptile: e.matmul(
                ps_O[64:128, 0:SELP], bfv(A1, FV5 + kc * D + pr * 128 + 64, 64), ptile[:, 512:512 + SELP],
                start=(kc == 0), stop=(kc == NSC - 1), tile_position=(0, 64),
                skip_group_check=True))
            dbank_a = ps_d0 if kc % 2 == 0 else ps_d1
            dbank_b = ps_d2 if kc % 2 == 0 else ps_d3
            rowa = 0 if kc % 2 == 0 else 32
            rowb = 64 if kc % 2 == 0 else 32
            st = kc < 2
            sp = kc >= NSC - 2
            OP("tensor", lambda e, dbank_a=dbank_a, rowa=rowa, st=st, sp=sp, ptile=ptile, ocol=ocol: e.matmul(
                dbank_a[rowa:rowa + 1, 0:SELP], ocol, ptile[:, 0:SELP],
                start=st, stop=sp, tile_position=(0, rowa)))
            vav = OP("tensor", lambda e, dbank_b=dbank_b, rowb=rowb, st=st, sp=sp, ptile=ptile, ocol=ocol: e.matmul(
                dbank_b[rowb:rowb + 1, 0:SELP], ocol, ptile[:, 512:512 + SELP],
                start=st, stop=sp, tile_position=(0, rowb)))
        W("vector", "cP", vav)
        hasodd = NSC > 1
        OP("vector", lambda e: e.tensor_copy(rrow[0:1, 0:SELP], ps_d0[0:1, 0:SELP]))
        if hasodd:
            OP("vector", lambda e: e.tensor_tensor(rrow[0:1, 0:SELP], rrow[0:1, 0:SELP], ps_d1[32:33, 0:SELP], ALU.add))
        OP("vector", lambda e: e.tensor_copy(rrow[0:1, 512:512 + SELP], ps_d2[64:65, 0:SELP]))
        if hasodd:
            OP("vector", lambda e: e.tensor_tensor(rrow[0:1, 512:512 + SELP], rrow[0:1, 512:512 + SELP], ps_d3[32:33, 0:SELP], ALU.add))
        if SELP == 512:
            vrc = OP("vector", lambda e: e.reciprocal(rrow, rrow))
        else:
            OP("vector", lambda e: e.reciprocal(rrow[0:1, 0:SELP], rrow[0:1, 0:SELP]))
            vrc = OP("vector", lambda e: e.reciprocal(rrow[0:1, 512:512 + SELP], rrow[0:1, 512:512 + SELP]))
        W("tensor", "cV", vrc)
        vb1 = OP("tensor", lambda e: e.matmul(
            ps_d0[0:64, 0:SELP], ones_row[0:1, 0:64], rrow[0:1, 0:SELP], start=True, stop=True))
        vb2 = OP("tensor", lambda e: e.matmul(
            ps_d0[64:128, 0:SELP], ones_row[0:1, 0:64], rrow[0:1, 512:512 + SELP],
            start=True, stop=True, tile_position=(0, 64)))
        W("vector", "cP", vb2)
        OP("vector", lambda e: e.tensor_copy(f32v(SCRA, 0, SELP), ps_d0[:, 0:SELP]))
        voT = OP("vector", lambda e, pr=pr: e.tensor_tensor(
            bfv(A4, FOT + pr * SELP, SELP), ps_O[:, 0:SELP], f32v(SCRA, 0, SELP), ALU.mult))
        W("tensor", "cV", voT)
    v_fattn = Sp.val("cV")

    W("tensor", "dW")
    for wc in range(NWC):
        for hb in range(2):
            vm = None
            for pr in range(NP):
                vm = OP("tensor", lambda e, wc=wc, pr=pr, hb=hb: e.matmul(
                    ps_G[:, 0:512], bfv(A4, FOT + pr * SELP + wc * 128, 128),
                    bfv(A2, FOW + pr * D + hb * 512, 512), start=(pr == 0), stop=(pr == NP - 1)))
            W("vector", "cP", vm)
            vv = OP("vector", lambda e, wc=wc, hb=hb: e.tensor_tensor(
                f32v(SCRA, wc * D + hb * 512, 512), ps_G[:, 0:512],
                f32v(A1, X5 // 2 + wc * D + hb * 512, 512), ALU.add))
            W("tensor", "cV", vv)
    v_x1 = Sp.val("cV")

    X1T = NP * SELP
    W("sync", "dL1")
    W("sync", "cP", Sp.val("cP"))
    vl2 = DMA("sync", "dL2", rowst, ln2_in.rearrange("(a d) -> a d", a=1))
    W("tensor", "dL2")
    bc_row(rowst, bcast, D)
    vhn_prev = Sp.val("cV")
    for wc in range(NWC):
        W("scalar", "cV", vhn_prev)
        vsq = OP("scalar", lambda e, wc=wc: e.activation(
            f32v(SCRA, NWC * D, D), f32v(SCRA, wc * D, D), AF.Square, accum_out=stat[:, 1:2]))
        W("vector", "cA", vsq)
        vv = OP("vector", lambda e: e.tensor_scalar(stat[:, 2:3], stat[:, 1:2], 1.0 / D, 1e-6, ALU.mult, ALU.add))
        W("scalar", "cV", vv)
        OP("scalar", lambda e: e.activation(stat[:, 3:4], stat[:, 2:3], AF.Ln))
        vr2 = OP("scalar", lambda e: e.activation(stat[:, 4:5], stat[:, 3:4], AF.Exp, scale=epsc[:, 1:2]))
        W("vector", "cA", vr2)
        OP("vector", lambda e, wc=wc: e.tensor_scalar(
            f32v(SCRA, NWC * D, D), f32v(SCRA, wc * D, D), stat[:, 4:5], None, ALU.mult))
        vhn = OP("vector", lambda e: e.tensor_tensor(
            bfv(SCRA, 2 * (NWC + 1) * D, D), f32v(SCRA, NWC * D, D), bcast, ALU.mult))
        vhn_prev = vhn
        W("tensor", "cV", vhn)
        for dc in range(DC):
            GPS()
            vt = OP("tensor", lambda e, dc=dc: e.transpose(
                ps_G.bitcast(BF16)[:, 0:128], bfv(SCRA, 2 * (NWC + 1) * D + dc * 128, 128), idbf))
            W("vector", "cP", vt)
            vv = OP("vector", lambda e, dc=dc, wc=wc: e.tensor_copy(
                bfv(A4, X1T + dc * WIN + wc * 128, 128), ps_G.bitcast(BF16)[:, 0:128]))
            W("tensor", "cV", vv)
    v_x1T = Sp.val("cV")

    H1G = 0
    _v0 = Sp.val("cP")
    w1_reads = {-2: _v0, -1: _v0}
    for fg in range(FC // 4):
        W("gpsimd", "cP", w1_reads[fg - 2])
        vw1 = DMA("gpsimd", f"dW1{fg % 2}", bfv(A3, 0 if fg % 2 == 0 else 4096, DC * 512).rearrange("p (dc f) -> p dc f", dc=DC),
                  fw1[:, fg * 512:(fg + 1) * 512].rearrange("(dc p) f -> p dc f", p=128))
        W("tensor", f"dW1{fg % 2}")
        for fs in range(4):
            vm = None
            for dc in range(DC):
                vm = OP("tensor", lambda e, fg=fg, fs=fs, dc=dc: e.matmul(
                    ps_G[:, 0:WIN], bfv(A3, (0 if fg % 2 == 0 else 4096) + dc * 512 + fs * 128, 128),
                    bfv(A4, X1T + dc * WIN, WIN), start=(dc == 0), stop=(dc == DC - 1)))
            w1_reads[fg] = vm
            W("vector", "cP", vm)
            OP("vector", lambda e: e.tensor_copy(f32v(SCRA, 2 * (NWC + 1) * D // 2, WIN), ps_G[:, 0:WIN]))
            h1 = f32v(SCRA, 2 * (NWC + 1) * D // 2, WIN)
            vv = OP("vector", lambda e, h1=h1: e.tensor_tensor(f32v(SCRA, 7000, WIN), h1, h1, ALU.mult))
            W("tensor", "cV", vv)
            OP("vector", lambda e: e.tensor_scalar(f32v(SCRA, 7000, WIN), f32v(SCRA, 7000, WIN), C1G, C0G, ALU.mult, ALU.add))
            vu = OP("vector", lambda e, h1=h1: e.tensor_tensor(f32v(SCRA, 7424, WIN), f32v(SCRA, 7000, WIN), h1, ALU.mult))
            W("scalar", "cV", vu)
            vt = OP("scalar", lambda e: e.activation(f32v(SCRA, 7000, WIN), f32v(SCRA, 7424, WIN), AF.Tanh))
            W("vector", "cA", vt)
            OP("vector", lambda e: e.tensor_scalar(f32v(SCRA, 7000, WIN), f32v(SCRA, 7000, WIN), 0.5, 0.5, ALU.mult, ALU.add))
            vg = OP("vector", lambda e, fg=fg, fs=fs, h1=h1: e.tensor_tensor(
                bfv(A1, H1G + (fg * 4 + fs) * WIN, WIN), f32v(SCRA, 7000, WIN), h1, ALU.mult))
    v_h1g = Sp.val("cV")

    _v0 = Sp.val("cP")
    w2_reads = {(0,): _v0, (1,): _v0}
    W("tensor", "cV", v_h1g)
    for wc in range(NWC):
        for hb in range(2):
            vm = None
            for fg in range(FC // 4):
                key = (fg % 2,)
                W("gpsimd", "cP", w2_reads[key])
                vw2 = DMA("gpsimd", f"dW2{fg % 2}", bfv(A2, FVW + 2048 * (fg % 2), 4 * 512).rearrange("p (g o) -> p g o", g=4),
                          fw2[fg * 512:(fg + 1) * 512, hb * 512:(hb + 1) * 512].rearrange("(g p) o -> p g o", p=128))
                W("tensor", f"dW2{fg % 2}")
                for g in range(4):
                    vm = OP("tensor", lambda e, wc=wc, hb=hb, fg=fg, g=g: e.matmul(
                        ps_G[:, 0:512], bfv(A1, H1G + (fg * 4 + g) * WIN + wc * 128, 128),
                        bfv(A2, FVW + 2048 * (fg % 2) + g * 512, 512),
                        start=(fg == 0 and g == 0), stop=(fg == FC // 4 - 1 and g == 3)))
                w2_reads[key] = vm
            W("vector", "cP", vm)
            OP("vector", lambda e, wc=wc, hb=hb: e.tensor_tensor(
                f32v(SCRA, wc * D + hb * 512, 512), f32v(SCRA, wc * D + hb * 512, 512), ps_G[:, 0:512], ALU.add))
            vpr2 = OP("vector", lambda e, wc=wc, hb=hb: e.tensor_scalar(
                f32v(SCRA, wc * D + hb * 512, 512), f32v(SCRA, wc * D + hb * 512, 512),
                wselc[:, wc:wc + 1], None, ALU.mult))
            W("tensor", "cV", vpr2)
        W("gpsimd", "cV", Sp.val("cV"))
        IDMA("dSC", xn_scr, IndirectOffsetOnAxis(selc[:, wc:wc + 1], 0), f32v(SCRA, wc * D, D), None,
             bounds_check=S - 1, oob_is_err=False)
    Sp.barrier()

    vo_hist = {}

    def ph6_load(tc):
        b = tc % 2
        g0 = f32v(SCRA, b * D, D)
        s0 = f32v(SCRA, 2048 + b * D, D)
        x0 = f32v(SCRA, 4096 + b * D, D)
        if tc >= 2:
            W("sync", "cV", vo_hist[tc - 2])
            W("scalar", "cV", vo_hist[tc - 2])
            W("gpsimd", "cV", vo_hist[tc - 2])
            W("sync", f"dY{b}")
        DMA("sync", f"dE{b}", g0, xn_scr[tc * 128:(tc + 1) * 128, :])
        DMA("scalar", f"dES{b}", s0, s_scr[tc * 128:(tc + 1) * 128, :])
        DMA("gpsimd", f"dX6{b}", x0, x_in[tc * 128:(tc + 1) * 128, :])

    def ph6_compute(tc):
        b = tc % 2
        g0 = f32v(SCRA, b * D, D)
        s0 = f32v(SCRA, 2048 + b * D, D)
        x0 = f32v(SCRA, 4096 + b * D, D)
        W("scalar", f"dE{b}")
        vsl = OP("scalar", lambda e, g0=g0: e.activation(g0, g0, AF.Silu))
        W("vector", "cA", vsl)
        W("vector", f"dES{b}")
        W("vector", f"dX6{b}")
        OP("vector", lambda e, g0=g0, s0=s0: e.tensor_tensor(g0, g0, s0, ALU.mult))
        vo = OP("vector", lambda e, g0=g0, x0=x0: e.tensor_tensor(g0, g0, x0, ALU.add))
        vo_hist[tc] = vo
        W("sync", "cV", vo)
        DMA("sync", f"dY{b}", y_out[tc * 128:(tc + 1) * 128, :], g0)

    ph6_load(0)
    for tc in range(NQC):
        if tc + 1 < NQC:
            ph6_load(tc + 1)
        ph6_compute(tc)
    Sp.barrier()
    if dbg:
        dbg_a4 = nc.dram_tensor("dbg_a4", [128, A4.shape[-1]], BF16, kind="ExternalOutput").ap()
        dbg_a1 = nc.dram_tensor("dbg_a1", [128, NSC * SELP], F32, kind="ExternalOutput").ap()
        dbg_ws = nc.dram_tensor("dbg_ws", [128, NSC], F32, kind="ExternalOutput").ap()
        dbg_sel = nc.dram_tensor("dbg_sel", [128, NSC], I32, kind="ExternalOutput").ap()
        dbg_wpm = nc.dram_tensor("dbg_wpm", [128, NKC], F32, kind="ExternalOutput").ap()
        dbg_bis = nc.dram_tensor("dbg_bis", [128, 16], F32, kind="ExternalOutput").ap()
        dbg_sc1 = nc.dram_tensor("dbg_sc1", [2, 16], F32, kind="ExternalOutput").ap()
        dbg_rank = nc.dram_tensor("dbg_rank", [128, NKC], F32, kind="ExternalOutput").ap()
        dbg_mask = nc.dram_tensor("dbg_mask", [128, NKC], F32, kind="ExternalOutput").ap()
        dbg_stat = nc.dram_tensor("dbg_stat", [128, 24], F32, kind="ExternalOutput").ap()
        DMA("sync", "dDBG", dbg_a4, A4[:, :])
        DMA("sync", "dDBG", dbg_a1, f32v(A1, MSL // 2, NSC * SELP))
        DMA("sync", "dDBG", dbg_ws, wselc)
        DMA("sync", "dDBG", dbg_sel, selc)
        DMA("sync", "dDBG", dbg_wpm, w_pm)
        DMA("sync", "dDBG", dbg_bis, bis)
        DMA("sync", "dDBG", dbg_sc1, sc1)
        DMA("sync", "dDBG", dbg_rank, rankc)
        vmf = OP("vector", lambda e: e.tensor_copy(stat[:, 0:NKC] if NKC <= 24 else stat[:, 0:24], maskf[:, 0:min(NKC, 24)]))
        W("sync", "cV", vmf)
        DMA("sync", "dDBG", dbg_mask, maskf if False else maskf)
        DMA("sync", "dDBG", dbg_stat, stat)
        Sp.barrier()
    Sp.build()
    return nc


def make_in_maps(cfg, inputs):
    S, SQ = cfg.S, cfg.SQ
    x = np.ascontiguousarray(inputs["L0_attn_x"])
    m = inputs["L0_ffn_attention_mask"]
    pos = inputs["L0_ffn_position_ids"]
    B = x.shape[0]
    in_maps = []
    for core in range(2 * B):
        b, half = core // 2, core % 2
        perm = np.concatenate([np.arange(half * SQ, (half + 1) * SQ),
                               np.arange((1 - half) * SQ, (2 - half) * SQ) % S])
        cst = host_constants(cfg, perm)
        xb = np.ascontiguousarray(x[b][perm])
        mb = np.ascontiguousarray(m[b, 0][perm][:, perm])
        pb = np.ascontiguousarray(pos[b][perm]).astype(np.int32)
        im = {
            "x": xb, "m": mb, "pos": pb,
            "wdq_t": np.ascontiguousarray(inputs["L0_attn_w_down_q"].T),
            "wdkv_t": np.ascontiguousarray(inputs["L0_attn_w_down_kv"].T),
            "wuq_t": np.ascontiguousarray(inputs["L0_attn_w_up_q"].T),
            "wuk_t": np.ascontiguousarray(inputs["L0_attn_w_up_k"].T),
            "wuv_t": np.ascontiguousarray(inputs["L0_attn_w_up_v"].T),
            "wo_t": np.ascontiguousarray(inputs["L0_attn_w_out"].T),
            "g2": inputs["L0_attn_ln_scale"], "b2": inputs["L0_attn_ln_bias"],
            "rw": np.ascontiguousarray(inputs["L0_ffn_router_w"][:, 0]),
            "rb": inputs["L0_ffn_router_b"],
            "fwq": inputs["L0_ffn_wq"], "fwk": inputs["L0_ffn_wk"],
            "fwv": inputs["L0_ffn_wv"], "fwo": inputs["L0_ffn_wo"],
            "fw1": inputs["L0_ffn_w1"], "fw2": inputs["L0_ffn_w2"],
            "ln1": inputs["L0_ffn_ln1"], "ln2": inputs["L0_ffn_ln2"],
        }
        im.update(cst)
        in_maps.append(im)
    return in_maps


def assemble(cfg, results, B):
    S, SQ, D = cfg.S, cfg.SQ, cfg.D
    out = np.zeros((B, S, D), np.float32)
    for core, res in enumerate(results):
        b, half = core // 2, core % 2
        out[b, half * SQ:(half + 1) * SQ] = res["y"]
    return out


def kernel(**inputs):
    from concourse.bass_utils import run_bass_kernel_spmd
    cfg = Cfg(4096)
    nc = build_program(cfg)
    in_maps = make_in_maps(cfg, inputs)
    res = run_bass_kernel_spmd(nc, in_maps, core_ids=list(range(len(in_maps))))
    return assemble(cfg, res.results, inputs["L0_attn_x"].shape[0])


# version 12
# speedup vs baseline: 1.1484x; 1.0170x over previous
import math
import numpy as np
import ml_dtypes
import concourse.bass as bass
import concourse.mybir as mybir
from concourse.bass_core_rust import IndirectOffsetOnAxis

F32 = mybir.dt.float32
BF16 = mybir.dt.bfloat16
I32 = mybir.dt.int32
U16 = mybir.dt.uint16
AF = mybir.ActivationFunctionType
ALU = mybir.AluOpType
AXX = mybir.AxisListType.X

C0G = 0.7978845608028654
C1G = C0G * 0.044715
TWO_PI = 2.0 * math.pi


class Cfg:
    def __init__(self, S=4096):
        self.S = S
        self.SQ = S // 2
        self.D = 1024
        self.H = 16
        self.DH = 64
        self.NP = 8
        self.C = 512
        self.F = 4096
        self.KSEL = S // 8
        self.KC = self.KSEL - 1
        self.SELP = max(128, ((self.KC + 127) // 128) * 128)
        self.NSC = self.SELP // 128
        self.NKC = S // 128
        self.NQC = self.SQ // 128
        self.QB = min(512, self.SQ)
        self.NQB = self.SQ // self.QB
        self.DC = self.D // 128
        self.CC = self.C // 128
        self.FC = self.F // 128
        self.WIN = min(3 * 128, self.SELP) if self.SELP >= 512 else self.SELP
        self.NWC = self.WIN // 128


def host_constants(cfg, perm):
    S, SQ, SELP, NKC = cfg.S, cfg.SQ, cfg.SELP, cfg.NKC
    t = np.arange(S, dtype=np.float64)[perm]
    inv = 10000.0 ** (-(np.arange(32) % 16) / 16.0)
    th = t[:, None] * inv[None, :]
    cosA = np.cos(th).astype(np.float32)
    sinA = np.sin(th).astype(np.float32)
    rows_j = (np.arange(128) % 64) // 2
    sgn = np.where(np.arange(128) % 2 == 0, -1.0, 1.0).astype(np.float32)
    C_tab = np.ascontiguousarray(cosA[:, rows_j].T)
    S_tab = np.ascontiguousarray(sinA[:, rows_j].T * sgn[:, None])
    okp = np.ones((128, 1), np.float32)
    okp[(cfg.KC % 128) if cfg.SELP > cfg.KC else 128:, 0] = 0.0
    pf = np.zeros((1, SELP), np.float32)
    pf[0, cfg.KC:] = float(2 * S)
    swap = np.arange(128)
    swap = np.where(swap % 64 < 32, swap + 32, swap - 32)
    P = np.zeros((128, 128), np.float32)
    P[swap, np.arange(128)] = 1.0
    return {
        "c_ctabk": C_tab.astype(ml_dtypes.bfloat16),
        "c_stabk": S_tab.astype(ml_dtypes.bfloat16),
        "c_ctabq": np.ascontiguousarray((C_tab[:, :SQ] * 0.125).astype(ml_dtypes.bfloat16)),
        "c_stabq": np.ascontiguousarray((S_tab[:, :SQ] * 0.125).astype(ml_dtypes.bfloat16)),
        "c_idf32": np.eye(128, dtype=np.float32),
        "c_idbf": np.eye(128, dtype=np.float32).astype(ml_dtypes.bfloat16),
        "c_ones_row": np.ones((1, 512), np.float32),
        "c_upper01": (np.arange(128)[:, None] <= np.arange(128)[None, :]).astype(ml_dtypes.bfloat16),
        "c_ones_col_bf": np.ones((128, 1), np.float32).astype(ml_dtypes.bfloat16),
        "c_ones_kpad_bf": okp.astype(ml_dtypes.bfloat16),
        "c_iota_rank": np.broadcast_to(np.arange(1, SELP + 1, dtype=np.float32)[None, :], (128, SELP)).copy(),
        "c_c128": np.broadcast_to((128.0 * np.arange(NKC, dtype=np.float32))[None, :], (128, NKC)).copy().astype(ml_dtypes.bfloat16),
        "c_pcol": np.arange(128, dtype=np.float32).reshape(128, 1),
        "c_pcol_bf": np.arange(128, dtype=np.float32).astype(ml_dtypes.bfloat16).reshape(128, 1),
        "c_padfix": pf,
        "c_pswapb": P.astype(ml_dtypes.bfloat16),
        "c_invb": (10000.0 ** (-2 * (np.arange(128) % 32) / 64.0)).astype(np.float32).reshape(128, 1),
        "c_eps": np.broadcast_to(np.array([1e-5, -0.5], np.float32)[None, :], (128, 2)).copy(),
        "c_sgnb": np.where(np.arange(128) % 64 < 32, -1.0, 1.0).astype(np.float32).reshape(128, 1),
    }


CONST_SPECS = lambda cfg: [
    ("c_ctabk", [128, cfg.S], BF16), ("c_stabk", [128, cfg.S], BF16),
    ("c_ctabq", [128, cfg.SQ], BF16), ("c_stabq", [128, cfg.SQ], BF16),
    ("c_idf32", [128, 128], F32), ("c_idbf", [128, 128], BF16),
    ("c_ones_row", [1, 512], F32), ("c_upper01", [128, 128], BF16),
    ("c_ones_col_bf", [128, 1], BF16), ("c_ones_kpad_bf", [128, 1], BF16),
    ("c_iota_rank", [128, cfg.SELP], F32), ("c_c128", [128, cfg.NKC], BF16),
    ("c_pcol", [128, 1], F32), ("c_pcol_bf", [128, 1], BF16),
    ("c_padfix", [1, cfg.SELP], F32), ("c_pswapb", [128, 128], BF16),
    ("c_invb", [128, 1], F32), ("c_sgnb", [128, 1], F32),
    ("c_eps", [128, 2], F32),
]


class Script:
    ENGINES = ("sync", "vector", "scalar", "tensor", "gpsimd")

    def __init__(self, nc):
        self.nc = nc
        self.q = {e: [] for e in self.ENGINES}
        self.sems = {}

    def sem(self, name):
        if name not in self.sems:
            self.sems[name] = [self.nc.alloc_semaphore(name), 0]
        return self.sems[name]

    def inc(self, name, k=1):
        s = self.sem(name)
        s[1] += k
        return s[0], s[1]

    def val(self, name):
        return self.sem(name)[1]

    def do(self, eng, fn):
        self.q[eng].append(fn)

    def barrier(self):
        snap = [(s[0], s[1]) for s in self.sems.values() if s[1] > 0]
        for e in self.ENGINES:
            def f(eng, snap=snap):
                for h, v in snap:
                    eng.wait_ge(h, v)
            self.do(e, f)

    def build(self):
        with self.nc.Block() as blk:
            blk.sync(lambda e: [f(e) for f in self.q["sync"]])
            blk.vector(lambda e: [f(e) for f in self.q["vector"]])
            blk.scalar(lambda e: [f(e) for f in self.q["scalar"]])
            blk.tensor(lambda e: [f(e) for f in self.q["tensor"]])
            blk.gpsimd(lambda e: [f(e) for f in self.q["gpsimd"]])


def build_program(cfg, dbg=False):
    nc = bass.Bass("TRN2", target_bir_lowering=False)
    S, SQ, D, C, F = cfg.S, cfg.SQ, cfg.D, cfg.C, cfg.F
    NP, KC, SELP, NSC, NKC, NQC, QB, NQB = (cfg.NP, cfg.KC, cfg.SELP, cfg.NSC,
                                            cfg.NKC, cfg.NQC, cfg.QB, cfg.NQB)
    DC, CC, FC, WIN, NWC = cfg.DC, cfg.CC, cfg.FC, cfg.WIN, cfg.NWC
    KSEL = cfg.KSEL

    def ein(name, shape, dt=F32):
        return nc.dram_tensor(name, shape, dt, kind="ExternalInput").ap()

    x_in = ein("x", [S, D])
    m_in = ein("m", [S, S])
    pos_in = ein("pos", [S], I32)
    wdq_t = ein("wdq_t", [D, C]); wdkv_t = ein("wdkv_t", [D, C])
    wuq_t = ein("wuq_t", [C, D]); wuk_t = ein("wuk_t", [C, D]); wuv_t = ein("wuv_t", [C, D])
    wo_t = ein("wo_t", [D, D])
    g2_in = ein("g2", [D]); b2_in = ein("b2", [D])
    rw_in = ein("rw", [D]); rb_in = ein("rb", [1])
    fwq = ein("fwq", [D, D]); fwk = ein("fwk", [D, D]); fwv = ein("fwv", [D, D]); fwo = ein("fwo", [D, D])
    fw1 = ein("fw1", [D, F]); fw2 = ein("fw2", [F, D])
    ln1_in = ein("ln1", [D]); ln2_in = ein("ln2", [D])
    cst = {nm: ein(nm, sh, dt) for nm, sh, dt in CONST_SPECS(cfg)}

    y_out = nc.dram_tensor("y", [SQ, D], F32, kind="ExternalOutput").ap()

    xn_scr = nc.dram_tensor("xn_scr", [S, D], F32, kind="ExternalOutput").ap()
    hp_scr = nc.dram_tensor("hp_scr", [S, D], BF16, kind="ExternalOutput").ap()
    mt_scr = nc.dram_tensor("mt_scr", [S, SELP], F32, kind="ExternalOutput").ap()
    sel_cl = nc.dram_tensor("sel_cl", [2 * SELP], U16, kind="ExternalOutput").ap()
    s_scr = nc.dram_tensor("s_scr", [SQ, D], F32, kind="ExternalOutput").ap()
    w_scr = nc.dram_tensor("w_scr", [S], F32, kind="ExternalOutput").ap()
    sel_scr = nc.dram_tensor("sel_scr", [SELP], I32, kind="ExternalOutput").ap()
    q_scr = nc.dram_tensor("q_scr", [NP, 128, SQ], BF16, kind="ExternalOutput").ap()
    o_scr = nc.dram_tensor("o_scr", [NP, 128, SQ], BF16, kind="ExternalOutput").ap()

    def sb(name, shape, dt):
        return nc.alloc_sbuf_tensor(name, shape, dt).ap()

    idf32 = sb("idf32", [128, 128], F32)
    idbf = sb("idbf", [128, 128], BF16)
    ones_row = sb("ones_row", [1, 512], F32)
    onesbf_row = sb("onesbf_row", [1, 512], BF16)
    upper01 = sb("upper01", [128, 128], BF16)
    ones_col_bf = sb("ones_col_bf", [128, 1], BF16)
    ones_kpad_bf = sb("ones_kpad_bf", [128, 1], BF16)
    iota_rank = sb("iota_rank", [128, SELP], F32)
    c128c = sb("c128c", [128, NKC], BF16)
    pcol = sb("pcol", [128, 1], F32)
    pcol_bf = sb("pcol_bf", [128, 1], BF16)
    padfix = sb("padfix", [1, SELP], F32)
    pswapb = sb("pswapb", [128, 128], BF16)
    invb = sb("invb", [128, 1], F32)
    sgnb = sb("sgnb", [128, 1], F32)
    epsc = sb("epsc", [128, 2], F32)
    g2c = sb("g2c", [128, DC], F32)
    b2cc = sb("b2cc", [128, DC], F32)
    b2bf = sb("b2bf", [128, DC], BF16)
    vrowbf = sb("vrowbf", [1, 2 * C], BF16)
    rowst = sb("rowst", [1, D], F32)
    bcast = sb("bcast", [128, D], F32)
    stat = sb("stat", [128, 24], F32)
    w_pm = sb("w_pm", [128, NKC], F32)
    bis = sb("bis", [128, 16], F32)
    sc1 = sb("sc1", [2, 16], F32)
    rankc = sb("rankc", [128, NKC], F32)
    maskc = sb("maskc", [128, NKC], BF16)
    iw16 = sb("iw16", [128, SELP // 16], U16)
    maskf = sb("maskf", [128, NKC], F32)
    oh = sb("oh", [128, SELP], BF16)
    selrow = sb("selrow", [1, SELP], F32)
    selc = sb("selc", [128, NSC], I32)
    posc = sb("posc", [128, NSC], I32)
    wselc = sb("wselc", [128, NSC], F32)
    rrow = sb("rrow", [1, 1024], F32)
    vrow = rrow[0:1, 0:2 * C]
    selrow_i = rrow[0:1, 512:512 + SELP].bitcast(I32)
    posrow = rrow[0:1, 0:SELP]
    poscf = sb("poscf", [128, NSC], F32)

    SCRA = sb("SCRA", [128, 16384], BF16)
    PH5_A1 = 2 * NSC * D + 2 * S + NSC * D + 2 * NSC * SELP
    VST = 8 * 65
    A1 = sb("A1", [128, max(4 * S + NKC * VST, PH5_A1, FC * WIN)], BF16)
    A2 = sb("A2", [128, max(CC * S, 2 * DC * D)], BF16)
    A3 = sb("A3", [128, max(CC * SQ, 8192, 2 * NP * SELP)], BF16)
    A4 = sb("A4", [128, max(2 * S, NP * SELP + DC * WIN, 2 * SQ)], BF16)
    WA = sb("WA", [128, max(2 * DC * C, 2 * CC * D, DC * D)], BF16)

    def bfv(arena, c0, n, p0=0, p=128):
        return arena[p0:p0 + p, c0:c0 + n]

    def f32v(arena, c0, n, p0=0, p=128):
        return arena[p0:p0 + p, 2 * c0:2 * (c0 + n)].bitcast(F32)

    def i32v(arena, c0, n, p0=0, p=128):
        return arena[p0:p0 + p, 2 * c0:2 * (c0 + n)].bitcast(I32)

    PS_A = nc.alloc_psum_tensor("PS_A", [128, 2048], F32).ap()
    PS_B = nc.alloc_psum_tensor("PS_B", [128, 2048], F32).ap()
    ps_S = PS_A[:, 0:1024]
    ps_G = PS_A[:, 1024:1536]
    ps_d3 = PS_A[:, 1536:2048]
    ps_O = PS_B[:, 0:512]
    ps_d0 = PS_B[:, 512:1024]
    ps_d1 = PS_B[:, 1024:1536]
    ps_d2 = PS_B[:, 1536:2048]

    Sp = Script(nc)

    def DMA(queue, ch, out, in_, **kw):
        h, v = Sp.inc(ch, 16)
        Sp.do(queue, lambda e, out=out, in_=in_, kw=kw, h=h: e.dma_start(out=out, in_=in_, **kw).then_inc(h, 16))
        return v

    def IDMA(ch, out, out_off, in_, in_off, **kw):
        h, v = Sp.inc(ch, 16)
        Sp.do("gpsimd", lambda e, out=out, out_off=out_off, in_=in_, in_off=in_off, kw=kw, h=h:
              e.indirect_dma_start(out, out_off, in_, in_off, **kw).then_inc(h, 16))
        return v

    def W(eng, sem, v=None):
        s = Sp.sem(sem)
        if v is None:
            v = s[1]
        if v <= 0:
            return
        Sp.do(eng, lambda e, h=s[0], v=v: e.wait_ge(h, v))

    SEM_OF = {"tensor": "cP", "vector": "cV", "scalar": "cA", "gpsimd": "cG"}

    def OP(eng, fn, selfwait=True):
        s = Sp.sem(SEM_OF[eng])
        nprev = s[1]
        h, v = Sp.inc(SEM_OF[eng], 1)
        if selfwait and nprev > 0 and eng != "tensor":
            Sp.do(eng, lambda e, fn=fn, h=h, nprev=nprev: (
                e.wait_ge(h, nprev), fn(e).then_inc(h, 1)))
        else:
            Sp.do(eng, lambda e, fn=fn, h=h: fn(e).then_inc(h, 1))
        return v

    def OPn(eng, fn):
        Sp.do(eng, fn)

    def GPS(eng="tensor"):
        W(eng, "cV", Sp.val("cV"))

    def MSET(eng, ap, val):
        return OP(eng, lambda e, ap=ap, val=val: e.memset(ap, val))

    MSET("vector", ps_d3[:, :], 0.0)
    MSET("vector", bis[:, :], 0.0)
    MSET("vector", stat[:, :], 0.0)
    v_init0 = MSET("vector", sc1[:, :], 0.0)
    W("scalar", "cV", v_init0)
    for dst, key in [(idf32, "c_idf32"), (idbf, "c_idbf"), (ones_row, "c_ones_row"),
                     (upper01, "c_upper01"), (ones_col_bf, "c_ones_col_bf"),
                     (ones_kpad_bf, "c_ones_kpad_bf"), (iota_rank, "c_iota_rank"),
                     (c128c, "c_c128"), (pcol, "c_pcol"), (pcol_bf, "c_pcol_bf"),
                     (padfix, "c_padfix"), (pswapb, "c_pswapb"), (invb, "c_invb"),
                     (sgnb, "c_sgnb"), (epsc, "c_eps")]:
        DMA("sync", "dC", dst, cst[key])
    DMA("gpsimd", "dC2", onesbf_row, cst["c_ones_row"])
    T_CQ, T_SQ = 0, SQ
    T_CK, T_SK = 0, S
    DMA("sync", "dC", bfv(A4, T_CQ, SQ), cst["c_ctabq"])
    DMA("sync", "dC", bfv(A4, T_SQ, SQ), cst["c_stabq"])
    DMA("sync", "dC", g2c, g2_in.rearrange("(c p) -> p c", p=128), allow_slow_non_contiguous=True)
    DMA("sync", "dC", b2cc, b2_in.rearrange("(c p) -> p c", p=128), allow_slow_non_contiguous=True)
    W_A, W_B = 0, DC * C
    DMA("gpsimd", "dW", bfv(WA, W_A, DC * C).rearrange("p (dc c) -> p dc c", dc=DC), wdq_t.rearrange("(dc p) c -> p dc c", p=128))
    vwAB = DMA("gpsimd", "dW", bfv(WA, W_B, DC * C).rearrange("p (dc c) -> p dc c", dc=DC), wdkv_t.rearrange("(dc p) c -> p dc c", p=128))

    vrw = DMA("sync", "dC", rowst, rw_in.rearrange("(a d) -> a d", a=1))

    def bc_row(src_row, dst_full, width, wait_val=None):
        for i in range((width + 511) // 512):
            w0, w1 = i * 512, min(width, (i + 1) * 512)
            W("tensor", "cV", Sp.val("cV"))
            vm = OP("tensor", lambda e, w0=w0, w1=w1: e.matmul(
                ps_G[:, 0:w1 - w0], ones_row[0:1, 0:128], src_row[0:1, w0:w1], start=True, stop=True))
            W("vector", "cP", vm)
            OP("vector", lambda e, w0=w0, w1=w1: e.tensor_copy(dst_full[:, w0:w1], ps_G[:, 0:w1 - w0]))

    W("tensor", "dC")
    bc_row(rowst, bcast, D)
    v_rwbc = Sp.val("cV")

    W("vector", "dW")
    W("vector", "dC")
    for dc in range(DC):
        OP("vector", lambda e, dc=dc: e.tensor_scalar(
            bfv(WA, W_A + dc * C, C), bfv(WA, W_A + dc * C, C), g2c[:, dc:dc + 1], None, ALU.mult))
        OP("vector", lambda e, dc=dc: e.tensor_scalar(
            bfv(WA, W_B + dc * C, C), bfv(WA, W_B + dc * C, C), g2c[:, dc:dc + 1], None, ALU.mult))
    OP("vector", lambda e: e.reciprocal(stat[:, 16:16 + DC], g2c))
    OP("vector", lambda e: e.tensor_tensor(stat[:, 16:16 + DC], b2cc, stat[:, 16:16 + DC], ALU.mult))
    vbg = OP("vector", lambda e: e.tensor_copy(b2bf, stat[:, 16:16 + DC]))
    W("tensor", "cV", vbg)
    for row, woff in ((0, W_A), (1, W_B)):
        vm = None
        W("tensor", "cV", Sp.val("cV"))
        for dc in range(DC):
            vm = OP("tensor", lambda e, row=row, woff=woff, dc=dc: e.matmul(
                ps_G[0:1, 0:C], b2bf[:, dc:dc + 1], bfv(WA, woff + dc * C, C),
                start=(dc == 0), stop=(dc == DC - 1)))
        W("vector", "cP", vm)
        OP("vector", lambda e, row=row: e.tensor_copy(vrow[0:1, row * C:(row + 1) * C], ps_G[0:1, 0:C]))
    v_vrow = OP("vector", lambda e: e.tensor_copy(vrowbf, vrow))

    hist = {}

    def ph1_a(t):
        b = t % 2
        xc = f32v(SCRA, b * D, D)
        if t >= 2:
            W("sync", "cV", hist[t - 2]["vxn"])
            W("sync", "cA", hist[t - 2]["vcp"])
        DMA("sync", f"dX{b}", xc, x_in[t * 128:(t + 1) * 128, :])
        dumpA = f32v(SCRA, 4096, D)
        W("scalar", f"dX{b}")
        if t >= 2:
            W("scalar", "cV", hist[t - 2]["v2"])
        vsq = OP("scalar", lambda e, xc=xc, dumpA=dumpA, b=b: e.activation(
            dumpA, xc, AF.Square, accum_out=stat[:, 12 * b + 1:12 * b + 2]))
        vcp = OP("scalar", lambda e, xc=xc, dumpA=dumpA, b=b: e.activation(
            dumpA, xc, AF.Copy, accum_out=stat[:, 12 * b:12 * b + 1]))
        hist[t] = {"vsq": vsq, "vcp": vcp}

    def ph1_b(t):
        b = t % 2
        sc = 12 * b
        st = lambda k: stat[:, sc + k:sc + k + 1]
        xc = f32v(SCRA, b * D, D)
        xnc = f32v(SCRA, 2048 + b * D, D)
        dumpB = f32v(SCRA, 5120, D)
        hpc = bfv(SCRA, 12288 + b * D, D)
        W("vector", "cA", hist[t]["vcp"])
        if t == 0:
            W("vector", "cV", v_rwbc)
        OP("vector", lambda e: e.tensor_scalar(st(2), st(0), 1.0 / D, None, ALU.mult))
        OP("vector", lambda e: e.tensor_tensor(st(3), st(2), st(2), ALU.mult))
        v2 = OP("vector", lambda e: e.scalar_tensor_tensor(
            st(4), st(1), 1.0 / D, st(3), ALU.mult, ALU.subtract))
        hist[t]["v2"] = v2
        W("scalar", "cV", v2)
        OP("scalar", lambda e: e.activation(st(5), st(4), AF.Ln, bias=epsc[:, 0:1]))
        v3 = OP("scalar", lambda e: e.activation(st(6), st(5), AF.Exp, scale=epsc[:, 1:2]))
        OP("vector", lambda e: e.tensor_scalar(st(7), st(4), 1e-5, None, ALU.add))
        OP("vector", lambda e: e.reciprocal(st(8), st(7)))
        v5 = OP("vector", lambda e: e.tensor_tensor(st(9), st(4), st(8), ALU.mult))
        W("scalar", "cV", v5)
        OP("scalar", lambda e: e.activation(st(10), st(9), AF.Ln, bias=epsc[:, 0:1]))
        v6 = OP("scalar", lambda e: e.activation(st(11), st(10), AF.Exp, scale=epsc[:, 1:2]))
        W("vector", "cA", v3)
        if t >= 2:
            W("vector", f"dXN{b}")
        vxn = OP("vector", lambda e, xc=xc, xnc=xnc: e.tensor_scalar(
            xnc, xc, st(2), st(6), ALU.subtract, ALU.mult))
        vwc = OP("vector", lambda e, xnc=xnc, dumpB=dumpB, t=t: e.scalar_tensor_tensor(
            dumpB, xnc, 1.0, bcast, ALU.mult, ALU.mult, accum_out=w_pm[:, t:t + 1]))
        W("vector", "cA", v6)
        if t >= 2:
            W("vector", f"dHP{b}")
        vhp = OP("vector", lambda e, xnc=xnc, hpc=hpc: e.tensor_scalar(
            hpc, xnc, st(11), None, ALU.mult))
        W("sync", "cV", vxn)
        DMA("sync", f"dXN{b}", xn_scr[t * 128:(t + 1) * 128, :], xnc)
        W("scalar", "cV", vhp)
        DMA("scalar", f"dHP{b}", hp_scr[t * 128:(t + 1) * 128, :], hpc)
        W("sync", "cV", vwc)
        DMA("sync", "dWS", w_scr[t * 128:(t + 1) * 128].rearrange("(p a) -> p a", a=1), w_pm[:, t:t + 1])
        hist[t]["vxn"] = vxn

    for t in range(NKC):
        ph1_a(t)
        if t >= 1:
            ph1_b(t - 1)
    ph1_b(NKC - 1)
    Sp.barrier()

    for dc in range(DC):
        DMA("scalar", "dT", bfv(A1, dc * S, S), hp_scr[:, dc * 128:(dc + 1) * 128], transpose=True)
    v_hpT = Sp.val("dT")

    WBC, CMPC = 0, S
    CW = min(2048, S)
    stg = f32v(SCRA, CMPC, CW)[0:1, 0:S if S <= 2048 else 2048]

    nhalf = S // 2048 if S >= 2048 else 1
    HWW = S // max(1, nhalf)
    for hwi in range(max(1, S // HWW)):
        W("sync", "dWS")
        vld = DMA("sync", "dSTG", f32v(SCRA, CMPC, HWW)[0:1, :], w_scr[hwi * HWW:(hwi + 1) * HWW].rearrange("(a s) -> a s", a=1))
        W("tensor", "dSTG")
        vm = None
        for i in range(HWW // 512):
            GPS()
            vm = OP("tensor", lambda e, i=i, hwi=hwi: e.matmul(
                ps_G[:, 0:512], ones_row[0:1, 0:128],
                f32v(SCRA, CMPC, HWW)[0:1, i * 512:(i + 1) * 512], start=True, stop=True))
            W("vector", "cP", vm)
            vv = OP("vector", lambda e, i=i, hwi=hwi: e.tensor_copy(
                f32v(SCRA, WBC + hwi * HWW + i * 512, 512), ps_G[:, 0:512]))
        W("sync", "cV", Sp.val("cV"))
    wb = lambda c0, n: f32v(SCRA, WBC + c0, n)
    cmp_t = lambda n: f32v(SCRA, CMPC, n)
    NCMP = (S + CW - 1) // CW

    OP("vector", lambda e: e.memset(bis[:, 0:1], -64.0))
    OP("vector", lambda e: e.memset(bis[:, 1:2], 128.0 / 129.0))
    OP("vector", lambda e: e.memset(bis[:, 8:9], 64.0))

    def count_gt(thr_col, out_col):
        for ci in range(NCMP):
            OP("vector", lambda e, ci=ci: e.tensor_scalar(
                cmp_t(CW), wb(ci * CW, CW), bis[:, thr_col:thr_col + 1], 0.0, ALU.is_gt,
                ALU.add, accum_out=stat[:, 20 + ci:21 + ci]))
        if NCMP == 1:
            return OP("vector", lambda e: e.tensor_copy(bis[:, out_col:out_col + 1], stat[:, 20:21]))
        v = OP("vector", lambda e: e.tensor_tensor(bis[:, out_col:out_col + 1], stat[:, 20:21], stat[:, 21:22], ALU.add))
        return v

    def cross_sum_bcast(src_col, dst_col):
        vcb = OP("vector", lambda e: e.tensor_copy(maskc[:, 0:1], bis[:, src_col:src_col + 1]))
        W("tensor", "cV", vcb)
        vm = OP("tensor", lambda e: e.matmul(ps_G[0:1, 0:1], maskc[:, 0:1], ones_col_bf, start=True, stop=True))
        W("vector", "cP", vm)
        vv = OP("vector", lambda e: e.tensor_copy(sc1[0:1, 15:16], ps_G[0:1, 0:1]))
        W("tensor", "cV", vv)
        vb = OP("tensor", lambda e: e.matmul(ps_G[:, 1:2], ones_row[0:1, 0:128], sc1[0:1, 15:16], start=True, stop=True))
        W("vector", "cP", vb)
        return OP("vector", lambda e: e.tensor_copy(bis[:, dst_col:dst_col + 1], ps_G[:, 1:2]))

    for rnd in range(5):
        OP("vector", lambda e: e.tensor_scalar(bis[:, 2:3], pcol, 1.0, bis[:, 1:2], ALU.add, ALU.mult))
        OP("vector", lambda e: e.tensor_tensor(bis[:, 2:3], bis[:, 2:3], bis[:, 0:1], ALU.add))
        count_gt(2, 5)
        OP("vector", lambda e: e.tensor_scalar(bis[:, 6:7], bis[:, 5:6], float(KSEL), None, ALU.is_ge))
        cross_sum_bcast(6, 7)
        OP("vector", lambda e: e.tensor_scalar(stat[:, 17:18], bis[:, 7:8], 1.0, bis[:, 1:2], ALU.add, ALU.mult))
        OP("vector", lambda e: e.tensor_tensor(stat[:, 17:18], stat[:, 17:18], bis[:, 0:1], ALU.add))
        OP("vector", lambda e: e.tensor_tensor(stat[:, 18:19], stat[:, 17:18], bis[:, 8:9], ALU.min))
        OP("vector", lambda e: e.tensor_scalar(stat[:, 19:20], bis[:, 7:8], 127.5, None, ALU.is_le))
        OP("vector", lambda e: e.tensor_tensor(stat[:, 18:19], stat[:, 18:19], stat[:, 19:20], ALU.mult))
        OP("vector", lambda e: e.tensor_scalar(stat[:, 19:20], stat[:, 19:20], -1.0, 1.0, ALU.mult, ALU.add))
        OP("vector", lambda e: e.tensor_tensor(stat[:, 19:20], stat[:, 19:20], bis[:, 8:9], ALU.mult))
        OP("vector", lambda e: e.tensor_tensor(bis[:, 8:9], stat[:, 18:19], stat[:, 19:20], ALU.add))
        OP("vector", lambda e: e.scalar_tensor_tensor(stat[:, 12:13], bis[:, 7:8], 1.0, bis[:, 1:2], ALU.mult, ALU.mult))
        OP("vector", lambda e: e.tensor_tensor(bis[:, 0:1], bis[:, 0:1], stat[:, 12:13], ALU.add))
        OP("vector", lambda e: e.tensor_scalar(bis[:, 1:2], bis[:, 1:2], 1.0 / 129.0, None, ALU.mult))

    def cross_max(col, out_idx):
        GPS()
        vt = OP("tensor", lambda e, col=col: e.transpose(ps_G[0:1, 0:128], stat[:, col:col + 1], idf32))
        W("vector", "cP", vt)
        return OP("vector", lambda e, out_idx=out_idx: e.tensor_reduce(
            sc1[0:1, out_idx:out_idx + 1], ps_G[0:1, 0:128], AXX, ALU.max))

    def masked_max(cmp_op, thr_col, out_idx):
        BIG = 1.0e30
        for ci in range(NCMP):
            flg = cmp_t(CW)
            wf = f32v(SCRA, CMPC + CW, CW)
            OP("vector", lambda e, ci=ci, flg=flg: e.tensor_scalar(
                flg, wb(ci * CW, CW), bis[:, thr_col:thr_col + 1], None, cmp_op))
            OP("vector", lambda e, ci=ci, flg=flg, wf=wf: e.tensor_tensor(
                wf, wb(ci * CW, CW), flg, ALU.mult))
            OP("vector", lambda e, flg=flg: e.tensor_scalar(
                flg, flg, BIG, -BIG, ALU.mult, ALU.add))
            OP("vector", lambda e, flg=flg, wf=wf: e.tensor_tensor(
                wf, wf, flg, ALU.add))
            OP("vector", lambda e, ci=ci, wf=wf: e.tensor_reduce(
                stat[:, 13 + ci:14 + ci], wf, AXX, ALU.max))
        if NCMP == 1:
            vv = OP("vector", lambda e: e.tensor_copy(stat[:, 15:16], stat[:, 13:14]))
        else:
            vv = OP("vector", lambda e: e.tensor_tensor(stat[:, 15:16], stat[:, 13:14], stat[:, 14:15], ALU.max))
        W("tensor", "cV", vv)
        return cross_max(15, out_idx)

    v1 = masked_max(ALU.is_le, 8, 0)
    W("tensor", "cV", v1)
    GPS()
    vb = OP("tensor", lambda e: e.matmul(ps_G[:, 2:3], ones_row[0:1, 0:128], sc1[0:1, 0:1], start=True, stop=True))
    W("vector", "cP", vb)
    OP("vector", lambda e: e.tensor_copy(bis[:, 9:10], ps_G[:, 2:3]))
    count_gt(9, 10)
    OP("vector", lambda e: e.tensor_copy(sc1[0:1, 1:2], bis[0:1, 10:11]))
    v2 = masked_max(ALU.is_lt, 9, 2)
    OP("vector", lambda e: e.tensor_scalar(sc1[0:1, 3:4], sc1[0:1, 1:2], float(KC), None, ALU.is_equal))
    OP("vector", lambda e: e.tensor_tensor(sc1[0:1, 4:5], sc1[0:1, 3:4], sc1[0:1, 0:1], ALU.mult))
    OP("vector", lambda e: e.tensor_scalar(sc1[0:1, 5:6], sc1[0:1, 3:4], -1.0, 1.0, ALU.mult, ALU.add))
    OP("vector", lambda e: e.tensor_tensor(sc1[0:1, 5:6], sc1[0:1, 5:6], sc1[0:1, 2:3], ALU.mult))
    vthr = OP("vector", lambda e: e.tensor_tensor(sc1[0:1, 6:7], sc1[0:1, 4:5], sc1[0:1, 5:6], ALU.add))
    W("tensor", "cV", vthr)
    GPS()
    vbt = OP("tensor", lambda e: e.matmul(ps_G[:, 4:5], ones_row[0:1, 0:128], sc1[0:1, 6:7], start=True, stop=True))
    W("vector", "cP", vbt)
    OP("vector", lambda e: e.tensor_copy(bis[:, 12:13], ps_G[:, 4:5]))
    OP("vector", lambda e: e.tensor_scalar(maskf[:, 0:NKC], w_pm, bis[:, 12:13], None, ALU.is_gt))
    vmk = OP("vector", lambda e: e.tensor_copy(maskc[:, 0:NKC], maskf[:, 0:NKC]))
    W("tensor", "cV", vmk)
    vtr = OP("tensor", lambda e: e.matmul(ps_d2[0:1, 16:16 + NKC], ones_col_bf, maskc, start=True, stop=True))
    GPS()
    vri = OP("tensor", lambda e: e.matmul(ps_G[:, 0:NKC], upper01, maskc, start=True, stop=True))
    W("vector", "cP", vri)
    OP("vector", lambda e: e.tensor_copy(rrow[0:1, 0:NKC], ps_d2[0:1, 16:16 + NKC]))
    OP("vector", lambda e: e.memset(rrow[0:1, 128:129], 0.0))
    OP("vector", lambda e: e.memset(rrow[0:1, 256:256 + NKC], 0.0))
    vsc = None
    if NKC > 1:
        vsc = OP("vector", lambda e: e.tensor_tensor_scan(
            rrow[0:1, 129:128 + NKC], rrow[0:1, 0:NKC - 1], rrow[0:1, 256:256 + NKC - 1], 0.0, ALU.add, ALU.add))
    W("tensor", "cV", Sp.val("cV"))
    vbx = OP("tensor", lambda e: e.matmul(ps_d3[:, 32:32 + NKC], ones_row[0:1, 0:128], rrow[0:1, 128:128 + NKC], start=True, stop=True))
    W("vector", "cP", vbx)
    OP("vector", lambda e: e.tensor_copy(rankc, ps_G[:, 0:NKC]))
    vrk = OP("vector", lambda e: e.tensor_tensor(rankc, rankc, ps_d3[:, 32:32 + NKC], ALU.add))

    vm1 = vm2 = None
    for t in range(NKC):
        voh = OP("vector", lambda e, t=t: e.tensor_scalar(
            oh, iota_rank, rankc[:, t:t + 1], maskf[:, t:t + 1], ALU.is_equal, ALU.mult))
        W("tensor", "cV", voh)
        vm1 = OP("tensor", lambda e, t=t: e.matmul(
            ps_d0[0:1, 0:SELP], c128c[:, t:t + 1], oh, start=(t == 0), stop=(t == NKC - 1)))
        vm2 = OP("tensor", lambda e, t=t: e.matmul(
            ps_d1[0:1, 0:SELP], pcol_bf, oh, start=(t == 0), stop=(t == NKC - 1)))
        if t < NKC - 1:
            W("vector", "cP", vm2)
    W("vector", "cP", vm2)
    OP("vector", lambda e: e.tensor_copy(selrow, ps_d0[0:1, 0:SELP]))
    OP("vector", lambda e: e.tensor_tensor(selrow, selrow, ps_d1[0:1, 0:SELP], ALU.add))
    OP("vector", lambda e: e.tensor_tensor(selrow, selrow, padfix, ALU.add))
    vsi = OP("vector", lambda e: e.tensor_copy(selrow_i, selrow))
    W("sync", "cV", vsi)
    DMA("sync", "dSELW", sel_scr.rearrange("(a s) -> a s", a=1), selrow_i)
    W("sync", "dSELW")
    OP("vector", lambda e: e.tensor_scalar(selrow, selrow, float(S - 1), None, ALU.min))
    W("vector", "dSELW")
    vsi2 = OP("vector", lambda e: e.tensor_copy(selrow_i, selrow))
    W("sync", "cV", vsi2)
    DMA("sync", "dSCL", sel_cl.rearrange("(a s) -> a s", a=1), selrow_i.bitcast(U16))
    W("sync", "dSCL")
    for sc_ in range(NSC):
        DMA("sync", "dSEL", selc[:, sc_:sc_ + 1], sel_scr[sc_ * 128:(sc_ + 1) * 128].rearrange("(p a) -> p a", a=1))
    Sp.barrier()

    CKV0 = 0
    CQ0 = 0
    PSB = [ps_G, ps_d3]
    psb_rel = {0: Sp.val("cV"), 1: Sp.val("cV")}
    psb_ti = [0]

    def psb_next():
        i = psb_ti[0] % 2
        psb_ti[0] += 1
        if psb_rel[i]:
            W("tensor", "cV", psb_rel[i])
        return i, PSB[i]

    def c_proj(arena, dst_base, stride, woff, vrow_idx, ntok):
        for cs in range(CC):
            for tb in range(ntok // QB):
                t0 = tb * QB
                i, psb = psb_next()
                for dc in range(DC):
                    OP("tensor", lambda e, cs=cs, dc=dc, t0=t0, woff=woff, psb=psb: e.matmul(
                        psb[:, 0:QB], bfv(WA, woff + dc * C + cs * 128, 128),
                        bfv(A1, dc * S + t0, QB), start=(dc == 0), stop=False))
                vm = OP("tensor", lambda e, cs=cs, t0=t0, vrow_idx=vrow_idx, psb=psb: e.matmul(
                    psb[:, 0:QB], vrowbf[0:1, vrow_idx * C + cs * 128:vrow_idx * C + (cs + 1) * 128],
                    onesbf_row[0:1, 0:QB], start=False, stop=True))
                W("vector", "cP", vm)
                vv = OP("vector", lambda e, cs=cs, t0=t0, arena=arena, dst_base=dst_base, stride=stride, psb=psb: e.tensor_copy(
                    bfv(arena, dst_base + cs * stride + t0, QB), psb[:, 0:QB]))
                psb_rel[i] = vv
        return

    W("tensor", "cV", v_vrow)
    W("tensor", "dT")
    c_proj(A3, CQ0, SQ, W_A, 0, SQ)
    c_proj(A2, CKV0, S, W_B, 1, S)
    v_c = Sp.val("cV")

    U_Q = 0
    U_K, U_V = 0, CC * D
    W("gpsimd", "cP", Sp.val("cP"))
    vwq = DMA("gpsimd", "dW", bfv(WA, U_Q, CC * D).rearrange("p (cc o) -> p cc o", cc=CC), wuq_t.rearrange("(cc p) o -> p cc o", p=128))

    RQ = 0
    RSF, RTF = 3584, 4096
    SHUF = [i ^ 1 for i in range(32)]

    def rope_tile(ct_off, st_off, t0, dst_ap, vmm, i, psb):
        W("vector", "cP", vmm)
        W("vector", "dQW")
        OP("vector", lambda e, psb=psb: e.stream_shuffle(f32v(SCRA, RSF, QB), psb[:, 0:QB], SHUF))
        vm1 = OP("vector", lambda e, t0=t0, ct_off=ct_off, psb=psb: e.tensor_tensor(
            f32v(SCRA, RTF, QB), psb[:, 0:QB], bfv(A4, ct_off + t0, QB), ALU.mult))
        psb_rel[i] = vm1
        OP("vector", lambda e, t0=t0, st_off=st_off: e.tensor_tensor(
            f32v(SCRA, RSF, QB), f32v(SCRA, RSF, QB), bfv(A4, st_off + t0, QB), ALU.mult))
        return OP("vector", lambda e, dst_ap=dst_ap: e.tensor_tensor(
            dst_ap, f32v(SCRA, RTF, QB), f32v(SCRA, RSF, QB), ALU.add))

    W("tensor", "dW")
    W("tensor", "cV", v_c)
    vqd = None
    vr = None
    for pr in range(NP):
        for tb in range(NQB):
            t0 = tb * QB
            i, psb = psb_next()
            vm = None
            for cc_ in range(CC):
                vm = OP("tensor", lambda e, pr=pr, cc_=cc_, t0=t0, psb=psb: e.matmul(
                    psb[:, 0:QB], bfv(WA, U_Q + cc_ * D + pr * 128, 128),
                    bfv(A3, CQ0 + cc_ * SQ + t0, QB), start=(cc_ == 0), stop=(cc_ == CC - 1)))
            vr = rope_tile(T_CQ, T_SQ, t0, bfv(SCRA, RQ, QB), vm, i, psb)
            W("sync", "cV", vr)
            vqd = DMA("sync", "dQW", q_scr[pr, :, t0:t0 + QB], bfv(SCRA, RQ, QB))
    v_qmm = Sp.val("cP")

    W("sync", "cV", vr)
    DMA("sync", "dTK", bfv(A4, T_CK, S), cst["c_ctabk"])
    DMA("sync", "dTK", bfv(A4, T_SK, S), cst["c_stabk"])
    W("gpsimd", "cP", v_qmm)
    DMA("gpsimd", "dW", bfv(WA, U_K, CC * D).rearrange("p (cc o) -> p cc o", cc=CC), wuk_t.rearrange("(cc p) o -> p cc o", p=128))
    vwu = DMA("gpsimd", "dW", bfv(WA, U_V, CC * D).rearrange("p (cc o) -> p cc o", cc=CC), wuv_t.rearrange("(cc p) o -> p cc o", p=128))
    W("tensor", "dW")
    W("tensor", "dTK")
    W("vector", "dTK")

    K0 = 0
    V0 = 4 * S
    GRP = 2
    PPG = NP // GRP

    QT, PT = 0, 1024
    EVA, EVB = 2560, 3072
    OT = 8192
    DRW = 5120
    FLUSH_KC = 8 if NKC > 8 else 2
    NPT = 4 if NKC >= 4 else 2
    OAb = [PS_B[:, 0:512], PS_B[:, 1024:1536]]
    OBb = [PS_B[:, 512:1024], PS_B[:, 1536:2048]]
    vex = {}
    vavden = {}
    vsc_last = {}
    v_oT = {}
    pend = []
    vwo = None
    for g in range(GRP):
        MSET("vector", bfv(A1, V0, NKC * VST).rearrange(
            "p (n d) -> p n d", d=65)[:, :, 64:65], 1.0)
        for pl in range(PPG):
            pr = g * PPG + pl
            for tb in range(S // QB):
                t0 = tb * QB
                i, psb = psb_next()
                vm = None
                for cc_ in range(CC):
                    vm = OP("tensor", lambda e, pr=pr, cc_=cc_, t0=t0, psb=psb: e.matmul(
                        psb[:, 0:QB], bfv(WA, U_K + cc_ * D + pr * 128, 128),
                        bfv(A2, CKV0 + cc_ * S + t0, QB), start=(cc_ == 0), stop=(cc_ == CC - 1)))
                rope_tile(T_CK, T_SK, t0, bfv(A1, K0 + pl * S + t0, QB), vm, i, psb)
        for tc in range(NKC):
            i, psb = psb_next()
            vm = None
            for cc_ in range(CC):
                vm = OP("tensor", lambda e, tc=tc, cc_=cc_, g=g, psb=psb: e.matmul(
                    psb[:, 0:512], bfv(A2, CKV0 + cc_ * S + tc * 128, 128),
                    bfv(WA, U_V + cc_ * D + g * 512, 512), start=(cc_ == 0), stop=(cc_ == CC - 1)))
            W("vector", "cP", vm)
            vv = OP("vector", lambda e, tc=tc, psb=psb: e.tensor_copy(
                bfv(A1, V0 + tc * VST, VST).rearrange("p (h d) -> p h d", h=8)[:, :, 0:64],
                psb[:, 0:512].rearrange("p (h d) -> p h d", h=8)))
            psb_rel[i] = vv
        if g == GRP - 1:
            W("gpsimd", "cP", Sp.val("cP"))
            vwo = DMA("gpsimd", "dW", bfv(WA, 0, DC * D).rearrange("p (hc o) -> p hc o", hc=DC), wo_t.rearrange("(hc p) o -> p hc o", p=128))
            DMA("gpsimd", "dW", bfv(A2, 0, DC * D).rearrange("p (dc o) -> p dc o", dc=DC), fwq.rearrange("(dc p) o -> p dc o", p=128))
            DMA("gpsimd", "dW", bfv(A2, DC * D, DC * D).rearrange("p (dc o) -> p dc o", dc=DC), fwk.rearrange("(dc p) o -> p dc o", p=128))

        W("sync", "dQW")
        W("sync", "cV", Sp.val("cV"))

        def flush_ep(item):
            (pu, pslot, ppr, pt0, dra_p, drb_p, OA_p, OB_p, vrc_p) = item
            W("tensor", "cV", vrc_p)
            OP("tensor", lambda e, OA_p=OA_p, dra_p=dra_p: e.matmul(
                OA_p[0:64, 0:QB], ones_row[0:1, 0:64], dra_p, start=True, stop=True))
            vb2 = OP("tensor", lambda e, OB_p=OB_p, drb_p=drb_p: e.matmul(
                OB_p[0:64, 0:QB], ones_row[0:1, 0:64], drb_p, start=True, stop=True))
            W("vector", "cP", vb2)
            W("vector", f"dOW{pslot}")
            OP("vector", lambda e, OA_p=OA_p, pslot=pslot: e.tensor_tensor(
                bfv(SCRA, OT + pslot * 1024, QB, p0=0, p=64),
                f32v(SCRA, EVA, QB, p0=0, p=64), OA_p[0:64, 0:QB], ALU.mult))
            voTB = OP("vector", lambda e, OB_p=OB_p, pslot=pslot: e.tensor_tensor(
                bfv(SCRA, OT + pslot * 1024 + 512, QB, p0=0, p=64),
                f32v(SCRA, EVB, QB, p0=0, p=64), OB_p[0:64, 0:QB], ALU.mult))
            v_oT[pu] = voTB
            W("sync", "cV", voTB)
            DMA("sync", f"dOW{pslot}", o_scr[ppr, 0:64, pt0:pt0 + QB],
                bfv(SCRA, OT + pslot * 1024, QB, p0=0, p=64))
            DMA("sync", f"dOW{pslot}", o_scr[ppr, 64:128, pt0:pt0 + QB],
                bfv(SCRA, OT + pslot * 1024 + 512, QB, p0=0, p=64))

        for pl in range(PPG):
            pr = g * PPG + pl
            for qb in range(NQB):
                t0 = qb * QB
                u = (g * PPG + pl) * NQB + qb
                uslot = u % 2
                qtile = bfv(SCRA, QT + uslot * 512, QB)
                if (u - 2) in vsc_last:
                    W("sync", "cP", vsc_last[u - 2])
                DMA("sync", f"dQL{uslot}", qtile, q_scr[pr, :, t0:t0 + QB])
                W("tensor", f"dQL{uslot}")
                OA, OB = OAb[uslot], OBb[uslot]

                def avden(kc, u=u, pl=pl, OA=OA, OB=OB):
                    ptile = bfv(SCRA, PT + (kc % NPT) * 1024, 1024)
                    W("tensor", "cA", vex[(u, kc)])
                    if kc == 0 and (u - 2) in v_oT:
                        W("tensor", "cV", v_oT[u - 2])
                    OP("tensor", lambda e, kc=kc, ptile=ptile, OA=OA, pl=pl: e.matmul(
                        OA[0:65, 0:QB], bfv(A1, V0 + kc * VST + (2 * pl) * 65, 65),
                        ptile[:, 0:QB], start=(kc == 0), stop=(kc == NKC - 1)))
                    vavden[(u, kc)] = OP("tensor", lambda e, kc=kc, ptile=ptile, OB=OB, pl=pl: e.matmul(
                        OB[0:65, 0:QB], bfv(A1, V0 + kc * VST + (2 * pl + 1) * 65, 65),
                        ptile[:, 512:512 + QB], start=(kc == 0), stop=(kc == NKC - 1)))

                for kc in range(NKC):
                    sb_ = PS_A[:, (kc % 2) * 1024:(kc % 2) * 1024 + 1024]
                    bk = (u, kc - 2) if kc >= 2 else (u - 1, NKC - 2 + kc)
                    if bk in vex:
                        W("tensor", "cA", vex[bk])
                    OP("tensor", lambda e, pl=pl, kc=kc, qtile=qtile, sb_=sb_: e.matmul(
                        sb_[:, 0:QB], bfv(A1, K0 + pl * S + kc * 128, 128, p0=0, p=64),
                        qtile[0:64, :], start=True, stop=True))
                    vs2 = OP("tensor", lambda e, pl=pl, kc=kc, qtile=qtile, sb_=sb_: e.matmul(
                        sb_[:, 512:512 + QB], bfv(A1, K0 + pl * S + kc * 128, 128, p0=64, p=64),
                        qtile[64:128, :], start=True, stop=True))
                    vsc_last[u] = vs2
                    if kc >= 1:
                        avden(kc - 1)
                    if kc == FLUSH_KC and pend:
                        flush_ep(pend.pop())
                    ptile = bfv(SCRA, PT + (kc % NPT) * 1024, 1024)
                    W("scalar", "cP", vs2)
                    pk = (u, kc - NPT) if kc >= NPT else (u - 1, NKC - NPT + kc)
                    if pk in vavden:
                        W("scalar", "cP", vavden[pk])
                    if QB == 512:
                        vex[(u, kc)] = OP("scalar", lambda e, ptile=ptile, sb_=sb_: e.activation(
                            ptile, sb_, AF.Exp))
                    else:
                        OP("scalar", lambda e, ptile=ptile, sb_=sb_: e.activation(
                            ptile[:, 0:QB], sb_[:, 0:QB], AF.Exp))
                        vex[(u, kc)] = OP("scalar", lambda e, ptile=ptile, sb_=sb_: e.activation(
                            ptile[:, 512:512 + QB], sb_[:, 512:512 + QB], AF.Exp))
                avden(NKC - 1)
                dra = f32v(SCRA, DRW + uslot * 1024, QB, p0=0, p=1)
                drb = f32v(SCRA, DRW + uslot * 1024 + 512, QB, p0=0, p=1)
                W("vector", "cP", vavden[(u, NKC - 1)])
                OP("vector", lambda e, OA=OA: e.tensor_copy(
                    f32v(SCRA, EVA, QB, p0=0, p=64), OA[0:64, 0:QB]))
                OP("vector", lambda e, OB=OB: e.tensor_copy(
                    f32v(SCRA, EVB, QB, p0=0, p=64), OB[0:64, 0:QB]))
                OP("vector", lambda e, OA=OA, dra=dra: e.tensor_copy(dra, OA[64:65, 0:QB]))
                OP("vector", lambda e, OB=OB, drb=drb: e.tensor_copy(drb, OB[64:65, 0:QB]))
                OP("vector", lambda e, dra=dra: e.reciprocal(dra, dra))
                vrc = OP("vector", lambda e, drb=drb: e.reciprocal(drb, drb))
                pend.append((u, uslot, pr, t0, dra, drb, OA, OB, vrc))
        while pend:
            flush_ep(pend.pop())
        if g < GRP - 1:
            Sp.barrier()
    Sp.barrier()

    OS = 0
    ohist = {}
    for tc in range(NQC):
        slot = tc % 2
        if tc >= 2:
            W("sync", "cP", ohist[tc - 2]["vm"])
        vld = None
        for pr in range(NP):
            vld = DMA("sync", f"dOS{slot}", bfv(A4, OS + slot * NP * 128 + pr * 128, 128),
                      o_scr[pr, :, tc * 128:(tc + 1) * 128])
        W("tensor", f"dOS{slot}")
        if tc == 0:
            W("tensor", "dW")
        for hb in range(2):
            vm = None
            for pr in range(NP):
                vm = OP("tensor", lambda e, pr=pr, hb=hb, slot=slot: e.matmul(
                    ps_G[:, 0:512], bfv(A4, OS + slot * NP * 128 + pr * 128, 128),
                    bfv(WA, pr * D + hb * 512, 512), start=(pr == 0), stop=(pr == NP - 1)))
            if tc >= 1:
                W("scalar", "cV", ohist[tc - 1]["vs"][hb])
            vxn = DMA("scalar", f"dXR{hb}", f32v(SCRA, 0 + hb * 512, 512), xn_scr[tc * 128:(tc + 1) * 128, hb * 512:(hb + 1) * 512])
            W("vector", "cP", vm)
            W("vector", f"dXR{hb}")
            if tc >= 1:
                W("vector", f"dSW{hb}")
            vs = OP("vector", lambda e, hb=hb: e.tensor_tensor(
                f32v(SCRA, 1024 + hb * 512, 512), ps_G[:, 0:512], f32v(SCRA, 0 + hb * 512, 512), ALU.add))
            W("tensor", "cV", vs)
            W("sync", "cV", vs)
            vsw = DMA("sync", f"dSW{hb}", s_scr[tc * 128:(tc + 1) * 128, hb * 512:(hb + 1) * 512], f32v(SCRA, 1024 + hb * 512, 512))
            ohist.setdefault(tc, {"vm": None, "vs": {}, "vsw": {}})
            ohist[tc]["vm"] = vm
            ohist[tc]["vs"][hb] = vs
            ohist[tc]["vsw"][hb] = vsw
    Sp.barrier()

    FQW, FKW = 0, DC * D
    FVW, FOW = 0, DC * D
    X5 = 0
    MR = 2 * NSC * D
    FV5 = MR + 2 * S
    MSL = FV5 + NSC * D
    W("gpsimd", "dSEL")
    for sc_ in range(NSC):
        vz = MSET("gpsimd", f32v(A1, X5 // 2 + sc_ * D, D), 0.0)
        W("gpsimd", "cG", Sp.val("cG"))
        IDMA("dGX", f32v(A1, X5 // 2 + sc_ * D, D), None, xn_scr, IndirectOffsetOnAxis(selc[:, sc_:sc_ + 1], 0),
             bounds_check=S - 1, oob_is_err=False)
        MSET("gpsimd", posc[:, sc_:sc_ + 1], 0)
        W("gpsimd", "cG", Sp.val("cG"))
        IDMA("dGX", posc[:, sc_:sc_ + 1], None, pos_in.rearrange('(s a) -> s a', a=1), IndirectOffsetOnAxis(selc[:, sc_:sc_ + 1], 0),
             bounds_check=S - 1, oob_is_err=False)
    v_gcmp = Sp.val("cG")

    W("sync", "dC")
    W("sync", "cP", Sp.val("cP"))
    vrw2 = DMA("sync", "dRW", rowst, rw_in.rearrange("(a d) -> a d", a=1))
    W("tensor", "dRW")
    bc_row(rowst, bcast, D)
    W("vector", "dGX")
    W("vector", "cG", v_gcmp)
    for sc_ in range(NSC):
        OP("vector", lambda e, sc_=sc_: e.scalar_tensor_tensor(
            f32v(SCRA, 0, D), f32v(A1, X5 // 2 + sc_ * D, D), 1.0, bcast, ALU.mult, ALU.mult,
            accum_out=wselc[:, sc_:sc_ + 1]))
    W("sync", "dRW")
    W("sync", "cP", Sp.val("cP"))
    vrb = DMA("sync", "dRB", rowst[0:1, 0:1], rb_in.rearrange("(b a) -> b a", b=1))
    W("tensor", "dRB")
    GPS()
    vbb = OP("tensor", lambda e: e.matmul(ps_G[:, 5:6], ones_row[0:1, 0:128], rowst[0:1, 0:1], start=True, stop=True))
    W("vector", "cP", vbb)
    OP("vector", lambda e: e.tensor_copy(stat[:, 0:1], ps_G[:, 5:6]))
    for sc_ in range(NSC):
        OP("vector", lambda e, sc_=sc_: e.tensor_tensor(
            wselc[:, sc_:sc_ + 1], wselc[:, sc_:sc_ + 1], stat[:, 0:1], ALU.add))

    W("sync", "dRB")
    W("sync", "cP", Sp.val("cP"))
    vl1 = DMA("sync", "dL1", rowst, ln1_in.rearrange("(a d) -> a d", a=1))
    W("tensor", "dL1")
    bc_row(rowst, bcast, D)
    HN = 0
    vhn_prev = Sp.val("cV")
    for sc_ in range(NSC):
        W("scalar", "cV", vhn_prev)
        vsq = OP("scalar", lambda e, sc_=sc_: e.activation(
            f32v(SCRA, 0, D), f32v(A1, X5 // 2 + sc_ * D, D), AF.Square, accum_out=stat[:, 1:2]))
        W("vector", "cA", vsq)
        vv = OP("vector", lambda e: e.tensor_scalar(stat[:, 2:3], stat[:, 1:2], 1.0 / D, 1e-6, ALU.mult, ALU.add))
        W("scalar", "cV", vv)
        OP("scalar", lambda e: e.activation(stat[:, 3:4], stat[:, 2:3], AF.Ln))
        vr2 = OP("scalar", lambda e: e.activation(stat[:, 4:5], stat[:, 3:4], AF.Exp, scale=epsc[:, 1:2]))
        W("vector", "cA", vr2)
        OP("vector", lambda e, sc_=sc_: e.tensor_scalar(
            f32v(SCRA, 0, D), f32v(A1, X5 // 2 + sc_ * D, D), stat[:, 4:5], None, ALU.mult))
        vhn = OP("vector", lambda e: e.tensor_tensor(
            bfv(SCRA, 2048, D), f32v(SCRA, 0, D), bcast, ALU.mult))
        vhn_prev = vhn
        W("tensor", "cV", vhn)
        for dc in range(DC):
            GPS()
            vt = OP("tensor", lambda e, dc=dc: e.transpose(
                ps_G.bitcast(BF16)[:, 0:128], bfv(SCRA, 2048 + dc * 128, 128), idbf))
            W("vector", "cP", vt)
            vv = OP("vector", lambda e, dc=dc, sc_=sc_: e.tensor_copy(
                bfv(A4, HN + dc * SELP + sc_ * 128, 128), ps_G.bitcast(BF16)[:, 0:128]))
            W("tensor", "cV", vv)

    TH, TW, TCK, TSK, TCQ, TSQ2 = 4096, 4608, 5120, 5632, 6144, 6656
    for sc_ in range(NSC):
        vv = OP("vector", lambda e, sc_=sc_: e.tensor_copy(poscf[:, sc_:sc_ + 1], posc[:, sc_:sc_ + 1]))
        W("sync", "cV", vv)
        DMA("sync", "dPW", w_scr[sc_ * 128:(sc_ + 1) * 128].rearrange("(p a) -> p a", a=1), poscf[:, sc_:sc_ + 1])
    W("sync", "dPW")
    vpr = DMA("sync", "dPR", posrow, w_scr[0:SELP].rearrange("(a s) -> a s", a=1))
    W("tensor", "dPR")
    GPS()
    vm = OP("tensor", lambda e: e.matmul(ps_G[:, 0:SELP], ones_row[0:1, 0:128], posrow, start=True, stop=True))
    W("vector", "cP", vm)
    vth = OP("vector", lambda e: e.tensor_scalar(f32v(SCRA, TH, SELP), ps_G[:, 0:SELP], invb, None, ALU.mult))

    def build_sincos(shift, dst_off):
        W("vector", "cA", Sp.val("cA"))
        OP("vector", lambda e: e.tensor_scalar(
            f32v(SCRA, TW, SELP), f32v(SCRA, TH, SELP), 1.0 / TWO_PI, 0.5 + shift / TWO_PI, ALU.mult, ALU.add))
        vv = OP("vector", lambda e: e.tensor_copy(i32v(SCRA, TW + 512, SELP), f32v(SCRA, TW, SELP)))
        OP("vector", lambda e: e.tensor_copy(f32v(SCRA, TW, SELP), i32v(SCRA, TW + 512, SELP)))
        OP("vector", lambda e: e.tensor_scalar(
            f32v(SCRA, TW, SELP), f32v(SCRA, TW, SELP), -TWO_PI, None, ALU.mult))
        OP("vector", lambda e: e.tensor_tensor(
            f32v(SCRA, TW, SELP), f32v(SCRA, TW, SELP), f32v(SCRA, TH, SELP), ALU.add))
        OP("vector", lambda e: e.tensor_scalar(
            f32v(SCRA, TW, SELP), f32v(SCRA, TW, SELP), shift, None, ALU.add))
        FB = f32v(SCRA, TW + 512, SELP)
        OP("vector", lambda e: e.tensor_scalar(FB, f32v(SCRA, TW, SELP), math.pi, None, ALU.is_gt))
        OP("vector", lambda e: e.tensor_scalar(FB, FB, TWO_PI, None, ALU.mult))
        OP("vector", lambda e: e.tensor_tensor(f32v(SCRA, TW, SELP), f32v(SCRA, TW, SELP), FB, ALU.subtract))
        OP("vector", lambda e: e.tensor_scalar(FB, f32v(SCRA, TW, SELP), -math.pi, None, ALU.is_lt))
        OP("vector", lambda e: e.tensor_scalar(FB, FB, TWO_PI, None, ALU.mult))
        vv = OP("vector", lambda e: e.tensor_tensor(f32v(SCRA, TW, SELP), f32v(SCRA, TW, SELP), FB, ALU.add))
        W("scalar", "cV", vv)
        return OP("scalar", lambda e, dst_off=dst_off: e.activation(
            f32v(SCRA, dst_off, SELP), f32v(SCRA, TW, SELP), AF.Sin))

    vs1 = build_sincos(0.0, TSK)
    vs2 = build_sincos(math.pi / 2.0, TCK)
    W("vector", "cA", vs2)
    OP("vector", lambda e: e.tensor_scalar(f32v(SCRA, TSK, SELP), f32v(SCRA, TSK, SELP), sgnb, None, ALU.mult))
    OP("vector", lambda e: e.tensor_scalar(f32v(SCRA, TCQ, SELP), f32v(SCRA, TCK, SELP), 0.125, None, ALU.mult))
    vtb = OP("vector", lambda e: e.tensor_scalar(f32v(SCRA, TSQ2, SELP), f32v(SCRA, TSK, SELP), 0.125, None, ALU.mult))

    FQ0, FK0 = 0, NP * SELP
    W("tensor", "cV", vtb)
    W("tensor", "dW")
    for (wq_off, dst0, tc_off, ts_off) in ((FQW, FQ0, TCQ, TSQ2), (FKW, FK0, TCK, TSK)):
        for pr in range(NP):
            vm = None
            for dc in range(DC):
                vm = OP("tensor", lambda e, pr=pr, dc=dc, wq_off=wq_off: e.matmul(
                    ps_G[:, 0:SELP], bfv(A2, wq_off + dc * D + pr * 128, 128),
                    bfv(A4, HN + dc * SELP, SELP), start=(dc == 0), stop=(dc == DC - 1)))
            W("vector", "cP", vm)
            v1 = OP("vector", lambda e: e.tensor_copy(bfv(SCRA, 2048, SELP), ps_G.bitcast(BF16)[:, 0:SELP] if False else ps_G[:, 0:SELP]))
            W("tensor", "cV", v1)
            GPS()
            vsw = OP("tensor", lambda e: e.matmul(ps_G[:, 0:SELP], pswapb, bfv(SCRA, 2048, SELP), start=True, stop=True))
            OP("vector", lambda e, tc_off=tc_off: e.tensor_tensor(
                f32v(SCRA, 0, SELP), bfv(SCRA, 2048, SELP), f32v(SCRA, tc_off, SELP), ALU.mult))
            W("vector", "cP", vsw)
            OP("vector", lambda e, ts_off=ts_off: e.tensor_tensor(
                f32v(SCRA, 512, SELP), ps_G[:, 0:SELP], f32v(SCRA, ts_off, SELP), ALU.mult))
            vv = OP("vector", lambda e, dst0=dst0, pr=pr: e.tensor_tensor(
                bfv(A3, dst0 + pr * SELP, SELP), f32v(SCRA, 0, SELP), f32v(SCRA, 512, SELP), ALU.add))
            W("tensor", "cV", vv)
    W("gpsimd", "cP", Sp.val("cP"))
    vfv = DMA("gpsimd", "dW", bfv(A2, FVW, DC * D).rearrange("p (dc o) -> p dc o", dc=DC), fwv.rearrange("(dc p) o -> p dc o", p=128))
    vfo = DMA("gpsimd", "dW", bfv(A2, FOW, DC * D).rearrange("p (dc o) -> p dc o", dc=DC), fwo.rearrange("(dc p) o -> p dc o", p=128))
    W("tensor", "dW")
    for sc_ in range(NSC):
        for hb in range(2):
            vm = None
            for dc in range(DC):
                vm = OP("tensor", lambda e, sc_=sc_, dc=dc, hb=hb: e.matmul(
                    ps_G[:, 0:512], bfv(A4, HN + dc * SELP + sc_ * 128, 128),
                    bfv(A2, FVW + dc * D + hb * 512, 512), start=(dc == 0), stop=(dc == DC - 1)))
            W("vector", "cP", vm)
            vv = OP("vector", lambda e, sc_=sc_, hb=hb: e.tensor_copy(
                bfv(A1, FV5 + sc_ * D + hb * 512, 512), ps_G[:, 0:512]))
            W("tensor", "cV", vv)

    iw_src = sel_cl.rearrange("(col x two) -> x col two", x=16, two=2)[:, :, 0:1]
    for gg in range(8):
        DMA("sync", "dIW", iw16[16 * gg:16 * gg + 16, :].rearrange("x (c a) -> x c a", a=1), iw_src,
            allow_slow_non_contiguous=True)
    T1 = 6144
    W("gpsimd", "dIW")
    for sc_ in range(NSC):
        if sc_ == NSC - 1:
            MSET("gpsimd", f32v(A1, MR // 2, S), 0.0)
        W("gpsimd", "cG", Sp.val("cG"))
        IDMA("dGM", f32v(A1, MR // 2, S), None, m_in, IndirectOffsetOnAxis(selc[:, sc_:sc_ + 1], 0),
             bounds_check=S - 1, oob_is_err=False)
        W("gpsimd", "dGM")
        W("gpsimd", "cV", Sp.val("cV"))
        vic = OP("gpsimd", lambda e: e.indirect_copy(
            f32v(SCRA, T1, SELP), f32v(A1, MR // 2, S), iw16,
            i_know_ap_gather_is_preferred=True))
        W("tensor", "cG", vic)
        for b in range(NSC):
            GPS()
            vt = OP("tensor", lambda e, b=b: e.transpose(
                ps_G[:, 0:128], f32v(SCRA, T1 + b * 128, 128), idf32))
            W("vector", "cP", vt)
            vv = OP("vector", lambda e, b=b, sc_=sc_: e.tensor_copy(
                f32v(A1, MSL // 2 + b * SELP + sc_ * 128, 128), ps_G[:, 0:128]))
            W("tensor", "cV", vv)
    v_mselg = Sp.val("cV")

    FOT = 0
    W("vector", "cV", v_mselg)
    vex = None
    for pr in range(NP):
        vav = None
        for kc in range(NSC):
            if vex is not None:
                W("vector", "cA", vex)
            vs1 = OP("tensor", lambda e, pr=pr, kc=kc: e.matmul(
                ps_S[:, 0:SELP], bfv(A3, FK0 + pr * SELP + kc * 128, 128, p0=0, p=64),
                bfv(A3, FQ0 + pr * SELP, SELP, p0=0, p=64), start=True, stop=True))
            vs2 = OP("tensor", lambda e, pr=pr, kc=kc: e.matmul(
                ps_S[:, 512:512 + SELP], bfv(A3, FK0 + pr * SELP + kc * 128, 128, p0=64, p=64),
                bfv(A3, FQ0 + pr * SELP, SELP, p0=64, p=64), start=True, stop=True))
            W("vector", "cP", vs2)
            OP("vector", lambda e, kc=kc: e.tensor_tensor(
                f32v(SCRA, 0, SELP), ps_S[:, 0:SELP], f32v(A1, MSL // 2 + kc * SELP, SELP), ALU.add))
            vma = OP("vector", lambda e, kc=kc: e.tensor_tensor(
                f32v(SCRA, 512, SELP), ps_S[:, 512:512 + SELP], f32v(A1, MSL // 2 + kc * SELP, SELP), ALU.add))
            W("scalar", "cV", vma)
            if kc >= 2:
                W("scalar", "cP", vav)
            ptile = bfv(SCRA, 2048 + (kc % 2) * 1024, 1024)
            if SELP == 512:
                vex = OP("scalar", lambda e, ptile=ptile: e.activation(
                    ptile, f32v(SCRA, 0, 1024), AF.Exp))
            else:
                OP("scalar", lambda e, ptile=ptile: e.activation(
                    ptile[:, 0:SELP], f32v(SCRA, 0, SELP), AF.Exp))
                vex = OP("scalar", lambda e, ptile=ptile: e.activation(
                    ptile[:, 512:512 + SELP], f32v(SCRA, 512, SELP), AF.Exp))
            W("tensor", "cA", vex)
            ocol = ones_kpad_bf if kc == NSC - 1 else ones_col_bf
            OP("tensor", lambda e, pr=pr, kc=kc, ptile=ptile: e.matmul(
                ps_O[0:64, 0:SELP], bfv(A1, FV5 + kc * D + pr * 128, 64), ptile[:, 0:SELP],
                start=(kc == 0), stop=(kc == NSC - 1), tile_position=(0, 0)))
            vav = OP("tensor", lambda e, pr=pr, kc=kc, ptile=ptile: e.matmul(
                ps_O[64:128, 0:SELP], bfv(A1, FV5 + kc * D + pr * 128 + 64, 64), ptile[:, 512:512 + SELP],
                start=(kc == 0), stop=(kc == NSC - 1), tile_position=(0, 64),
                skip_group_check=True))
            dbank_a = ps_d0 if kc % 2 == 0 else ps_d1
            dbank_b = ps_d2 if kc % 2 == 0 else ps_d3
            rowa = 0 if kc % 2 == 0 else 32
            rowb = 64 if kc % 2 == 0 else 32
            st = kc < 2
            sp = kc >= NSC - 2
            OP("tensor", lambda e, dbank_a=dbank_a, rowa=rowa, st=st, sp=sp, ptile=ptile, ocol=ocol: e.matmul(
                dbank_a[rowa:rowa + 1, 0:SELP], ocol, ptile[:, 0:SELP],
                start=st, stop=sp, tile_position=(0, rowa)))
            vav = OP("tensor", lambda e, dbank_b=dbank_b, rowb=rowb, st=st, sp=sp, ptile=ptile, ocol=ocol: e.matmul(
                dbank_b[rowb:rowb + 1, 0:SELP], ocol, ptile[:, 512:512 + SELP],
                start=st, stop=sp, tile_position=(0, rowb)))
        W("vector", "cP", vav)
        hasodd = NSC > 1
        OP("vector", lambda e: e.tensor_copy(rrow[0:1, 0:SELP], ps_d0[0:1, 0:SELP]))
        if hasodd:
            OP("vector", lambda e: e.tensor_tensor(rrow[0:1, 0:SELP], rrow[0:1, 0:SELP], ps_d1[32:33, 0:SELP], ALU.add))
        OP("vector", lambda e: e.tensor_copy(rrow[0:1, 512:512 + SELP], ps_d2[64:65, 0:SELP]))
        if hasodd:
            OP("vector", lambda e: e.tensor_tensor(rrow[0:1, 512:512 + SELP], rrow[0:1, 512:512 + SELP], ps_d3[32:33, 0:SELP], ALU.add))
        if SELP == 512:
            vrc = OP("vector", lambda e: e.reciprocal(rrow, rrow))
        else:
            OP("vector", lambda e: e.reciprocal(rrow[0:1, 0:SELP], rrow[0:1, 0:SELP]))
            vrc = OP("vector", lambda e: e.reciprocal(rrow[0:1, 512:512 + SELP], rrow[0:1, 512:512 + SELP]))
        W("tensor", "cV", vrc)
        vb1 = OP("tensor", lambda e: e.matmul(
            ps_d0[0:64, 0:SELP], ones_row[0:1, 0:64], rrow[0:1, 0:SELP], start=True, stop=True))
        vb2 = OP("tensor", lambda e: e.matmul(
            ps_d0[64:128, 0:SELP], ones_row[0:1, 0:64], rrow[0:1, 512:512 + SELP],
            start=True, stop=True, tile_position=(0, 64)))
        W("vector", "cP", vb2)
        OP("vector", lambda e: e.tensor_copy(f32v(SCRA, 0, SELP), ps_d0[:, 0:SELP]))
        voT = OP("vector", lambda e, pr=pr: e.tensor_tensor(
            bfv(A4, FOT + pr * SELP, SELP), ps_O[:, 0:SELP], f32v(SCRA, 0, SELP), ALU.mult))
        W("tensor", "cV", voT)
    v_fattn = Sp.val("cV")

    W("tensor", "dW")
    for wc in range(NWC):
        for hb in range(2):
            vm = None
            for pr in range(NP):
                vm = OP("tensor", lambda e, wc=wc, pr=pr, hb=hb: e.matmul(
                    ps_G[:, 0:512], bfv(A4, FOT + pr * SELP + wc * 128, 128),
                    bfv(A2, FOW + pr * D + hb * 512, 512), start=(pr == 0), stop=(pr == NP - 1)))
            W("vector", "cP", vm)
            vv = OP("vector", lambda e, wc=wc, hb=hb: e.tensor_tensor(
                f32v(SCRA, wc * D + hb * 512, 512), ps_G[:, 0:512],
                f32v(A1, X5 // 2 + wc * D + hb * 512, 512), ALU.add))
            W("tensor", "cV", vv)
    v_x1 = Sp.val("cV")

    X1T = NP * SELP
    W("sync", "dL1")
    W("sync", "cP", Sp.val("cP"))
    vl2 = DMA("sync", "dL2", rowst, ln2_in.rearrange("(a d) -> a d", a=1))
    W("tensor", "dL2")
    bc_row(rowst, bcast, D)
    vhn_prev = Sp.val("cV")
    for wc in range(NWC):
        W("scalar", "cV", vhn_prev)
        vsq = OP("scalar", lambda e, wc=wc: e.activation(
            f32v(SCRA, NWC * D, D), f32v(SCRA, wc * D, D), AF.Square, accum_out=stat[:, 1:2]))
        W("vector", "cA", vsq)
        vv = OP("vector", lambda e: e.tensor_scalar(stat[:, 2:3], stat[:, 1:2], 1.0 / D, 1e-6, ALU.mult, ALU.add))
        W("scalar", "cV", vv)
        OP("scalar", lambda e: e.activation(stat[:, 3:4], stat[:, 2:3], AF.Ln))
        vr2 = OP("scalar", lambda e: e.activation(stat[:, 4:5], stat[:, 3:4], AF.Exp, scale=epsc[:, 1:2]))
        W("vector", "cA", vr2)
        OP("vector", lambda e, wc=wc: e.tensor_scalar(
            f32v(SCRA, NWC * D, D), f32v(SCRA, wc * D, D), stat[:, 4:5], None, ALU.mult))
        vhn = OP("vector", lambda e: e.tensor_tensor(
            bfv(SCRA, 2 * (NWC + 1) * D, D), f32v(SCRA, NWC * D, D), bcast, ALU.mult))
        vhn_prev = vhn
        W("tensor", "cV", vhn)
        for dc in range(DC):
            GPS()
            vt = OP("tensor", lambda e, dc=dc: e.transpose(
                ps_G.bitcast(BF16)[:, 0:128], bfv(SCRA, 2 * (NWC + 1) * D + dc * 128, 128), idbf))
            W("vector", "cP", vt)
            vv = OP("vector", lambda e, dc=dc, wc=wc: e.tensor_copy(
                bfv(A4, X1T + dc * WIN + wc * 128, 128), ps_G.bitcast(BF16)[:, 0:128]))
            W("tensor", "cV", vv)
    v_x1T = Sp.val("cV")

    H1G = 0
    _v0 = Sp.val("cP")
    w1_reads = {-2: _v0, -1: _v0}
    for fg in range(FC // 4):
        W("gpsimd", "cP", w1_reads[fg - 2])
        vw1 = DMA("gpsimd", f"dW1{fg % 2}", bfv(A3, 0 if fg % 2 == 0 else 4096, DC * 512).rearrange("p (dc f) -> p dc f", dc=DC),
                  fw1[:, fg * 512:(fg + 1) * 512].rearrange("(dc p) f -> p dc f", p=128))
        W("tensor", f"dW1{fg % 2}")
        for fs in range(4):
            vm = None
            for dc in range(DC):
                vm = OP("tensor", lambda e, fg=fg, fs=fs, dc=dc: e.matmul(
                    ps_G[:, 0:WIN], bfv(A3, (0 if fg % 2 == 0 else 4096) + dc * 512 + fs * 128, 128),
                    bfv(A4, X1T + dc * WIN, WIN), start=(dc == 0), stop=(dc == DC - 1)))
            w1_reads[fg] = vm
            W("vector", "cP", vm)
            OP("vector", lambda e: e.tensor_copy(f32v(SCRA, 2 * (NWC + 1) * D // 2, WIN), ps_G[:, 0:WIN]))
            h1 = f32v(SCRA, 2 * (NWC + 1) * D // 2, WIN)
            vv = OP("vector", lambda e, h1=h1: e.tensor_tensor(f32v(SCRA, 7000, WIN), h1, h1, ALU.mult))
            W("tensor", "cV", vv)
            OP("vector", lambda e: e.tensor_scalar(f32v(SCRA, 7000, WIN), f32v(SCRA, 7000, WIN), C1G, C0G, ALU.mult, ALU.add))
            vu = OP("vector", lambda e, h1=h1: e.tensor_tensor(f32v(SCRA, 7424, WIN), f32v(SCRA, 7000, WIN), h1, ALU.mult))
            W("scalar", "cV", vu)
            vt = OP("scalar", lambda e: e.activation(f32v(SCRA, 7000, WIN), f32v(SCRA, 7424, WIN), AF.Tanh))
            W("vector", "cA", vt)
            OP("vector", lambda e: e.tensor_scalar(f32v(SCRA, 7000, WIN), f32v(SCRA, 7000, WIN), 0.5, 0.5, ALU.mult, ALU.add))
            vg = OP("vector", lambda e, fg=fg, fs=fs, h1=h1: e.tensor_tensor(
                bfv(A1, H1G + (fg * 4 + fs) * WIN, WIN), f32v(SCRA, 7000, WIN), h1, ALU.mult))
    v_h1g = Sp.val("cV")

    _v0 = Sp.val("cP")
    w2_reads = {(0,): _v0, (1,): _v0}
    W("tensor", "cV", v_h1g)
    for wc in range(NWC):
        for hb in range(2):
            vm = None
            for fg in range(FC // 4):
                key = (fg % 2,)
                W("gpsimd", "cP", w2_reads[key])
                vw2 = DMA("gpsimd", f"dW2{fg % 2}", bfv(A2, FVW + 2048 * (fg % 2), 4 * 512).rearrange("p (g o) -> p g o", g=4),
                          fw2[fg * 512:(fg + 1) * 512, hb * 512:(hb + 1) * 512].rearrange("(g p) o -> p g o", p=128))
                W("tensor", f"dW2{fg % 2}")
                for g in range(4):
                    vm = OP("tensor", lambda e, wc=wc, hb=hb, fg=fg, g=g: e.matmul(
                        ps_G[:, 0:512], bfv(A1, H1G + (fg * 4 + g) * WIN + wc * 128, 128),
                        bfv(A2, FVW + 2048 * (fg % 2) + g * 512, 512),
                        start=(fg == 0 and g == 0), stop=(fg == FC // 4 - 1 and g == 3)))
                w2_reads[key] = vm
            W("vector", "cP", vm)
            OP("vector", lambda e, wc=wc, hb=hb: e.tensor_tensor(
                f32v(SCRA, wc * D + hb * 512, 512), f32v(SCRA, wc * D + hb * 512, 512), ps_G[:, 0:512], ALU.add))
            vpr2 = OP("vector", lambda e, wc=wc, hb=hb: e.tensor_scalar(
                f32v(SCRA, wc * D + hb * 512, 512), f32v(SCRA, wc * D + hb * 512, 512),
                wselc[:, wc:wc + 1], None, ALU.mult))
            W("tensor", "cV", vpr2)
        W("gpsimd", "cV", Sp.val("cV"))
        IDMA("dSC", xn_scr, IndirectOffsetOnAxis(selc[:, wc:wc + 1], 0), f32v(SCRA, wc * D, D), None,
             bounds_check=S - 1, oob_is_err=False)
    Sp.barrier()

    vo_hist = {}

    def ph6_load(tc):
        b = tc % 2
        g0 = f32v(SCRA, b * D, D)
        s0 = f32v(SCRA, 2048 + b * D, D)
        x0 = f32v(SCRA, 4096 + b * D, D)
        if tc >= 2:
            W("sync", "cV", vo_hist[tc - 2])
            W("scalar", "cV", vo_hist[tc - 2])
            W("gpsimd", "cV", vo_hist[tc - 2])
            W("sync", f"dY{b}")
        DMA("sync", f"dE{b}", g0, xn_scr[tc * 128:(tc + 1) * 128, :])
        DMA("scalar", f"dES{b}", s0, s_scr[tc * 128:(tc + 1) * 128, :])
        DMA("gpsimd", f"dX6{b}", x0, x_in[tc * 128:(tc + 1) * 128, :])

    def ph6_compute(tc):
        b = tc % 2
        g0 = f32v(SCRA, b * D, D)
        s0 = f32v(SCRA, 2048 + b * D, D)
        x0 = f32v(SCRA, 4096 + b * D, D)
        W("scalar", f"dE{b}")
        vsl = OP("scalar", lambda e, g0=g0: e.activation(g0, g0, AF.Silu))
        W("vector", "cA", vsl)
        W("vector", f"dES{b}")
        W("vector", f"dX6{b}")
        OP("vector", lambda e, g0=g0, s0=s0: e.tensor_tensor(g0, g0, s0, ALU.mult))
        vo = OP("vector", lambda e, g0=g0, x0=x0: e.tensor_tensor(g0, g0, x0, ALU.add))
        vo_hist[tc] = vo
        W("sync", "cV", vo)
        DMA("sync", f"dY{b}", y_out[tc * 128:(tc + 1) * 128, :], g0)

    ph6_load(0)
    for tc in range(NQC):
        if tc + 1 < NQC:
            ph6_load(tc + 1)
        ph6_compute(tc)
    Sp.barrier()
    if dbg:
        dbg_a4 = nc.dram_tensor("dbg_a4", [128, A4.shape[-1]], BF16, kind="ExternalOutput").ap()
        dbg_a1 = nc.dram_tensor("dbg_a1", [128, NSC * SELP], F32, kind="ExternalOutput").ap()
        dbg_ws = nc.dram_tensor("dbg_ws", [128, NSC], F32, kind="ExternalOutput").ap()
        dbg_sel = nc.dram_tensor("dbg_sel", [128, NSC], I32, kind="ExternalOutput").ap()
        dbg_wpm = nc.dram_tensor("dbg_wpm", [128, NKC], F32, kind="ExternalOutput").ap()
        dbg_bis = nc.dram_tensor("dbg_bis", [128, 16], F32, kind="ExternalOutput").ap()
        dbg_sc1 = nc.dram_tensor("dbg_sc1", [2, 16], F32, kind="ExternalOutput").ap()
        dbg_rank = nc.dram_tensor("dbg_rank", [128, NKC], F32, kind="ExternalOutput").ap()
        dbg_mask = nc.dram_tensor("dbg_mask", [128, NKC], F32, kind="ExternalOutput").ap()
        dbg_stat = nc.dram_tensor("dbg_stat", [128, 24], F32, kind="ExternalOutput").ap()
        DMA("sync", "dDBG", dbg_a4, A4[:, :])
        DMA("sync", "dDBG", dbg_a1, f32v(A1, MSL // 2, NSC * SELP))
        DMA("sync", "dDBG", dbg_ws, wselc)
        DMA("sync", "dDBG", dbg_sel, selc)
        DMA("sync", "dDBG", dbg_wpm, w_pm)
        DMA("sync", "dDBG", dbg_bis, bis)
        DMA("sync", "dDBG", dbg_sc1, sc1)
        DMA("sync", "dDBG", dbg_rank, rankc)
        vmf = OP("vector", lambda e: e.tensor_copy(stat[:, 0:NKC] if NKC <= 24 else stat[:, 0:24], maskf[:, 0:min(NKC, 24)]))
        W("sync", "cV", vmf)
        DMA("sync", "dDBG", dbg_mask, maskf if False else maskf)
        DMA("sync", "dDBG", dbg_stat, stat)
        Sp.barrier()
    Sp.build()
    return nc


def make_in_maps(cfg, inputs):
    S, SQ = cfg.S, cfg.SQ
    x = np.ascontiguousarray(inputs["L0_attn_x"])
    m = inputs["L0_ffn_attention_mask"]
    pos = inputs["L0_ffn_position_ids"]
    B = x.shape[0]
    in_maps = []
    for core in range(2 * B):
        b, half = core // 2, core % 2
        perm = np.concatenate([np.arange(half * SQ, (half + 1) * SQ),
                               np.arange((1 - half) * SQ, (2 - half) * SQ) % S])
        cst = host_constants(cfg, perm)
        xb = np.ascontiguousarray(x[b][perm])
        mb = np.ascontiguousarray(m[b, 0][perm][:, perm])
        pb = np.ascontiguousarray(pos[b][perm]).astype(np.int32)
        im = {
            "x": xb, "m": mb, "pos": pb,
            "wdq_t": np.ascontiguousarray(inputs["L0_attn_w_down_q"].T),
            "wdkv_t": np.ascontiguousarray(inputs["L0_attn_w_down_kv"].T),
            "wuq_t": np.ascontiguousarray(inputs["L0_attn_w_up_q"].T),
            "wuk_t": np.ascontiguousarray(inputs["L0_attn_w_up_k"].T),
            "wuv_t": np.ascontiguousarray(inputs["L0_attn_w_up_v"].T),
            "wo_t": np.ascontiguousarray(inputs["L0_attn_w_out"].T),
            "g2": inputs["L0_attn_ln_scale"], "b2": inputs["L0_attn_ln_bias"],
            "rw": np.ascontiguousarray(inputs["L0_ffn_router_w"][:, 0]),
            "rb": inputs["L0_ffn_router_b"],
            "fwq": inputs["L0_ffn_wq"], "fwk": inputs["L0_ffn_wk"],
            "fwv": inputs["L0_ffn_wv"], "fwo": inputs["L0_ffn_wo"],
            "fw1": inputs["L0_ffn_w1"], "fw2": inputs["L0_ffn_w2"],
            "ln1": inputs["L0_ffn_ln1"], "ln2": inputs["L0_ffn_ln2"],
        }
        im.update(cst)
        in_maps.append(im)
    return in_maps


def assemble(cfg, results, B):
    S, SQ, D = cfg.S, cfg.SQ, cfg.D
    out = np.zeros((B, S, D), np.float32)
    for core, res in enumerate(results):
        b, half = core // 2, core % 2
        out[b, half * SQ:(half + 1) * SQ] = res["y"]
    return out


def kernel(**inputs):
    from concourse.bass_utils import run_bass_kernel_spmd
    cfg = Cfg(4096)
    nc = build_program(cfg)
    in_maps = make_in_maps(cfg, inputs)
    res = run_bass_kernel_spmd(nc, in_maps, core_ids=list(range(len(in_maps))))
    return assemble(cfg, res.results, inputs["L0_attn_x"].shape[0])
